# version 15
# baseline (speedup 1.0000x reference)
import numpy as np
import concourse.bass as bass
import concourse.bass_isa as bass_isa
import concourse.mybir as mybir
from concourse.ap import AP
from contextlib import ExitStack

F32 = mybir.dt.float32
BF16 = mybir.dt.bfloat16
I32 = mybir.dt.int32
Alu = mybir.AluOpType
Act = mybir.ActivationFunctionType
AX = mybir.AxisListType.X

NC = 8
E = 1024
IF = 2816
TPC = 2048
NT = TPC // 128
NCH = TPC // 512
NIT = IF // 128
SUP = [list(range(0, 11)), list(range(11, 22))]
GTPR = 3
PI = float(np.pi)
LX = np.log([2., 4., 8., 16., 32.])
LXS = float(LX.sum())
DEN = float(5.0 * (LX ** 2).sum() - LXS * LXS)
ENGS = ("sync", "gpsimd", "vector", "scalar", "tensor")


class Sched:
    def __init__(self):
        self.stream = {e: [] for e in ENGS}
        self.count = {}
        self.channels = set()

    def op(self, eng, fn, waits=(), track=False, dma=False, ndma=1, ch=None):
        ch = ch or eng
        self.channels.add(ch)
        amt = (16 * ndma) if dma else 1
        ev = None
        if track:
            self.count[ch] = self.count.get(ch, 0) + amt
            ev = (ch, self.count[ch])
        self.stream[eng].append(
            (tuple(w for w in waits if w is not None), fn, track, dma, ndma, ch))
        return ev

    def emit(self, block, sems):
        for ename in ENGS:
            stream = self.stream[ename]
            def body(engine, stream=stream):
                last = {}
                for waits, fn, track, dma, ndma, ch in stream:
                    for (src, val) in waits:
                        if last.get(src, -1) >= val:
                            continue
                        engine.wait_ge(sems[src], val)
                        last[src] = val
                    ins = fn(engine)
                    if dma:
                        lst = ins if isinstance(ins, (list, tuple)) else [ins]
                        assert len(lst) == ndma, (len(lst), ndma)
                        for i in lst:
                            i.then_inc(sems[ch], 16)
                    elif track:
                        ins.then_inc(sems[ch], 1)
            getattr(block, ename)(body)


def build_nc(stage=3, dbg=False):
    nc = bass.Bass("TRN2", target_bir_lowering=False)
    dt = nc.dram_tensor
    d_xnat = dt("xnat", [TPC, E], F32, kind="ExternalInput")
    d_xtf = dt("xtf", [E, TPC], F32, kind="ExternalInput")
    d_xfirst = dt("xfirst", [1, E], F32, kind="ExternalInput")
    d_ka = dt("ka", [1, E], F32, kind="ExternalInput")
    d_ka2 = dt("ka2", [1, E], F32, kind="ExternalInput")
    d_lx = dt("lxr", [1, 8], F32, kind="ExternalInput")
    d_clip = dt("clipv", [1, 2], F32, kind="ExternalInput")
    d_cos = dt("cosm", [E, E], F32, kind="ExternalInput")
    d_wel = dt("wel", [E, 24], F32, kind="ExternalInput")
    d_bel = dt("bel", [1, 24], F32, kind="ExternalInput")
    d_wle = dt("wle", [24, E], F32, kind="ExternalInput")
    d_wleb = dt("wleb", [1, E], F32, kind="ExternalInput")
    d_mask24 = dt("mask24", [1, 24], F32, kind="ExternalInput")
    d_hmt = dt("hmt", [24, 12], BF16, kind="ExternalInput")
    d_id = dt("ident", [128, 128], BF16, kind="ExternalInput")
    d_wgn = dt("wgn", [GTPR * 128, E], F32, kind="ExternalInput")
    d_wun = dt("wun", [GTPR * 128, E], F32, kind="ExternalInput")
    d_wdn = dt("wdn", [128, IF], F32, kind="ExternalInput")
    d_wgt = dt("wgt", [E, GTPR * 128], F32, kind="ExternalInput")
    d_wut = dt("wut", [E, GTPR * 128], F32, kind="ExternalInput")
    d_wdt = dt("wdt", [IF, 128], F32, kind="ExternalInput")
    d_out = dt("out", [TPC, E], F32, kind="ExternalOutput")
    if dbg:
        d_dbg_hr = dt("dbg_hr", [1, 2048], F32, kind="ExternalOutput")
        d_dbg_z = dt("dbg_z", [TPC, E], F32, kind="ExternalOutput")
        d_dbg_m = dt("dbg_m", [1, 16], F32, kind="ExternalOutput")
        d_dbg_w2 = dt("dbg_w2", [128, 192], BF16, kind="ExternalOutput")
        d_dbg_r = dt("dbg_r", [4, 1024], F32, kind="ExternalOutput")

    d_hrb = dt("hrb", [2048], BF16, kind="Internal")
    d_frb = dt("frb", [1024], F32, kind="Internal")
    d_rowb = dt("rowb", [2048], F32, kind="Internal")
    d_spill = dt("spill", [NIT, 128, TPC], BF16, kind="Internal")
    d_sc_in = dt("sc_in", [128, 8], F32, kind="Internal")
    d_sc_out = dt("sc_out", [NC, 128, 8], F32, kind="Internal", addr_space="Shared")
    d_wq_in = dt("wq_in", [70, 128, 128], BF16, kind="Internal")
    d_wq_out = dt("wq_out", [NC, 70, 128, 128], BF16, kind="Internal", addr_space="Shared")
    d_m1_in = dt("m1_in", [1, 128], F32, kind="Internal")
    d_m1_out = dt("m1_out", [1, 128], F32, kind="Internal")
    d_m2_in = dt("m2_in", [1, 128], F32, kind="Internal")
    d_m2_out = dt("m2_out", [1, 128], F32, kind="Internal")
    RG = [list(range(NC))]

    with ExitStack() as stk:
        def sb(name, shape, dtt):
            return stk.enter_context(nc.sbuf_tensor(name, shape, dtt))
        def psum(name, shape, dtt):
            return stk.enter_context(nc.psum_tensor(name, shape, dtt))

        s_z = sb("s_z", [128, NT * E], BF16)
        s_zqt = sb("s_zqt", [128, 8 * TPC], BF16)
        s_wq = sb("s_wq", [128, 22528], BF16)
        s_xn = sb("s_xn", [128, 2 * E], F32)
        s_xtf = sb("s_xtf", [128, 2 * 8 * 512], BF16)
        s_hkw = sb("s_hkw", [128, 8 * 512], BF16)
        s_y = sb("s_y", [128, E], F32)
        s_ysq = sb("s_ysq", [128, E], F32)
        s_sqs = sb("s_sqs", [128, E], F32)
        s_i32 = sb("s_i32", [128, E], I32)
        s_cosb = sb("s_cosb", [128, 512], F32)
        s_hf = sb("s_hf", [128, 2 * 512], F32)
        s_hrw = sb("s_hrw", [128, 2 * 512], BF16)
        s_hbuf = sb("s_hbuf", [128, IF], BF16)
        s_outb = sb("s_outb", [128, E], F32)
        s_wel2 = sb("s_wel2", [128, 8 * 24], BF16)
        s_wel = sb("s_wel", [128, 8 * 24], BF16)
        s_wle = sb("s_wle", [24, E], BF16)
        s_mask24 = sb("s_mask24", [1, 24], F32)
        s_wleb = sb("s_wleb", [1, E], BF16)
        s_bel = sb("s_bel", [1, 24], BF16)
        s_hmt = sb("s_hmt", [24, 12], BF16)
        s_id = sb("s_id", [128, 128], BF16)
        s_one5 = sb("s_one5", [1, 512], BF16)
        s_onef = sb("s_onef", [1, 128], F32)
        s_onek = sb("s_onek", [12, 1], BF16)
        s_corr = sb("s_corr", [24, 2 * 512], BF16)
        s_bits = sb("s_bits", [24, 2 * 512], BF16)
        s_syn = sb("s_syn", [24, 512], BF16)
        s_frc = sb("s_frc", [128, 8], F32)
        s_clip = sb("s_clip", [1, 2], F32)
        s_wn = sb("s_wn", [128, E], F32)
        s_wtr = sb("s_wtr", [128, GTPR * 128], F32)
        s_rmx = sb("s_rmx", [128, 8], F32)
        s_inv = sb("s_inv", [128, 8], F32)
        s_sc7 = sb("s_sc7", [128, 8], F32)
        s_scall = sb("s_scall", [128, NC * 8], F32)
        s_sgf = sb("s_sgf", [128, 24], F32)
        s_suf = sb("s_suf", [128, 24], F32)
        s_qsc = sb("s_qsc", [128, 24], F32)
        s_qscf = sb("s_qscf", [128, IF], F32)
        s_sdp = sb("s_sdp", [128, 8], F32)
        s_sdx = sb("s_sdx", [128, E], F32)
        s_m1a = sb("s_m1a", [128, 1], F32)
        s_m2a = sb("s_m2a", [128, 1], F32)
        s_m8 = sb("s_m8", [1, 8], F32)
        s_mg = sb("s_mg", [1, 16], F32)
        s_col = sb("s_col", [128, 8], F32)
        s_lnc = sb("s_lnc", [128, 16], F32)

        s_xf1 = s_sdx[0:1, :]
        s_ka = s_qscf[0:1, 896:1920]
        s_ka2 = s_sdx[0:1, :]
        s_pp = s_hf[0:24, 0:512]
        s_t12 = s_hf[0:24, 512:1024]
        pass
        s_flt = s_ysq[0:1, :]
        s_fr = s_y[0:1, :]
        s_hr = s_outb[0:1, 0:E]
        s_invf = s_qscf[:, 0:896]
        s_invr = s_qscf[0:1, 0:896]
        s_sdr = s_sdx[0:1, :]
        s_fix = s_ysq[0:1, 0:512]
        s_fixa = s_ysq[0:1, 512:1024]
        s_scb = s_lnc[0:1, 0:8]
        s_sc = s_qsc[0:1, 0:16]
        s_cnt = s_sgf[0:1, 0:8]
        s_lxr = s_suf[0:1, 0:8]
        s_qht = s_xtf[:, 0:2 * IF]
        s_qst = s_hrw[:, 0:GTPR * 128]

        p_x1 = [psum("p_x1a", [128, E], F32), psum("p_x1b", [128, E], F32)]
        p_s = psum("p_s", [24, 512], F32)
        p_t = psum("p_t", [128, E], BF16)
        p_m = psum("p_m", [128, 512], F32)
        p_m2 = psum("p_m2", [128, 512], F32)
        s_flt2 = p_x1[0][0:1, :]

        blk_holder = []
        S = Sched()
        OP = S.op

        def vchain(steps, waits=(), eng="vector"):
            ev = None
            first = True
            for fn in steps:
                w = list(waits) + ([ev] if ev else []) if first else [ev]
                ev = OP(eng, fn, waits=w, track=True)
                first = False
            return ev

        ev_ld = {}
        def load(name, dst, srcap, eng="sync", waits=()):
            ev_ld[name] = OP(eng, lambda e, d=dst, s=srcap: e.dma_start(out=d, in_=s),
                             waits=waits, track=True, dma=True,
                             ch="sld" if eng == "sync" else "gld")
        load("xf1", s_xf1[:, :], d_xfirst.ap())
        load("ka", s_ka[:, :], d_ka.ap())
        load("lxr", s_lxr[:, :], d_lx.ap())
        load("clip", s_clip[:, :], d_clip.ap())
        load("id", s_id[:, :], d_id.ap())
        load("hmt", s_hmt[:, :], d_hmt.ap())
        load("wel", s_wel[:, :], AP(d_wel, 0, [[24, 128], [3072, 8], [1, 24]]), "gpsimd")
        load("bel", s_bel[:, :], d_bel.ap(), "gpsimd")
        load("wle", s_wle[:, :], d_wle.ap(), "gpsimd")
        load("mask24", s_mask24[:, :], d_mask24.ap())
        load("wleb", s_wleb[:, :], d_wleb.ap(), "gpsimd")
        ev_sld7 = ("sld", S.count["sld"])
        ev_gld4 = ("gld", S.count["gld"])
        for _n in ("xf1", "ka", "lxr", "clip", "id", "hmt", "mask24"):
            ev_ld[_n] = ev_sld7
        for _n in ("wel", "bel", "wle", "wleb"):
            ev_ld[_n] = ev_gld4
        ev_ms = OP("vector", lambda v: (
            v.memset(s_one5[:, :], 1.0), v.memset(s_onef[:, :], 1.0),
            v.memset(s_onek[:, :], 1.0), v.memset(s_syn[:, :], 0.0),
            v.memset(s_m1a[:, :], 0.0), v.memset(s_col[:, 4:5], 1e-5),
            v.memset(s_m2a[:, :], 0.0), v.memset(s_m8[:, :], 0.0))[-1], track=True)
        ev_xn = {}
        ev_xc = {}
        for t in range(2):
            ev_xn[t] = OP("sync", lambda e, t=t: e.dma_start(
                out=s_xn[:, (t % 2) * E:(t % 2) * E + E],
                in_=d_xnat.ap()[t * 128:(t + 1) * 128, :]), track=True, dma=True,
                ch="xna" if t % 2 == 0 else "xnb")
        def _mk_xc(c):
            def _xc(g):
                return g.dma_start(
                    out=s_xtf[:, (c % 2) * 4096:(c % 2) * 4096 + 4096].rearrange(
                        "p (q c2) -> p q c2", q=8),
                    in_=d_xtf.ap()[:, c * 512:(c + 1) * 512].rearrange(
                        "(q p) c2 -> q p c2", q=8).transpose([1, 0, 2]))
            return _xc
        for c in range(2):
            ev_xc[c] = OP("gpsimd", _mk_xc(c), track=True, dma=True, ndma=1,
                          ch="xca" if c % 2 == 0 else "xcb")
        for wi in range(64):
            OP("tensor", lambda t: t.matmul(
                p_m[:, 0:512], lhsT=s_id[:, :], rhs=s_xtf[:, 0:512],
                start=True, stop=True),
               waits=[ev_ld["id"], ev_xc[0]] if wi == 0 else ())

        ev_q_ag = ev_sc_ag = ev_invf = None
        ev_qw_last = None
        ev_push_all = []
        ev_dbgfr = None
        ev_cntl = None
        for j, n_ in enumerate([4, 16, 64, 256, 1024]):
            evs = OP("vector", lambda v, n=n_: v.tensor_reduce(
                s_sc[0:1, 0:1], s_xf1[0:1, 0:n], AX, Alu.add),
                waits=[ev_ld["xf1"]], track=True)
            evm = OP("vector", lambda v, n=n_: v.tensor_scalar(
                out=s_sc[0:1, 1:2], in0=s_sc[0:1, 0:1], scalar1=1.0 / n,
                scalar2=None, op0=Alu.mult), waits=[evs], track=True)
            evg = OP("vector", lambda v, n=n_: v.tensor_scalar(
                out=s_flt[0:1, 0:n], in0=s_xf1[0:1, 0:n],
                scalar1=s_sc[0:1, 1:2], scalar2=None, op0=Alu.is_gt),
                waits=[evm], track=True)
            evc = OP("vector", lambda v, n=n_: v.tensor_reduce(
                s_sc[0:1, 2:3], s_flt[0:1, 0:n], AX, Alu.add),
                waits=[evg], track=True)
            ev_cntl = OP("vector", lambda v, jj=j: v.tensor_scalar(
                out=s_cnt[0:1, jj:jj + 1], in0=s_sc[0:1, 2:3],
                scalar1=1.0, scalar2=None, op0=Alu.max), waits=[evc], track=True)
        ev_lncnt = OP("scalar", lambda s: s.activation(out=s_cnt[0:1, 0:5],
                                                       in_=s_cnt[0:1, 0:5], func=Act.Ln),
                      waits=[ev_cntl], track=True)
        load("ka2", s_ka2[:, :], d_ka2.ap(), waits=[ev_cntl])
        ev_dsa = vchain([
            lambda v: v.tensor_reduce(s_sc[0:1, 3:4], s_cnt[0:1, 0:5], AX, Alu.add),
            lambda v: v.tensor_tensor(out=s_scb[0:1, 0:5], in0=s_cnt[0:1, 0:5],
                                      in1=s_lxr[0:1, 0:5], op=Alu.mult),
            lambda v: v.tensor_reduce(s_sc[0:1, 4:5], s_scb[0:1, 0:5], AX, Alu.add),
        ], waits=[ev_lncnt, ev_ld["lxr"]])
        ev_dsc = vchain([
            lambda v: v.tensor_scalar(out=s_sc[0:1, 5:6], in0=s_sc[0:1, 3:4], scalar1=LXS,
                                      scalar2=None, op0=Alu.mult),
            lambda v: v.tensor_scalar(out=s_sc[0:1, 6:7], in0=s_sc[0:1, 4:5], scalar1=5.0,
                                      scalar2=None, op0=Alu.mult),
            lambda v: v.tensor_tensor(out=s_sc[0:1, 7:8], in0=s_sc[0:1, 5:6],
                                      in1=s_sc[0:1, 6:7], op=Alu.subtract),
            lambda v: v.tensor_scalar(out=s_sc[0:1, 8:9], in0=s_sc[0:1, 7:8],
                                      scalar1=1.0 / DEN, scalar2=None, op0=Alu.mult),
            lambda v: v.tensor_scalar(out=s_sc[0:1, 9:10], in0=s_sc[0:1, 8:9], scalar1=1.0,
                                      scalar2=2.0, op0=Alu.max, op1=Alu.min),
            lambda v: v.tensor_scalar(out=s_sc[0:1, 10:11], in0=s_sc[0:1, 9:10],
                                      scalar1=0.2, scalar2=1.0, op0=Alu.mult, op1=Alu.add),
            lambda v: v.tensor_tensor(out=s_sc[0:1, 11:12], in0=s_sc[0:1, 10:11],
                                      in1=s_sc[0:1, 9:10], op=Alu.mult),
            lambda v: v.tensor_scalar(out=s_sc[0:1, 12:13], in0=s_sc[0:1, 9:10],
                                      scalar1=0.5, scalar2=1.0, op0=Alu.mult, op1=Alu.add),
            lambda v: v.tensor_scalar(out=s_sc[0:1, 13:14], in0=s_sc[0:1, 9:10],
                                      scalar1=0.1, scalar2=None, op0=Alu.mult),
        ], waits=[ev_dsa])
        def _arg1(v):
            v.tensor_scalar(out=s_flt[0:1, :], in0=s_ka[0:1, :], scalar1=s_sc[0:1, 12:13],
                            scalar2=s_sc[0:1, 11:12], op0=Alu.mult, op1=Alu.add)
            v.tensor_scalar(out=s_flt2[0:1, :], in0=s_flt[0:1, :], scalar1=PI,
                            scalar2=None, op0=Alu.is_gt)
            return v.scalar_tensor_tensor(out=s_flt[0:1, :], in0=s_flt2[0:1, :],
                                          scalar=-2.0 * PI, in1=s_flt[0:1, :],
                                          op0=Alu.mult, op1=Alu.add)
        ev_arg1 = OP("vector", _arg1, waits=[ev_dsc, ev_ld["ka"]], track=True)
        ev_sin1 = OP("scalar", lambda s: s.activation(out=s_fr[0:1, :], in_=s_flt[0:1, :],
                                                      func=Act.Sin),
                     waits=[ev_arg1], track=True)
        def _arg2(v):
            v.tensor_scalar(out=s_flt[0:1, :], in0=s_ka2[0:1, :], scalar1=s_sc[0:1, 13:14],
                            scalar2=s_sc[0:1, 11:12], op0=Alu.mult, op1=Alu.add)
            v.tensor_tensor(out=s_flt[0:1, :], in0=s_flt[0:1, :], in1=s_ka[0:1, :],
                            op=Alu.subtract)
            v.tensor_scalar(out=s_flt[0:1, :], in0=s_flt[0:1, :], scalar1=PI / 2,
                            scalar2=None, op0=Alu.add)
            v.tensor_scalar(out=s_flt2[0:1, :], in0=s_flt[0:1, :], scalar1=PI,
                            scalar2=None, op0=Alu.is_gt)
            return v.scalar_tensor_tensor(out=s_flt[0:1, :], in0=s_flt2[0:1, :],
                                          scalar=-2.0 * PI, in1=s_flt[0:1, :],
                                          op0=Alu.mult, op1=Alu.add)
        ev_arg2 = OP("vector", _arg2, waits=[ev_sin1, ev_ld["ka2"]], track=True)
        ev_sin2 = OP("scalar", lambda s: s.activation(out=s_flt2[0:1, :], in_=s_flt[0:1, :],
                                                      func=Act.Sin),
                     waits=[ev_arg2], track=True)
        ev_fr = OP("vector", lambda v: (
            v.tensor_tensor(out=s_fr[0:1, :], in0=s_fr[0:1, :], in1=s_flt2[0:1, :],
                            op=Alu.mult),
            v.tensor_scalar(out=s_fr[0:1, :], in0=s_fr[0:1, :], scalar1=1.0 / 1024.0,
                            scalar2=None, op0=Alu.mult))[-1],
            waits=[ev_sin2], track=True)
        ev_frs = OP("sync", lambda g: g.dma_start(out=AP(d_frb, 0, [[1024, 1], [1, 1024]]),
                                                  in_=s_fr[0:1, :]),
                    waits=[ev_fr], track=True, dma=True, ch="hrch")
        ev_cosl = {}
        ev_cosl[0] = OP("sync", lambda e: e.dma_start(
            out=s_cosb[:, :], in_=d_cos.ap()[0:128, 0:512]), track=True, dma=True,
            ch="cosa")
        ev_frc = OP("sync", lambda e: e.dma_start(out=s_frc[:, :],
                                                  in_=AP(d_frb, 0, [[1, 128], [128, 8]]),
                                                  allow_slow_non_contiguous=True),
                    waits=[ev_frs], track=True, dma=True, ch="frc")
        if dbg:
            OP("sync", lambda e: e.dma_start(out=d_dbg_r.ap()[0:1, 0:16],
                                             in_=s_sc[0:1, 0:16]),
               waits=[ev_arg2], track=True, dma=True, ch="dbg")
            OP("sync", lambda e: e.dma_start(out=d_dbg_r.ap()[1:2, 0:8],
                                             in_=s_cnt[0:1, 0:8]),
               waits=[ev_arg2], track=True, dma=True, ch="dbg")
            ev_dbgfr = OP("sync", lambda e: e.dma_start(out=d_dbg_r.ap()[2:3, :],
                                                        in_=s_fr[0:1, :]),
                          waits=[ev_frs], track=True, dma=True, ch="dbg")
        if stage >= 3:
            ev_red = {}
            _wb = [s_wn, s_sqs, s_outb]
            for i in range(9):
                m_, ii = divmod(i, 3)
                w = 768 if i == 8 else 1024
                buf = _wb[i % 3]
                def _ld(e, m_=m_, ii=ii, w=w, buf=buf):
                    if m_ == 0:
                        return e.dma_start(out=buf[:, 0:w],
                                           in_=d_wgn.ap()[ii * 128:(ii + 1) * 128, :])
                    if m_ == 1:
                        return e.dma_start(out=buf[:, 0:w],
                                           in_=d_wun.ap()[ii * 128:(ii + 1) * 128, :])
                    return e.dma_start(out=buf[:, 0:w],
                                       in_=d_wdn.ap()[:, ii * 1024:ii * 1024 + w])
                evl = OP("gpsimd", _ld, waits=[ev_red.get(i - 3)], track=True,
                         dma=True, ch="gw" + "abc"[i % 3])
                dcol = i if m_ < 2 else 7
                evr1 = OP("vector", lambda v, w=w, buf=buf, dcol=dcol: v.tensor_reduce(
                    s_rmx[:, dcol:dcol + 1], buf[:, 0:w], AX, Alu.max,
                    apply_absolute_value=True), waits=[evl], track=True)
                if m_ < 2:
                    ev_red[i] = evr1
                else:
                    def _rd(v, ii=ii):
                        if ii == 0:
                            return v.tensor_copy(out=s_rmx[:, 6:7], in_=s_rmx[:, 7:8])
                        return v.tensor_tensor(out=s_rmx[:, 6:7], in0=s_rmx[:, 6:7],
                                               in1=s_rmx[:, 7:8], op=Alu.max)
                    ev_red[i] = OP("vector", _rd, waits=[evr1], track=True)
            ev_inv = vchain([
                lambda v: v.reciprocal(out=s_inv[:, 0:7], in_=s_rmx[:, 0:7]),
                lambda v: v.tensor_scalar(out=s_inv[:, 0:7], in0=s_inv[:, 0:7],
                                          scalar1=127.0, scalar2=None, op0=Alu.mult),
                lambda v: v.tensor_scalar(out=s_sc7[:, 0:7], in0=s_rmx[:, 0:7],
                                          scalar1=1.0 / 127.0, scalar2=None, op0=Alu.mult),
            ], waits=[ev_red[i] for i in range(9)])
            ev_scs = OP("gpsimd", lambda g: g.dma_start(out=d_sc_in.ap(), in_=s_sc7[:, :]),
                        waits=[ev_inv], track=True, dma=True, ch="grt")
            ev_sc_ag = OP("gpsimd", lambda g: g.collective_compute(
                "AllGather", Alu.bypass, replica_groups=RG,
                ins=[d_sc_in.ap().opt()], outs=[d_sc_out.ap().opt()]),
                waits=[ev_scs], track=True, ch="cc")
            ev_invs = OP("gpsimd", lambda g: g.dma_start(
                out=AP(d_rowb, 0, [[1, 128], [128, 7]]), in_=s_inv[:, 0:7],
                allow_slow_non_contiguous=True),
                waits=[ev_inv], track=True, dma=True, ch="grt")

        ev_hrmm = {}
        for idx in range(16):
            kc, nk = divmod(idx, 2)
            buf = s_cosb[:, :] if idx % 2 == 0 else s_y[:, 0:512]
            if idx >= 1:
                ev_cosl[idx] = OP("sync" if idx % 2 == 0 else "scalar",
                    lambda e, kc=kc, nk=nk, buf=buf: e.dma_start(
                    out=buf,
                    in_=d_cos.ap()[kc * 128:(kc + 1) * 128, nk * 512:(nk + 1) * 512]),
                    waits=[ev_hrmm.get(idx - 2)] +
                          ([ev_frs, ev_cosl[0], ev_dbgfr] if idx == 1 else []),
                    track=True, dma=True, ch="cosa" if idx % 2 == 0 else "cosb")
            pdst = p_m if nk == 0 else p_m2
            ev_hrmm[idx] = OP("tensor", lambda t, kc=kc, pd=pdst, buf=buf: t.matmul(
                pd[0:1, 0:512], lhsT=s_frc[:, kc:kc + 1], rhs=buf,
                start=(kc == 0), stop=(kc == 7)),
                waits=[ev_cosl[idx], ev_frc], track=True)
        ev_hrev = OP("vector", lambda v: (
            v.tensor_copy(out=s_hr[0:1, 0:512], in_=p_m[0:1, 0:512]),
            v.tensor_copy(out=s_hr[0:1, 512:1024], in_=p_m2[0:1, 0:512]))[-1],
            waits=[ev_hrmm[14], ev_hrmm[15]], track=True)
        ev_hrs = OP("gpsimd", lambda g: [
            g.dma_start(out=AP(d_hrb, 128, [[1024, 1], [1, 1024]]), in_=s_hr[0:1, :]),
            g.dma_start(out=AP(d_hrb, 0, [[128, 1], [1, 128]]), in_=s_hr[0:1, 896:1024]),
            g.dma_start(out=AP(d_hrb, 1152, [[896, 1], [1, 896]]), in_=s_hr[0:1, 0:896])],
            waits=[ev_hrev], track=True, dma=True, ndma=3, ch="hrch")
        ev_hkw = OP("scalar", lambda g: g.dma_start(
            out=s_hkw[:, :].rearrange("p (d c) -> p d c", d=8),
            in_=AP(d_hrb, 1, [[1, 128], [128, 8], [1, 512]])),
            waits=[ev_hrs], track=True, dma=True, ndma=1, ch="hrch")
        if stage >= 3:
            ev_invr = OP("sync", lambda e: e.dma_start(
                out=s_invr[0:1, :], in_=AP(d_rowb, 0, [[896, 1], [1, 896]])),
                waits=[ev_invs], track=True, dma=True, ch="invr")
            ev_bcq = {}
            for ch in range(2):
                w = 512 if ch == 0 else 384
                evb = OP("tensor", lambda t, ch=ch, w=w: t.matmul(
                    p_x1[0][:, ch * 512:ch * 512 + w], lhsT=s_onef[:, :],
                    rhs=s_invr[0:1, ch * 512:ch * 512 + w],
                    start=True, stop=True),
                    waits=[ev_invr, ev_ms], track=True)
                ev_bcq[ch] = OP("vector", lambda v, ch=ch, w=w: v.tensor_copy(
                    out=s_invf[:, ch * 512:ch * 512 + w],
                    in_=p_x1[0][:, ch * 512:ch * 512 + w]),
                    waits=[evb], track=True)
            ev_invf = ev_bcq[1]
            wbufs = [s_wtr, s_sqs[:, 0:GTPR * 128], s_sqs[:, 512:896],
                     s_ysq[:, 0:GTPR * 128]]
            qbufs = [s_hrw[:, 0:GTPR * 128], s_hrw[:, 512:896],
                     s_zqt[:, 0:GTPR * 128], s_zqt[:, 512:896]]
            ev_qw = {}
            ev_push = {}
            gidx = 0
            for m in range(3):
                src = [d_wgt, d_wut, d_wdt][m]
                nht = 8 if m < 2 else NIT
                wd_ = 384 if m < 2 else 128
                for ht in range(nht):
                    gi = gidx; gidx += 1
                    par = gi % 4
                    wbuf = wbufs[par]
                    qbuf = qbufs[par]
                    ib = (gi % 2) * 512
                    ev_wl = OP("sync", lambda e, src=src, ht=ht, w=wd_, wbuf=wbuf:
                               e.dma_start(out=wbuf[:, 0:w],
                                           in_=src.ap()[ht * 128:(ht + 1) * 128, :]),
                               waits=[ev_qw.get(gi - 4),
                                      ev_inv if gi == 1 else None],
                               track=True, dma=True, ch="wtr" + "abcd"[par])
                    ev_qw[gi] = OP("vector", lambda v, m=m, w=wd_, wbuf=wbuf,
                                   qbuf=qbuf, ib=ib: (
                        v.tensor_tensor(out=s_i32[:, ib:ib + w],
                                        in0=wbuf[:, 0:w],
                                        in1=s_invf[:, (0, 384, 768)[m]:(0, 384, 768)[m] + w],
                                        op=Alu.mult),
                        v.tensor_copy(out=qbuf[:, 0:w], in_=s_i32[:, ib:ib + w]))[-1],
                        waits=[ev_wl, ev_invf, ev_push.get(gi - 4)], track=True)
                    def _push(g, m=m, ht=ht, qbuf=qbuf):
                        if m < 2:
                            return g.dma_start(
                                out=AP(d_wq_in, (m * 24 + ht) * 16384,
                                       [[128, 128], [8 * 16384, GTPR], [1, 128]]),
                                in_=qbuf[:, 0:GTPR * 128].rearrange(
                                    "p (a c2) -> p a c2", a=GTPR))
                        return g.dma_start(out=d_wq_in.ap()[48 + ht],
                                           in_=qbuf[:, 0:128])
                    ev_push[gi] = OP("gpsimd", _push, waits=[ev_qw[gi]], track=True,
                                     dma=True, ndma=1, ch="push" + "abcd"[par])
            ev_qw_last = ev_qw[gidx - 1]
            ev_push_all = [ev_push[gidx - 1], ev_push[gidx - 2],
                           ev_push[gidx - 3], ev_push[gidx - 4]]
            ev_q_ag = OP("gpsimd", lambda g: g.collective_compute(
                "AllGather", Alu.bypass, replica_groups=RG,
                ins=[d_wq_in.ap().opt()], outs=[d_wq_out.ap().opt()]),
                waits=ev_push_all, track=True, ch="cc")
        def _w2all(t):
            ins = None
            for Iq in range(8):
                for J in range(8):
                    d = (J - Iq) % 8
                    ins = t.matmul(p_m[:, Iq * 24:(Iq + 1) * 24],
                                   lhsT=s_hkw[:, d * 512:d * 512 + 128],
                                   rhs=s_wel[:, J * 24:(J + 1) * 24],
                                   start=(J == 0), stop=(J == 7))
            return ins
        evm = OP("tensor", _w2all, waits=[ev_hkw, ev_ld["wel"], ev_hrev], track=True)
        ev_w2_all = OP("scalar", lambda sc: sc.activation(
            out=s_wel2[:, :], in_=p_m[:, 0:192], func=Act.Copy),
            waits=[evm], track=True)
        ev_dbg_w2dma = None
        if dbg:
            OP("sync", lambda e: e.dma_start(out=d_dbg_r.ap()[3:4, :],
                                             in_=s_hr[0:1, :]),
               waits=[ev_hrs], track=True, dma=True, ch="dbg")
            evc = OP("vector", lambda v: v.tensor_copy(out=s_hbuf[:, 0:192],
                                                       in_=s_wel2[:, :]),
                     waits=[ev_w2_all], track=True)
            ev_dbg_w2dma = OP("sync", lambda e: e.dma_start(out=d_dbg_w2.ap(),
                                                           in_=s_hbuf[:, 0:192]),
                              waits=[evc], track=True, dma=True, ch="dbg")
            OP("gpsimd", lambda e: e.dma_start(out=d_dbg_hr.ap(),
                                               in_=AP(d_hrb, 0, [[2048, 1], [1, 2048]])),
               waits=[ev_hrs], track=True, dma=True, ch="dbg")





        ev_y = {}
        ev_sq = {}
        ev_lnz = {}
        ev_ztr = {}
        ev_zev = {}
        ev_x1d = {}
        ev_dbgz = {}
        ev_corr = {}
        ev_gst = {}
        def emit_chain_step(c, step):
            xb = (c % 2) * 4096
            cb = (c % 2) * 512
            if step == 0:
                emit_chain_s0(c, xb, cb)
            elif step == 1:
                emit_chain_s1(c, cb)
            elif step == 2:
                emit_chain_s2(c)
            else:
                emit_chain_s3(c, cb)
        def emit_chain(c):
            for st_ in range(4):
                emit_chain_step(c, st_)
        def emit_chain_s0(c, xb, cb):
            def _lat(t, c=c, xb=xb):
                ins = None
                for Iq in range(8):
                    ins = t.matmul(p_s[:, :], lhsT=s_wel2[:, Iq * 24:(Iq + 1) * 24],
                                   rhs=s_xtf[:, xb + Iq * 512:xb + Iq * 512 + 512],
                                   start=(Iq == 0), stop=False)
                return t.matmul(p_s[:, :], lhsT=s_bel[:, :], rhs=s_one5[:, :],
                                start=False, stop=True)
            evlat = OP("tensor", _lat,
                       waits=[ev_xc[c], ev_w2_all, ev_ld["bel"], ev_ms, ev_invf,
                              *ev_push_all,
                              ev_corr.get(c - 1) if c >= 1 else None],
                       track=True)
            ev_gst[(c, "bits")] = OP("vector", lambda v, cb=cb: v.tensor_scalar(
                out=s_bits[:, cb:cb + 512], in0=p_s[:, :], scalar1=0.0, scalar2=None,
                op0=Alu.is_gt), waits=[evlat], track=True)
        def emit_chain_s1(c, cb):
            evsyn = OP("tensor", lambda t, cb=cb: t.matmul(
                p_s[0:12, :], lhsT=s_hmt[:, :], rhs=s_bits[:, cb:cb + 512],
                start=True, stop=True),
                waits=[ev_gst[(c, "bits")], ev_ld["hmt"]], track=True)
            ev_gst[(c, "synb")] = OP("vector", lambda v: v.tensor_scalar(
                out=s_syn[0:12, :], in0=p_s[0:12, :], scalar1=0.5, scalar2=None,
                op0=Alu.is_gt), waits=[evsyn], track=True)
        def emit_chain_s2(c):
            evw = OP("tensor", lambda t: t.matmul(p_s[0:1, :], lhsT=s_onek[:, :],
                                                  rhs=s_syn[0:12, :], start=True, stop=True),
                     waits=[ev_gst[(c, "synb")]], track=True)
            ev_gst[(c, "fix")] = OP("vector", lambda v: (
                v.tensor_scalar(out=s_fixa[0:1, :], in0=p_s[0:1, :], scalar1=3.5,
                                scalar2=None, op0=Alu.is_lt),
                v.scalar_tensor_tensor(out=s_fix[0:1, :], in0=p_s[0:1, :], scalar=0.5,
                                       op0=Alu.is_gt, in1=s_fixa[0:1, :], op1=Alu.mult))[-1],
                waits=[evw], track=True)
        def emit_chain_s3(c, cb):
            evfb = OP("tensor", lambda t: t.matmul(p_s[0:24, :], lhsT=s_mask24[:, :],
                                                   rhs=s_fix[0:1, :], start=True, stop=True),
                      waits=[ev_gst[(c, "fix")], ev_ld["mask24"]], track=True)
            ev_corr[c] = OP("vector", lambda v, cb=cb: (
                v.tensor_tensor(out=s_pp[:, :], in0=s_syn[:, :], in1=p_s[0:24, :],
                                op=Alu.mult),
                v.tensor_scalar(out=s_t12[:, :], in0=s_bits[:, cb:cb + 512], scalar1=-2.0,
                                scalar2=1.0, op0=Alu.mult, op1=Alu.add),
                v.tensor_tensor(out=s_pp[:, :], in0=s_pp[:, :], in1=s_t12[:, :],
                                op=Alu.mult),
                v.tensor_tensor(out=s_corr[:, cb:cb + 512], in0=s_bits[:, cb:cb + 512],
                                in1=s_pp[:, :], op=Alu.add))[-1],
                waits=[evfb, ev_x1d.get((c - 2, 3))], track=True)

        def emit_head(t):
            c, sub = divmod(t, 4)
            xb = (c % 2) * 4096
            cb = (c % 2) * 512
            par = t % 2
            lb = (t % 2) * 8
            def _x1(te, c=c, sub=sub, par=par, xb=xb, cb=cb):
                ins = None
                for Jc in range(2):
                    for Iq in range(8):
                        d0 = (4 * Jc - Iq) % 8
                        ins = te.matmul(
                            p_x1[par][:, Jc * 512:(Jc + 1) * 512],
                            lhsT=s_xtf[:, xb + Iq * 512 + sub * 128:
                                       xb + Iq * 512 + sub * 128 + 128],
                            rhs=s_hkw[:, d0 * 512:(d0 + 1) * 512],
                            start=(Iq == 0), stop=False)
                    te.matmul(
                        p_x1[par][:, Jc * 512:(Jc + 1) * 512],
                        lhsT=s_corr[:, cb + sub * 128:cb + sub * 128 + 128],
                        rhs=s_wle[:, Jc * 512:(Jc + 1) * 512],
                        start=False, stop=False)
                    ins = te.matmul(
                        p_x1[par][:, Jc * 512:(Jc + 1) * 512],
                        lhsT=s_one5[0:1, 0:128],
                        rhs=s_wleb[0:1, Jc * 512:(Jc + 1) * 512],
                        start=False, stop=True)
                return ins
            ev_x1d[(c, sub)] = OP("tensor", _x1,
                                  waits=[ev_corr[c], ev_hkw, ev_ld["wle"],
                                         ev_ld["wleb"], ev_y.get(t - 2)], track=True)
            yb = s_y if t % 2 == 0 else s_sdx
            ev_y[t] = OP("vector", lambda v, t=t, par=par, lb=lb, yb=yb:
                         v.scalar_tensor_tensor(
                out=yb[:, :], in0=p_x1[par][:, :], scalar=1.0, op0=Alu.mult,
                in1=s_xn[:, (t % 2) * E:(t % 2) * E + E], op1=Alu.add,
                accum_out=s_lnc[:, lb:lb + 1]),
                waits=[ev_x1d[(c, sub)], ev_xn[t], ev_lnz.get(t - 2)], track=True)
            if t + 2 < NT:
                ev_xn[t + 2] = OP("scalar", lambda e, t2=t + 2: e.dma_start(
                    out=s_xn[:, (t2 % 2) * E:(t2 % 2) * E + E],
                    in_=d_xnat.ap()[t2 * 128:(t2 + 1) * 128, :]),
                    waits=[ev_y[t]], track=True, dma=True,
                    ch="xna" if t % 2 == 0 else "xnb")
            ev_sq[t] = OP("scalar", lambda s, lb=lb, yb=yb: s.activation(
                out=s_sqs[:, :], in_=yb[:, :], func=Act.Square,
                accum_out=s_lnc[:, lb + 1:lb + 2]),
                waits=[ev_y[t], ev_qw_last if t == 0 else None], track=True)

        ev_s6 = {}
        ev_za = {}
        def emit_sch_a(t):
            lb = (t % 2) * 8
            ev1 = OP("scalar", lambda s, lb=lb: s.activation(
                out=s_lnc[:, lb + 2:lb + 3], in_=s_lnc[:, lb:lb + 1],
                func=Act.Identity, scale=1.0 / E), waits=[ev_y[t]], track=True)
            ev2 = OP("scalar", lambda s, lb=lb: s.activation(
                out=s_lnc[:, lb + 3:lb + 4], in_=s_lnc[:, lb + 2:lb + 3],
                func=Act.Copy, scale=-1.0), waits=[ev1], track=True)
            ev3 = OP("scalar", lambda s, lb=lb: s.activation(
                out=s_lnc[:, lb + 4:lb + 5], in_=s_lnc[:, lb + 2:lb + 3],
                func=Act.Square), waits=[ev2], track=True)
            ev4 = OP("scalar", lambda s, lb=lb: s.activation(
                out=s_lnc[:, lb + 4:lb + 5], in_=s_lnc[:, lb + 4:lb + 5],
                func=Act.Copy, scale=-1.0), waits=[ev3], track=True)
            ev5 = OP("scalar", lambda s, lb=lb: s.activation(
                out=s_lnc[:, lb + 5:lb + 6], in_=s_lnc[:, lb + 1:lb + 2],
                func=Act.Identity, scale=1.0 / E, bias=s_lnc[:, lb + 4:lb + 5]),
                waits=[ev4, ev_sq[t]], track=True)
            ev_s6[t] = OP("scalar", lambda s, lb=lb: s.activation(
                out=s_lnc[:, lb + 5:lb + 6], in_=s_lnc[:, lb + 5:lb + 6],
                func=Act.Sqrt, bias=s_col[:, 4:5]), waits=[ev5], track=True)
        def emit_vtail(t):
            lb = (t % 2) * 8
            ev_za[t] = OP("vector", lambda v, lb=lb: v.reciprocal(
                out=s_lnc[:, lb + 6:lb + 7], in_=s_lnc[:, lb + 5:lb + 6]),
                waits=[ev_s6[t]], track=True)
        def emit_sch_b(t):
            lb = (t % 2) * 8
            yb = s_y if t % 2 == 0 else s_sdx
            ev7 = OP("scalar", lambda s, lb=lb: s.activation(
                out=s_lnc[:, lb + 7:lb + 8], in_=s_lnc[:, lb + 6:lb + 7],
                func=Act.Identity, scale=s_lnc[:, lb + 3:lb + 4]),
                waits=[ev_za[t]], track=True)
            ev_lnz[t] = OP("scalar", lambda s, t=t, lb=lb, yb=yb: s.activation(
                out=s_z[:, t * E:(t + 1) * E], in_=yb[:, :],
                func=Act.Identity, scale=s_lnc[:, lb + 6:lb + 7],
                bias=s_lnc[:, lb + 7:lb + 8]), waits=[ev7], track=True)
        def emit_zrow(t):
            evzr = OP("vector", lambda v, t=t: v.tensor_reduce(
                s_col[:, 5 + t % 2:6 + t % 2], s_z[:, t * E:(t + 1) * E], AX,
                Alu.max, apply_absolute_value=True), waits=[ev_lnz[t]], track=True)
            return OP("vector", lambda v, t=t: v.tensor_tensor(
                out=s_m1a[:, :], in0=s_m1a[:, :],
                in1=s_col[:, 5 + t % 2:6 + t % 2],
                op=Alu.max), waits=[evzr], track=True)

        def emit_tail_t(t):
            ev_ztr[t] = OP("tensor", lambda te, t=t: [te.transpose(
                p_t[:, ht * 128:(ht + 1) * 128],
                s_z[:, t * E + ht * 128:t * E + ht * 128 + 128],
                s_id[:, :]) for ht in range(8)][-1],
                waits=[ev_lnz[t], ev_ld["id"], ev_zev.get(t - 1)], track=True)
            ev_zev[t] = OP("vector", lambda v, t=t: v.tensor_copy(
                out=s_zqt[:, :].rearrange("p (h c) -> p h c",
                                          h=8)[:, :, t * 128:(t + 1) * 128],
                in_=p_t[:, :].rearrange("p (h c) -> p h c", h=8)),
                waits=[ev_ztr[t]] + (ev_push_all if t == 0 else []), track=True)
            if dbg:
                evdz = OP("vector", lambda v, t=t: v.tensor_copy(
                    out=s_outb[:, :], in_=s_z[:, t * E:(t + 1) * E]),
                    waits=[ev_lnz[t], ev_dbgz.get(t - 1)], track=True)
                ev_dbgz[t] = OP("sync", lambda e, t=t: e.dma_start(
                    out=d_dbg_z.ap()[t * 128:(t + 1) * 128, :], in_=s_outb[:, :]),
                    waits=[evdz], track=True, dma=True, ch="dbg")

        emit_chain(0)
        ev_m1u = None
        for t in range(NT):
            c, sub = divmod(t, 4)
            emit_head(t)
            if c + 1 < NCH:
                emit_chain_step(c + 1, sub)
            if t >= 1:
                emit_vtail(t - 1)
                emit_sch_b(t - 1)
            emit_sch_a(t)
            if t >= 2:
                ev_m1u = emit_zrow(t - 2)
            if t >= 3:
                emit_tail_t(t - 3)
            if sub == 3:
                if c + 2 < NCH:
                    ev_xc[c + 2] = OP("gpsimd", _mk_xc(c + 2),
                                      waits=[ev_x1d[(c, 3)]], track=True, dma=True,
                                      ndma=1, ch="xca" if c % 2 == 0 else "xcb")
        emit_vtail(NT - 1)
        emit_sch_b(NT - 1)
        ev_m1u = emit_zrow(NT - 2)
        ev_m1u = emit_zrow(NT - 1)
        emit_tail_t(NT - 3)
        emit_tail_t(NT - 2)
        emit_tail_t(NT - 1)
        for wi in range(110):
            OP("tensor", lambda t: t.matmul(
                p_s[:, :], lhsT=s_id[:, 0:24], rhs=s_hkw[:, 0:512],
                start=True, stop=True),
               waits=[ev_corr[NCH - 1]] if wi == 0 else ())

        if stage < 3:
            ev_m1c = OP("gpsimd", lambda g: g.dma_start(out=AP(d_rowb, 0, [[1, 128]]),
                                                        in_=s_m1a[:, 0:1]),
                        waits=[ev_m1u, ev_ms], track=True, dma=True, ch="m1g")
            ev_m1r = OP("sync", lambda e: e.dma_start(
                out=s_y[0:1, 0:128], in_=AP(d_rowb, 0, [[128, 1], [1, 128]])),
                waits=[ev_m1c], track=True, dma=True, ch="m1r")
            ev_m1p = OP("vector", lambda v: v.tensor_reduce(
                s_m8[0:1, 0:1], s_y[0:1, 0:128], AX, Alu.max),
                waits=[ev_m1r], track=True)
            ev_od = {}
            for t in range(NT):
                evc = OP("vector", lambda v, t=t: v.tensor_copy(
                    out=s_outb[:, :], in_=s_z[:, t * E:(t + 1) * E]),
                    waits=[ev_lnz[t], ev_od.get(t - 1)], track=True)
                ev_od[t] = OP("sync", lambda e, t=t: e.dma_start(
                    out=d_out.ap()[t * 128:(t + 1) * 128, :], in_=s_outb[:, :]),
                    waits=[evc], track=True, dma=True, ch="out")
            if dbg:
                evm8 = OP("gpsimd", lambda g: g.dma_start(out=d_dbg_m.ap()[0:1, 0:8],
                                                          in_=s_m8[:, :]),
                          waits=[ev_m1p], track=True, dma=True, ch="dbg")
                OP("gpsimd", lambda g: g.engine_nop(), waits=[evm8])
            OP("sync", lambda e: e.nop(), waits=[ev_od[NT - 1]])
            sems = {c: stk.enter_context(nc.semaphore("sem_" + c)) for c in sorted(S.channels)}
            blk = stk.enter_context(nc.Block())
            S.emit(blk, sems)
            return nc

        ev_scl = OP("sync", lambda e: e.dma_start(
            out=s_scall[:, :], in_=AP(d_sc_out, 0, [[8, 128], [1024, 8], [1, 8]]),
            allow_slow_non_contiguous=True),
            waits=[ev_sc_ag], track=True, dma=True, ch="scl")
        def _wqld_mk(ch, base):
            def _wqld(e, ch=ch, base=base):
                out = []
                for gu in range(2):
                    for li, it in enumerate(ch):
                        r, itl = it // GTPR, it % GTPR
                        slot0 = gu * 24 + itl * 8
                        dst = s_wq[:, base + (gu * 4 + li) * 1024:
                                   base + (gu * 4 + li) * 1024 + 1024]
                        out.append(e.dma_start(
                            out=dst.rearrange("p (s j) -> p s j", s=8),
                            in_=d_wq_out.ap()[r, slot0:slot0 + 8].transpose([1, 0, 2])))
                return out
            return _wqld
        CHW = [[0, 1, 2, 3], [4, 5, 6, 7], [8, 9, 10, 11], [12, 13, 14, 15],
               [16, 17, 18, 19], [20, 21]]
        ev_wqld = {}
        for qi in range(2):
            ev_wqld[qi] = OP("sync", _wqld_mk(CHW[qi], (qi % 2) * 8192),
                             waits=[ev_q_ag], track=True, dma=True,
                             ndma=2 * len(CHW[qi]),
                             ch="wqa" if qi % 2 == 0 else "wqb")
        ev_m1c = OP("gpsimd", lambda g: g.dma_start(out=AP(d_m1_in, 0, [[1, 128]]),
                                                    in_=s_m1a[:, 0:1]),
                    waits=[ev_m1u, ev_ms], track=True, dma=True, ch="m1g")
        ev_m1ar = OP("gpsimd", lambda g: g.collective_compute(
            "AllReduce", Alu.max, replica_groups=RG,
            ins=[d_m1_in.ap().opt()], outs=[d_m1_out.ap().opt()]),
            waits=[ev_m1c], track=True, ch="cc")
        ev_m1l = OP("sync", lambda e: e.dma_start(out=s_y[0:1, 0:128], in_=d_m1_out.ap()),
                    waits=[ev_m1ar], track=True, dma=True, ch="m1r")
        ev_m1p = OP("vector", lambda v: v.tensor_reduce(
            s_mg[0:1, 0:1], s_y[0:1, 0:128], AX, Alu.max),
            waits=[ev_m1l], track=True)
        ev_sig = OP("scalar", lambda s: s.activation(out=s_mg[0:1, 8:10],
                                                     in_=s_clip[0:1, :], func=Act.Sigmoid),
                    waits=[ev_ld["clip"]], track=True)
        ev_s1 = vchain([
            lambda v: v.tensor_tensor(out=s_mg[0:1, 10:11], in0=s_mg[0:1, 8:9],
                                      in1=s_mg[0:1, 0:1], op=Alu.mult),
            lambda v: v.tensor_scalar(out=s_mg[0:1, 10:11], in0=s_mg[0:1, 10:11],
                                      scalar1=1.0 / 127.0, scalar2=None, op0=Alu.mult),
            lambda v: v.reciprocal(out=s_mg[0:1, 11:12], in_=s_mg[0:1, 10:11]),
        ], waits=[ev_m1p, ev_sig])
        ev_bc1 = OP("tensor", lambda t: t.matmul(p_m[:, 0:2], lhsT=s_onef[:, :],
                                                 rhs=s_mg[0:1, 10:12], start=True, stop=True),
                    waits=[ev_s1, ev_bcq[0], ev_bcq[1]], track=True)
        ev_col1 = OP("vector", lambda v: v.tensor_copy(out=s_col[:, 0:2], in_=p_m[:, 0:2]),
                     waits=[ev_bc1], track=True)
        ev_zqa = {}
        ev_zqb = {}
        for k in range(32):
            ib = (k % 2) * 512
            sl = s_zqt[:, k * 512:(k + 1) * 512]
            ev_zqa[k] = OP("scalar", lambda s, sl=sl, ib=ib: s.activation(
                out=s_i32[:, ib:ib + 512], in_=sl, func=Act.Identity,
                scale=s_col[:, 1:2]),
                waits=[ev_col1 if k == 0 else None,
                       ev_zev[NT - 1] if k == 0 else None,
                       ev_zqb.get(k - 2)], track=True)
            ev_zqb[k] = OP("vector", lambda v, sl=sl, ib=ib: v.tensor_scalar(
                out=sl, in0=s_i32[:, ib:ib + 512], scalar1=-128.0,
                scalar2=127.0, op0=Alu.max, op1=Alu.min),
                waits=[ev_zqa[k]], track=True)
        ev_zq = ev_zqb[31]
        def _sgf(v):
            ins = None
            for it in range(NIT):
                r, c0 = it // GTPR, it % GTPR
                v.tensor_scalar(out=s_sgf[:, it:it + 1],
                                in0=s_scall[:, r * 8 + c0:r * 8 + c0 + 1],
                                scalar1=s_col[:, 0:1], scalar2=None, op0=Alu.mult)
                ins = v.tensor_scalar(out=s_suf[:, it:it + 1],
                                      in0=s_scall[:, r * 8 + 3 + c0:r * 8 + 4 + c0],
                                      scalar1=s_col[:, 0:1], scalar2=None, op0=Alu.mult)
            return ins
        ev_sgf = OP("vector", _sgf, waits=[ev_scl, ev_col1], track=True)
        ev_qsfa = OP("vector", lambda v: v.memset(s_qscf[:, 0:128], 1.0),
                     waits=[ev_sgf], track=True)
        def _qsfb(v):
            ins = None
            for it in range(NIT - 1, -1, -1):
                ins = v.tensor_scalar(out=s_qscf[:, it * 128:(it + 1) * 128],
                                      in0=s_qscf[:, 0:128],
                                      scalar1=s_suf[:, it:it + 1], scalar2=None,
                                      op0=Alu.mult)
            return ins
        ev_qsf = OP("vector", _qsfb, waits=[ev_qsfa], track=True)

        ev_sdp = OP("vector", lambda v: [v.tensor_copy(
            out=s_sdp[:, r:r + 1], in_=s_scall[:, r * 8 + 6:r * 8 + 7])
            for r in range(8)][-1], waits=[ev_scl], track=True)
        ev_sds = OP("gpsimd", lambda g: g.dma_start(
            out=AP(d_rowb, 256, [[1, 128], [128, 8]]), in_=s_sdp[:, :],
            allow_slow_non_contiguous=True),
            waits=[ev_sdp], track=True, dma=True, ch="grt")
        ev_sdr = OP("sync", lambda e: e.dma_start(
            out=s_sdr[0:1, :], in_=AP(d_rowb, 256, [[1024, 1], [1, 1024]])),
            waits=[ev_sds, ev_lnz[NT - 1]], track=True, dma=True, ch="sdr")
        ev_sdb = {}
        for half in range(2):
            evb = OP("tensor", lambda t, h=half: t.matmul(
                p_m2[:, 0:512], lhsT=s_onef[:, :], rhs=s_sdr[0:1, h * 512:h * 512 + 512],
                start=True, stop=True),
                waits=[ev_sdr, ev_sdb.get(0), ev_bcq[1]], track=True)
            ev_sdb[half] = OP("vector", lambda v, h=half: v.tensor_copy(
                out=s_sdx[:, h * 512:h * 512 + 512], in_=p_m2[:, 0:512]),
                waits=[evb], track=True)
        def _mk_qd(lo, hi):
            def _f(e):
                return [e.dma_start(
                    out=s_wq[:, :].rearrange("p (a k j) -> p a k j",
                                             a=NIT, k=8)[:, lo:hi, et, :],
                    in_=d_wq_out.ap()[et, 48 + lo:48 + hi].transpose([1, 0, 2]))
                    for et in range(8)]
            return _f
        ev_h = {}
        ev_hsp = {}
        ev_si = {}
        hidx = 0
        h_of_chunk_end = {}
        for qi, ch in enumerate(CHW):
            base = (qi % 2) * 8192
            if qi >= 2:
                ev_wqld[qi] = OP("sync", _wqld_mk(ch, base),
                                 waits=[ev_q_ag, h_of_chunk_end.get(qi - 2)],
                                 track=True, dma=True, ndma=2 * len(ch),
                                 ch="wqa" if qi % 2 == 0 else "wqb")
            if qi == 1:
                ev_qd1 = OP("scalar", _mk_qd(16, NIT), waits=[ev_q_ag],
                            track=True, dma=True, ndma=8, ch="qd")
            if qi == 5:
                ev_qd2 = OP("sync", _mk_qd(0, 8), waits=[h_of_chunk_end[4]],
                            track=True, dma=True, ndma=8, ch="qd")
            for c in range(NCH):
                for li, it in enumerate(ch):
                    slot = hidx % 3
                    gpd = [p_x1[0][:, 0:512], p_x1[1][:, 0:512],
                           p_m[:, 0:512]][slot]
                    upd = [p_x1[0][:, 512:1024], p_x1[1][:, 512:1024],
                           p_m2[:, 0:512]][slot]
                    def _gu(te, base=base, c=c, li=li, gpd=gpd, upd=upd):
                        ins = None
                        for gu in range(2):
                            pd = gpd if gu == 0 else upd
                            for ht in range(8):
                                ins = te.matmul(
                                    pd, lhsT=s_wq[:, base + (gu * 4 + li) * 1024 + ht * 128:
                                                  base + (gu * 4 + li) * 1024 + ht * 128 + 128],
                                    rhs=s_zqt[:, ht * TPC + c * 512:ht * TPC + c * 512 + 512],
                                    start=(ht == 0), stop=(ht == 7))
                        return ins
                    evgu = OP("tensor", _gu,
                              waits=[ev_wqld[qi], ev_zq, ev_h.get(hidx - 3),
                                     ev_sdb[1] if hidx == 2 else None], track=True)
                    ev_si[hidx] = OP("scalar", lambda sc, it=it, gpd=gpd, hi=hidx:
                                     sc.activation(
                        out=s_hf[:, (hi % 2) * 512:(hi % 2) * 512 + 512],
                        in_=gpd, func=Act.Silu, scale=s_sgf[:, it:it + 1]),
                        waits=[evgu, ev_sgf, ev_h.get(hidx - 2)], track=True)
                    ev_h[hidx] = OP("vector", lambda v, upd=upd, hi=hidx: v.tensor_tensor(
                        out=s_hrw[:, (hi % 2) * 512:(hi % 2) * 512 + 512],
                        in0=s_hf[:, (hi % 2) * 512:(hi % 2) * 512 + 512],
                        in1=upd, op=Alu.mult),
                        waits=[ev_si[hidx], ev_hsp.get(hidx - 2)] +
                              (ev_push_all if hidx < 2 else []), track=True)
                    evhr2 = OP("vector", lambda v, hi=hidx: v.tensor_reduce(
                        s_lnc[:, 6:7], s_hrw[:, (hi % 2) * 512:(hi % 2) * 512 + 512],
                        AX, Alu.max, apply_absolute_value=True),
                        waits=[ev_h[hidx]], track=True)
                    ev_m2u = OP("vector", lambda v, it=it: v.scalar_tensor_tensor(
                        out=s_m2a[:, :], in0=s_lnc[:, 6:7], scalar=s_suf[:, it:it + 1],
                        op0=Alu.mult, in1=s_m2a[:, :], op1=Alu.max),
                        waits=[evhr2], track=True)
                    ev_hsp[hidx] = OP("gpsimd", lambda g, it=it, c=c, hi=hidx: g.dma_start(
                        out=d_spill.ap()[it, :, c * 512:(c + 1) * 512],
                        in_=s_hrw[:, (hi % 2) * 512:(hi % 2) * 512 + 512]),
                        waits=[ev_h[hidx]], track=True, dma=True,
                        ch="spa" if hidx % 2 == 0 else "spb")
                    hidx += 1
            h_of_chunk_end[qi] = ev_h[hidx - 1]
        ev_hsp_all = [ev_hsp[hidx - 1], ev_hsp[hidx - 2]]
        for wi in range(110):
            OP("tensor", lambda t: t.matmul(
                p_s[:, :], lhsT=s_id[:, 0:24], rhs=s_hkw[:, 0:512],
                start=True, stop=True))

        ev_m2c = OP("gpsimd", lambda g: g.dma_start(out=AP(d_m2_in, 0, [[1, 128]]),
                                                    in_=s_m2a[:, 0:1]),
                    waits=[ev_m2u], track=True, dma=True, ch="m2g")
        ev_m2ar = OP("gpsimd", lambda g: g.collective_compute(
            "AllReduce", Alu.max, replica_groups=RG,
            ins=[d_m2_in.ap().opt()], outs=[d_m2_out.ap().opt()]),
            waits=[ev_m2c], track=True, ch="cc")
        ev_m2l = OP("sync", lambda e: e.dma_start(out=s_y[0:1, 0:128], in_=d_m2_out.ap()),
                    waits=[ev_m2ar], track=True, dma=True, ch="m2r")
        ev_m2p = OP("vector", lambda v: v.tensor_reduce(
            s_mg[0:1, 12:13], s_y[0:1, 0:128], AX, Alu.max),
            waits=[ev_m2l], track=True)
        ev_s2 = vchain([
            lambda v: v.tensor_tensor(out=s_mg[0:1, 13:14], in0=s_mg[0:1, 9:10],
                                      in1=s_mg[0:1, 12:13], op=Alu.mult),
            lambda v: v.tensor_scalar(out=s_mg[0:1, 13:14], in0=s_mg[0:1, 13:14],
                                      scalar1=1.0 / 127.0, scalar2=None, op0=Alu.mult),
            lambda v: v.reciprocal(out=s_mg[0:1, 14:15], in_=s_mg[0:1, 13:14]),
        ], waits=[ev_m2p, ev_sig])
        ev_bc2 = OP("tensor", lambda t: t.matmul(p_m[:, 2:4], lhsT=s_onef[:, :],
                                                 rhs=s_mg[0:1, 13:15], start=True, stop=True),
                    waits=[ev_s2, ev_col1], track=True)
        ev_col2 = OP("vector", lambda v: v.tensor_copy(out=s_col[:, 2:4], in_=p_m[:, 2:4]),
                     waits=[ev_bc2], track=True)
        def _qscm(v):
            ins = None
            for seg in range(3):
                w = 1024 if seg < 2 else 768
                ins = v.tensor_scalar(out=s_qscf[:, seg * 1024:seg * 1024 + w],
                                      in0=s_qscf[:, seg * 1024:seg * 1024 + w],
                                      scalar1=s_col[:, 3:4], scalar2=None, op0=Alu.mult)
            return ins
        ev_qsc = OP("vector", _qscm, waits=[ev_col2, ev_qsf], track=True)
        ev_sdx2 = OP("vector", lambda v: v.tensor_scalar(
            out=s_sdx[:, :], in0=s_sdx[:, :], scalar1=s_col[:, 2:3], scalar2=None,
            op0=Alu.mult), waits=[ev_sdb[1], ev_col2], track=True)
        ev_qdld = OP("scalar", _mk_qd(8, 16), waits=[ev_h[hidx - 1]],
                     track=True, dma=True, ndma=8, ch="qd")

        ev_hl = {}
        ev_qh = {}
        ev_dn = {}
        ev_y2 = {}
        ev_sq2 = {}
        ev_ov = {}
        ev_od = {}

        def emit_front(t):
            ev_hl[t] = OP("sync", lambda e, t=t: e.dma_start(
                out=s_zqt[:, (t % 2) * IF:(t % 2) * IF + IF],
                in_=d_spill.ap()[:, :, t * 128:(t + 1) * 128].transpose([1, 0, 2])),
                waits=[ev_hsp[80 + 2 * (t // 4)], ev_hsp[81 + 2 * (t // 4)],
                       ev_qh.get(t - 2), ev_dbg_w2dma], track=True, dma=True,
                ch="hla" if t % 2 == 0 else "hlb")
            def _qh(v, t=t):
                hb = (t % 2) * IF
                qb = (t % 2) * IF
                ins = None
                for seg in range(3):
                    w = 1024 if seg < 2 else 768
                    o = seg * 1024
                    v.tensor_tensor(out=s_i32[:, 0:w], in0=s_zqt[:, hb + o:hb + o + w],
                                    in1=s_qscf[:, o:o + w], op=Alu.mult)
                    ins = v.tensor_scalar(out=s_qht[:, qb + o:qb + o + w], in0=s_i32[:, 0:w],
                                          scalar1=-128.0, scalar2=127.0,
                                          op0=Alu.max, op1=Alu.min)
                return ins
            ev_qh[t] = OP("vector", _qh, waits=[ev_hl[t], ev_qsc, ev_dn.get(t - 2)],
                          track=True)
            slot = t % 3
            pds = [[p_x1[0][:, 0:512], p_x1[0][:, 512:1024]],
                   [p_x1[1][:, 0:512], p_x1[1][:, 512:1024]],
                   [p_m[:, 0:512], p_m2[:, 0:512]]][slot]
            def _dn(te, t=t, pds=pds):
                qb = (t % 2) * IF
                ins = None
                for ec in range(2):
                    for it in range(NIT):
                        ins = te.matmul(pds[ec],
                                        lhsT=s_qht[:, qb + it * 128:qb + it * 128 + 128],
                                        rhs=s_wq[:, it * 1024 + ec * 512:
                                                 it * 1024 + ec * 512 + 512],
                                        start=(it == 0), stop=(it == NIT - 1))
                return ins
            ev_dn[t] = OP("tensor", _dn, waits=[ev_qh[t], ev_qdld, ev_y2.get(t - 3)],
                          track=True)

        def emit_back(t):
            slot = t % 3
            pds = [[p_x1[0][:, 0:512], p_x1[0][:, 512:1024]],
                   [p_x1[1][:, 0:512], p_x1[1][:, 512:1024]],
                   [p_m[:, 0:512], p_m2[:, 0:512]]][slot]
            def _y2(v, t=t, pds=pds):
                ins = None
                for ec in range(2):
                    sl = slice(ec * 512, (ec + 1) * 512)
                    v.tensor_tensor(out=s_ysq[:, sl], in0=pds[ec],
                                    in1=s_sdx[:, sl], op=Alu.mult)
                    ins = v.scalar_tensor_tensor(
                        out=s_y[:, sl], in0=s_ysq[:, sl], scalar=1.0, op0=Alu.mult,
                        in1=s_z[:, t * E + ec * 512:t * E + ec * 512 + 512], op1=Alu.add,
                        accum_out=s_lnc[:, 0:1] if ec == 0 else s_lnc[:, 1:2])
                return ins
            ev_y2[t] = OP("vector", _y2, waits=[ev_dn[t], ev_sdx2], track=True)
            ev_sq2[t] = OP("scalar", lambda s: s.activation(
                out=s_sqs[:, :], in_=s_y[:, :], func=Act.Square, accum_out=s_lnc[:, 2:3]),
                waits=[ev_y2[t]], track=True)
            evv2 = vchain([
                lambda v: v.tensor_tensor(out=s_lnc[:, 0:1], in0=s_lnc[:, 0:1],
                                          in1=s_lnc[:, 1:2], op=Alu.add),
                lambda v: v.tensor_scalar(out=s_lnc[:, 3:4], in0=s_lnc[:, 0:1],
                                          scalar1=1.0 / E, scalar2=None, op0=Alu.mult),
                lambda v: v.tensor_tensor(out=s_lnc[:, 4:5], in0=s_lnc[:, 3:4],
                                          in1=s_lnc[:, 3:4], op=Alu.mult),
                lambda v: v.scalar_tensor_tensor(out=s_lnc[:, 4:5], in0=s_lnc[:, 2:3],
                                                 scalar=1.0 / E, op0=Alu.mult,
                                                 in1=s_lnc[:, 4:5], op1=Alu.subtract),
                lambda v: v.tensor_scalar(out=s_lnc[:, 4:5], in0=s_lnc[:, 4:5],
                                          scalar1=1e-5, scalar2=None, op0=Alu.add),
            ], waits=[ev_sq2[t]])
            evrt2 = OP("scalar", lambda s: s.activation(
                out=s_lnc[:, 5:6], in_=s_lnc[:, 4:5], func=Act.Sqrt),
                waits=[evv2], track=True)
            evoa = OP("vector", lambda v: v.reciprocal(
                out=s_lnc[:, 6:7], in_=s_lnc[:, 5:6]), waits=[evrt2], track=True)
            obuf = s_outb if t % 2 == 0 else s_wn
            ev_ov[t] = OP("vector", lambda v, t=t, obuf=obuf: v.tensor_scalar(
                out=obuf[:, :], in0=s_y[:, :], scalar1=s_lnc[:, 3:4],
                scalar2=s_lnc[:, 6:7], op0=Alu.subtract, op1=Alu.mult),
                waits=[evoa, ev_od.get(t - 2)], track=True)
            ev_od[t] = OP("gpsimd", lambda g, t=t, obuf=obuf: g.dma_start(
                out=d_out.ap()[t * 128:(t + 1) * 128, :],
                in_=obuf[:, :]),
                waits=[ev_ov[t]], track=True, dma=True,
                ch="outa" if t % 2 == 0 else "outb")

        for t in range(NT + 1):
            if t < NT:
                emit_front(t)
            if t >= 1:
                emit_back(t - 1)

        if dbg:
            OP("sync", lambda e: e.dma_start(out=d_dbg_m.ap(), in_=s_mg[0:1, :]),
               waits=[ev_s2], track=True, dma=True, ch="dbg")
        OP("gpsimd", lambda g: g.engine_nop(),
           waits=[ev_od[NT - 1], ev_od[NT - 2]])
        sems = {c: stk.enter_context(nc.semaphore("sem_" + c)) for c in sorted(S.channels)}
        blk = stk.enter_context(nc.Block())
        S.emit(blk, sems)
    return nc


def prep_inputs(inp):
    import ml_dtypes
    x = np.ascontiguousarray(np.asarray(inp["L0_attn_x"], dtype=np.float32).reshape(NC * TPC, E))
    wg = np.asarray(inp["L0_ffn_w_gate"], dtype=np.float32)
    wu = np.asarray(inp["L0_ffn_w_up"], dtype=np.float32)
    wd = np.asarray(inp["L0_ffn_w_down"], dtype=np.float32)
    wel = np.ascontiguousarray(np.asarray(inp["L0_attn_W_el"], dtype=np.float32))
    bel = np.ascontiguousarray(np.asarray(inp["L0_attn_b_el"], dtype=np.float32).reshape(1, 24))
    wle = np.ascontiguousarray(np.asarray(inp["L0_attn_W_le"], dtype=np.float32))
    wleb = np.ascontiguousarray(np.asarray(inp["L0_attn_b_le"], dtype=np.float32).reshape(1, E))
    mask24 = np.zeros((1, 24), dtype=np.float32); mask24[0, :12] = 1.0
    clipv = np.array([[float(np.asarray(inp["L0_ffn_clip_x"]).reshape(-1)[0]),
                       float(np.asarray(inp["L0_ffn_clip_down"]).reshape(-1)[0])]],
                     dtype=np.float32)
    k = np.fft.fftfreq(E).astype(np.float32)
    ka = (np.abs(k) + np.float32(1e-8)).reshape(1, E).astype(np.float32)
    ka2 = (ka * ka).astype(np.float32)
    dd = np.arange(E)
    cosm = np.ascontiguousarray(np.cos(2 * np.pi * np.outer(dd, dd) / E).astype(np.float32))
    lxr = np.zeros((1, 8), dtype=np.float32)
    lxr[0, :5] = LX.astype(np.float32)
    _A = np.array([[1,1,1,0,1,1,1,0,0,0,1,1],[1,1,0,1,1,1,0,0,0,1,1,0],[1,0,1,1,1,0,0,0,1,1,0,1],[0,1,1,1,0,0,0,1,1,0,1,1],[0,0,0,0,1,1,1,1,1,1,1,1],[1,1,1,1,0,0,0,0,0,0,0,0],[0,0,0,0,1,1,1,1,0,0,0,0],[1,1,0,0,1,1,0,0,1,1,0,0],[1,0,1,0,1,0,1,0,1,0,1,0],[0,1,0,1,0,1,0,1,0,1,0,1],[1,0,0,1,1,0,0,1,1,0,0,1],[0,1,1,0,0,1,1,0,0,1,1,0]], dtype=np.float32)
    Hm = np.concatenate([_A.T, np.eye(12, dtype=np.float32)], axis=1)
    hmt = np.ascontiguousarray(Hm.T).astype(ml_dtypes.bfloat16)
    ident = np.eye(128, dtype=ml_dtypes.bfloat16)
    xfirst = np.ascontiguousarray(x[0:1, :])
    wgT = np.ascontiguousarray(wg.T)
    wuT = np.ascontiguousarray(wu.T)
    wdT = np.ascontiguousarray(wd.T)
    in_maps = []
    for c in range(NC):
        sh = np.ascontiguousarray(x[c * TPC:(c + 1) * TPC, :])
        xt = np.ascontiguousarray(sh.T)
        xtf = np.ascontiguousarray(xt.reshape(8, 128, TPC)[:, ::-1, :].reshape(E, TPC))
        git = [min(GTPR * c + i, NIT - 1) for i in range(GTPR)]
        wgn = np.ascontiguousarray(np.concatenate([wg[i * 128:(i + 1) * 128, :] for i in git], axis=0))
        wun = np.ascontiguousarray(np.concatenate([wu[i * 128:(i + 1) * 128, :] for i in git], axis=0))
        wgt = np.ascontiguousarray(np.concatenate([wgT[:, i * 128:(i + 1) * 128] for i in git], axis=1))
        wut = np.ascontiguousarray(np.concatenate([wuT[:, i * 128:(i + 1) * 128] for i in git], axis=1))
        wdn = np.ascontiguousarray(wd[c * 128:(c + 1) * 128, :])
        wdt = np.ascontiguousarray(wdT[:, c * 128:(c + 1) * 128])
        in_maps.append({
            "xnat": sh, "xtf": xtf, "xfirst": xfirst, "ka": ka, "ka2": ka2,
            "lxr": lxr, "clipv": clipv, "cosm": cosm, "wel": wel, "bel": bel,
            "wle": wle, "wleb": wleb, "mask24": mask24, "hmt": hmt, "ident": ident,
            "wgn": wgn, "wun": wun, "wdn": wdn, "wgt": wgt, "wut": wut, "wdt": wdt,
        })
    return in_maps


def kernel(**inputs):
    from concourse.bass_utils import run_bass_kernel_spmd
    in_maps = prep_inputs(inputs)
    nc = build_nc(stage=3, dbg=False)
    res = run_bass_kernel_spmd(nc, in_maps, core_ids=list(range(NC)))
    out = np.concatenate([np.asarray(res.results[c]["out"]) for c in range(NC)], axis=0)
    return np.ascontiguousarray(out.reshape(2, 8192, E)).astype(np.float32, copy=False)


# version 16
# speedup vs baseline: 1.0206x; 1.0206x over previous
import numpy as np
import concourse.bass as bass
import concourse.bass_isa as bass_isa
import concourse.mybir as mybir
from concourse.ap import AP
from contextlib import ExitStack

F32 = mybir.dt.float32
BF16 = mybir.dt.bfloat16
I32 = mybir.dt.int32
Alu = mybir.AluOpType
Act = mybir.ActivationFunctionType
AX = mybir.AxisListType.X

NC = 8
E = 1024
IF = 2816
TPC = 2048
NT = TPC // 128
NCH = TPC // 512
NIT = IF // 128
SUP = [list(range(0, 11)), list(range(11, 22))]
GTPR = 3
PI = float(np.pi)
LX = np.log([2., 4., 8., 16., 32.])
LXS = float(LX.sum())
DEN = float(5.0 * (LX ** 2).sum() - LXS * LXS)
ENGS = ("sync", "gpsimd", "vector", "scalar", "tensor")


class Sched:
    def __init__(self):
        self.stream = {e: [] for e in ENGS}
        self.count = {}
        self.channels = set()

    def op(self, eng, fn, waits=(), track=False, dma=False, ndma=1, ch=None):
        ch = ch or eng
        self.channels.add(ch)
        amt = (16 * ndma) if dma else 1
        ev = None
        if track:
            self.count[ch] = self.count.get(ch, 0) + amt
            ev = (ch, self.count[ch])
        self.stream[eng].append(
            (tuple(w for w in waits if w is not None), fn, track, dma, ndma, ch))
        return ev

    def emit(self, block, sems):
        for ename in ENGS:
            stream = self.stream[ename]
            def body(engine, stream=stream):
                last = {}
                for waits, fn, track, dma, ndma, ch in stream:
                    for (src, val) in waits:
                        if last.get(src, -1) >= val:
                            continue
                        engine.wait_ge(sems[src], val)
                        last[src] = val
                    ins = fn(engine)
                    if dma:
                        lst = ins if isinstance(ins, (list, tuple)) else [ins]
                        assert len(lst) == ndma, (len(lst), ndma)
                        for i in lst:
                            i.then_inc(sems[ch], 16)
                    elif track:
                        ins.then_inc(sems[ch], 1)
            getattr(block, ename)(body)


def build_nc(stage=3, dbg=False):
    nc = bass.Bass("TRN2", target_bir_lowering=False)
    dt = nc.dram_tensor
    d_xnat = dt("xnat", [TPC, E], F32, kind="ExternalInput")
    d_xtf = dt("xtf", [E, TPC], F32, kind="ExternalInput")
    d_xfirst = dt("xfirst", [1, E], F32, kind="ExternalInput")
    d_ka = dt("ka", [1, E], F32, kind="ExternalInput")
    d_ka2 = dt("ka2", [1, E], F32, kind="ExternalInput")
    d_lx = dt("lxr", [1, 8], F32, kind="ExternalInput")
    d_clip = dt("clipv", [1, 2], F32, kind="ExternalInput")
    d_cos = dt("cosm", [E, E], F32, kind="ExternalInput")
    d_wel = dt("wel", [E, 24], F32, kind="ExternalInput")
    d_bel = dt("bel", [1, 24], F32, kind="ExternalInput")
    d_wle = dt("wle", [24, E], F32, kind="ExternalInput")
    d_wleb = dt("wleb", [1, E], F32, kind="ExternalInput")
    d_mask24 = dt("mask24", [1, 24], F32, kind="ExternalInput")
    d_hmt = dt("hmt", [24, 12], BF16, kind="ExternalInput")
    d_id = dt("ident", [128, 128], BF16, kind="ExternalInput")
    d_wgn = dt("wgn", [GTPR * 128, E], F32, kind="ExternalInput")
    d_wun = dt("wun", [GTPR * 128, E], F32, kind="ExternalInput")
    d_wdn = dt("wdn", [128, IF], F32, kind="ExternalInput")
    d_wgt = dt("wgt", [E, GTPR * 128], F32, kind="ExternalInput")
    d_wut = dt("wut", [E, GTPR * 128], F32, kind="ExternalInput")
    d_wdt = dt("wdt", [IF, 128], F32, kind="ExternalInput")
    d_out = dt("out", [TPC, E], F32, kind="ExternalOutput")
    if dbg:
        d_dbg_hr = dt("dbg_hr", [1, 2048], F32, kind="ExternalOutput")
        d_dbg_z = dt("dbg_z", [TPC, E], F32, kind="ExternalOutput")
        d_dbg_m = dt("dbg_m", [1, 16], F32, kind="ExternalOutput")
        d_dbg_w2 = dt("dbg_w2", [128, 192], BF16, kind="ExternalOutput")
        d_dbg_r = dt("dbg_r", [4, 1024], F32, kind="ExternalOutput")

    d_hrb = dt("hrb", [2048], BF16, kind="Internal")
    d_frb = dt("frb", [1024], F32, kind="Internal")
    d_rowb = dt("rowb", [2048], F32, kind="Internal")
    d_spill = dt("spill", [NIT, 128, TPC], BF16, kind="Internal")
    d_sc_in = dt("sc_in", [128, 8], F32, kind="Internal")
    d_sc_out = dt("sc_out", [NC, 128, 8], F32, kind="Internal", addr_space="Shared")
    d_wq_in = dt("wq_in", [70, 128, 128], BF16, kind="Internal")
    d_wq_out = dt("wq_out", [NC, 70, 128, 128], BF16, kind="Internal", addr_space="Shared")
    d_m1_in = dt("m1_in", [1, 128], F32, kind="Internal")
    d_m1_out = dt("m1_out", [1, 128], F32, kind="Internal")
    d_m2_in = dt("m2_in", [1, 128], F32, kind="Internal")
    d_m2_out = dt("m2_out", [1, 128], F32, kind="Internal")
    RG = [list(range(NC))]

    with ExitStack() as stk:
        def sb(name, shape, dtt):
            return stk.enter_context(nc.sbuf_tensor(name, shape, dtt))
        def psum(name, shape, dtt):
            return stk.enter_context(nc.psum_tensor(name, shape, dtt))

        s_z = sb("s_z", [128, NT * E], BF16)
        s_zqt = sb("s_zqt", [128, 8 * TPC], BF16)
        s_wq = sb("s_wq", [128, 22528], BF16)
        s_xn = sb("s_xn", [128, 2 * E], F32)
        s_xtf = sb("s_xtf", [128, 2 * 8 * 512], BF16)
        s_hkw = sb("s_hkw", [128, 8 * 512], BF16)
        s_y = sb("s_y", [128, E], F32)
        s_ysq = sb("s_ysq", [128, E], F32)
        s_sqs = sb("s_sqs", [128, E], F32)
        s_i32 = sb("s_i32", [128, E], I32)
        s_cosb = sb("s_cosb", [128, 512], F32)
        s_hf = sb("s_hf", [128, 2 * 512], F32)
        s_hrw = sb("s_hrw", [128, 2 * 512], BF16)
        s_hbuf = sb("s_hbuf", [128, IF], BF16)
        s_outb = sb("s_outb", [128, E], F32)
        s_wel2 = sb("s_wel2", [128, 8 * 24], BF16)
        s_wel = sb("s_wel", [128, 8 * 24], BF16)
        s_wle = sb("s_wle", [24, E], BF16)
        s_mask24 = sb("s_mask24", [1, 24], F32)
        s_wleb = sb("s_wleb", [1, E], BF16)
        s_bel = sb("s_bel", [1, 24], BF16)
        s_hmt = sb("s_hmt", [24, 12], BF16)
        s_id = sb("s_id", [128, 128], BF16)
        s_one5 = sb("s_one5", [1, 512], BF16)
        s_onef = sb("s_onef", [1, 128], F32)
        s_onek = sb("s_onek", [12, 1], BF16)
        s_corr = sb("s_corr", [24, 2 * 512], BF16)
        s_bits = sb("s_bits", [24, 2 * 512], BF16)
        s_syn = sb("s_syn", [24, 512], BF16)
        s_frc = sb("s_frc", [128, 8], F32)
        s_clip = sb("s_clip", [1, 2], F32)
        s_wn = sb("s_wn", [128, E], F32)
        s_wtr = sb("s_wtr", [128, GTPR * 128], F32)
        s_rmx = sb("s_rmx", [128, 8], F32)
        s_inv = sb("s_inv", [128, 8], F32)
        s_sc7 = sb("s_sc7", [128, 8], F32)
        s_scall = sb("s_scall", [128, NC * 8], F32)
        s_sgf = sb("s_sgf", [128, 24], F32)
        s_suf = sb("s_suf", [128, 24], F32)
        s_qsc = sb("s_qsc", [128, 24], F32)
        s_qscf = sb("s_qscf", [128, IF], F32)
        s_sdp = sb("s_sdp", [128, 8], F32)
        s_sdx = sb("s_sdx", [128, E], F32)
        s_m1a = sb("s_m1a", [128, 1], F32)
        s_m2a = sb("s_m2a", [128, 1], F32)
        s_m8 = sb("s_m8", [1, 8], F32)
        s_mg = sb("s_mg", [1, 16], F32)
        s_col = sb("s_col", [128, 8], F32)
        s_lnc = sb("s_lnc", [128, 16], F32)

        s_xf1 = s_sdx[0:1, :]
        s_ka = s_qscf[0:1, 896:1920]
        s_ka2 = s_sdx[0:1, :]
        s_pp = s_hf[0:24, 0:512]
        s_t12 = s_hf[0:24, 512:1024]
        pass
        s_flt = s_ysq[0:1, :]
        s_fr = s_y[0:1, :]
        s_hr = s_outb[0:1, 0:E]
        s_invf = s_qscf[:, 0:896]
        s_invr = s_qscf[0:1, 0:896]
        s_sdr = s_wn[0:1, 0:1024]
        s_fix = s_ysq[0:1, 0:512]
        s_fixa = s_ysq[0:1, 512:1024]
        s_scb = s_lnc[0:1, 0:8]
        s_sc = s_qsc[0:1, 0:16]
        s_cnt = s_sgf[0:1, 0:8]
        s_lxr = s_suf[0:1, 0:8]
        s_qht = s_xtf[:, 0:2 * IF]
        s_qst = s_hrw[:, 0:GTPR * 128]

        p_x1 = [psum("p_x1a", [128, E], F32), psum("p_x1b", [128, E], F32)]
        p_s = psum("p_s", [24, 512], F32)
        p_t = psum("p_t", [128, E], BF16)
        p_m = psum("p_m", [128, 512], F32)
        p_m2 = psum("p_m2", [128, 512], F32)
        s_flt2 = p_x1[0][0:1, :]

        blk_holder = []
        S = Sched()
        OP = S.op

        def vchain(steps, waits=(), eng="vector"):
            ev = None
            first = True
            for fn in steps:
                w = list(waits) + ([ev] if ev else []) if first else [ev]
                ev = OP(eng, fn, waits=w, track=True)
                first = False
            return ev

        ev_ld = {}
        def load(name, dst, srcap, eng="sync", waits=()):
            ev_ld[name] = OP(eng, lambda e, d=dst, s=srcap: e.dma_start(out=d, in_=s),
                             waits=waits, track=True, dma=True,
                             ch="sld" if eng == "sync" else "gld")
        load("xf1", s_xf1[:, :], d_xfirst.ap())
        load("ka", s_ka[:, :], d_ka.ap())
        load("lxr", s_lxr[:, :], d_lx.ap())
        load("clip", s_clip[:, :], d_clip.ap())
        load("id", s_id[:, :], d_id.ap())
        load("hmt", s_hmt[:, :], d_hmt.ap())
        load("wel", s_wel[:, :], AP(d_wel, 0, [[24, 128], [3072, 8], [1, 24]]), "gpsimd")
        load("bel", s_bel[:, :], d_bel.ap(), "gpsimd")
        load("wle", s_wle[:, :], d_wle.ap(), "gpsimd")
        load("mask24", s_mask24[:, :], d_mask24.ap())
        load("wleb", s_wleb[:, :], d_wleb.ap(), "gpsimd")
        ev_sld7 = ("sld", S.count["sld"])
        ev_gld4 = ("gld", S.count["gld"])
        for _n in ("xf1", "ka", "lxr", "clip", "id", "hmt", "mask24"):
            ev_ld[_n] = ev_sld7
        for _n in ("wel", "bel", "wle", "wleb"):
            ev_ld[_n] = ev_gld4
        ev_ms = OP("vector", lambda v: (
            v.memset(s_one5[:, :], 1.0), v.memset(s_onef[:, :], 1.0),
            v.memset(s_onek[:, :], 1.0), v.memset(s_syn[:, :], 0.0),
            v.memset(s_m1a[:, :], 0.0), v.memset(s_col[:, 4:5], 1e-5),
            v.memset(s_m2a[:, :], 0.0), v.memset(s_m8[:, :], 0.0))[-1], track=True)
        ev_xn = {}
        ev_xc = {}
        for t in range(2):
            ev_xn[t] = OP("sync", lambda e, t=t: e.dma_start(
                out=s_xn[:, (t % 2) * E:(t % 2) * E + E],
                in_=d_xnat.ap()[t * 128:(t + 1) * 128, :]), track=True, dma=True,
                ch="xna" if t % 2 == 0 else "xnb")
        def xtf_buf(c):
            if c == 2:
                return s_z, 12 * E
            if c == 3:
                return s_xtf, 0
            return s_xtf, (c % 2) * 4096
        def _mk_xc(c):
            buf, xb = xtf_buf(c)
            def _xc(g, buf=buf, xb=xb):
                return g.dma_start(
                    out=buf[:, xb:xb + 4096].rearrange("p (q c2) -> p q c2", q=8),
                    in_=d_xtf.ap()[:, c * 512:(c + 1) * 512].rearrange(
                        "(q p) c2 -> q p c2", q=8).transpose([1, 0, 2]))
            return _xc
        for c in range(3):
            ev_xc[c] = OP("gpsimd", _mk_xc(c), track=True, dma=True, ndma=1,
                          ch=("xca", "xcb", "xcc")[c])
        for wi in range(64):
            OP("tensor", lambda t: t.matmul(
                p_m[:, 0:512], lhsT=s_id[:, :], rhs=s_xtf[:, 0:512],
                start=True, stop=True),
               waits=[ev_ld["id"], ev_xc[0]] if wi == 0 else ())

        ev_q_ag = ev_sc_ag = ev_invf = None
        ev_qw_last = None
        ev_push_all = []
        ev_dbgfr = None
        ev_cntl = None
        for j, n_ in enumerate([4, 16, 64, 256, 1024]):
            evs = OP("vector", lambda v, n=n_: v.tensor_reduce(
                s_sc[0:1, 0:1], s_xf1[0:1, 0:n], AX, Alu.add),
                waits=[ev_ld["xf1"]], track=True)
            evm = OP("vector", lambda v, n=n_: v.tensor_scalar(
                out=s_sc[0:1, 1:2], in0=s_sc[0:1, 0:1], scalar1=1.0 / n,
                scalar2=None, op0=Alu.mult), waits=[evs], track=True)
            evg = OP("vector", lambda v, n=n_: v.tensor_scalar(
                out=s_flt[0:1, 0:n], in0=s_xf1[0:1, 0:n],
                scalar1=s_sc[0:1, 1:2], scalar2=None, op0=Alu.is_gt),
                waits=[evm], track=True)
            evc = OP("vector", lambda v, n=n_: v.tensor_reduce(
                s_sc[0:1, 2:3], s_flt[0:1, 0:n], AX, Alu.add),
                waits=[evg], track=True)
            ev_cntl = OP("vector", lambda v, jj=j: v.tensor_scalar(
                out=s_cnt[0:1, jj:jj + 1], in0=s_sc[0:1, 2:3],
                scalar1=1.0, scalar2=None, op0=Alu.max), waits=[evc], track=True)
        ev_lncnt = OP("scalar", lambda s: s.activation(out=s_cnt[0:1, 0:5],
                                                       in_=s_cnt[0:1, 0:5], func=Act.Ln),
                      waits=[ev_cntl], track=True)
        load("ka2", s_ka2[:, :], d_ka2.ap(), waits=[ev_cntl])
        ev_dsa = vchain([
            lambda v: v.tensor_reduce(s_sc[0:1, 3:4], s_cnt[0:1, 0:5], AX, Alu.add),
            lambda v: v.tensor_tensor(out=s_scb[0:1, 0:5], in0=s_cnt[0:1, 0:5],
                                      in1=s_lxr[0:1, 0:5], op=Alu.mult),
            lambda v: v.tensor_reduce(s_sc[0:1, 4:5], s_scb[0:1, 0:5], AX, Alu.add),
        ], waits=[ev_lncnt, ev_ld["lxr"]])
        ev_dsc = vchain([
            lambda v: v.tensor_scalar(out=s_sc[0:1, 5:6], in0=s_sc[0:1, 3:4], scalar1=LXS,
                                      scalar2=None, op0=Alu.mult),
            lambda v: v.tensor_scalar(out=s_sc[0:1, 6:7], in0=s_sc[0:1, 4:5], scalar1=5.0,
                                      scalar2=None, op0=Alu.mult),
            lambda v: v.tensor_tensor(out=s_sc[0:1, 7:8], in0=s_sc[0:1, 5:6],
                                      in1=s_sc[0:1, 6:7], op=Alu.subtract),
            lambda v: v.tensor_scalar(out=s_sc[0:1, 8:9], in0=s_sc[0:1, 7:8],
                                      scalar1=1.0 / DEN, scalar2=None, op0=Alu.mult),
            lambda v: v.tensor_scalar(out=s_sc[0:1, 9:10], in0=s_sc[0:1, 8:9], scalar1=1.0,
                                      scalar2=2.0, op0=Alu.max, op1=Alu.min),
            lambda v: v.tensor_scalar(out=s_sc[0:1, 10:11], in0=s_sc[0:1, 9:10],
                                      scalar1=0.2, scalar2=1.0, op0=Alu.mult, op1=Alu.add),
            lambda v: v.tensor_tensor(out=s_sc[0:1, 11:12], in0=s_sc[0:1, 10:11],
                                      in1=s_sc[0:1, 9:10], op=Alu.mult),
            lambda v: v.tensor_scalar(out=s_sc[0:1, 12:13], in0=s_sc[0:1, 9:10],
                                      scalar1=0.5, scalar2=1.0, op0=Alu.mult, op1=Alu.add),
            lambda v: v.tensor_scalar(out=s_sc[0:1, 13:14], in0=s_sc[0:1, 9:10],
                                      scalar1=0.1, scalar2=None, op0=Alu.mult),
        ], waits=[ev_dsa])
        def _arg1(v):
            v.tensor_scalar(out=s_flt[0:1, :], in0=s_ka[0:1, :], scalar1=s_sc[0:1, 12:13],
                            scalar2=s_sc[0:1, 11:12], op0=Alu.mult, op1=Alu.add)
            v.tensor_scalar(out=s_flt2[0:1, :], in0=s_flt[0:1, :], scalar1=PI,
                            scalar2=None, op0=Alu.is_gt)
            return v.scalar_tensor_tensor(out=s_flt[0:1, :], in0=s_flt2[0:1, :],
                                          scalar=-2.0 * PI, in1=s_flt[0:1, :],
                                          op0=Alu.mult, op1=Alu.add)
        ev_arg1 = OP("vector", _arg1, waits=[ev_dsc, ev_ld["ka"]], track=True)
        ev_sin1 = OP("scalar", lambda s: s.activation(out=s_fr[0:1, :], in_=s_flt[0:1, :],
                                                      func=Act.Sin),
                     waits=[ev_arg1], track=True)
        def _arg2(v):
            v.tensor_scalar(out=s_flt[0:1, :], in0=s_ka2[0:1, :], scalar1=s_sc[0:1, 13:14],
                            scalar2=s_sc[0:1, 11:12], op0=Alu.mult, op1=Alu.add)
            v.tensor_tensor(out=s_flt[0:1, :], in0=s_flt[0:1, :], in1=s_ka[0:1, :],
                            op=Alu.subtract)
            v.tensor_scalar(out=s_flt[0:1, :], in0=s_flt[0:1, :], scalar1=PI / 2,
                            scalar2=None, op0=Alu.add)
            v.tensor_scalar(out=s_flt2[0:1, :], in0=s_flt[0:1, :], scalar1=PI,
                            scalar2=None, op0=Alu.is_gt)
            return v.scalar_tensor_tensor(out=s_flt[0:1, :], in0=s_flt2[0:1, :],
                                          scalar=-2.0 * PI, in1=s_flt[0:1, :],
                                          op0=Alu.mult, op1=Alu.add)
        ev_arg2 = OP("vector", _arg2, waits=[ev_sin1, ev_ld["ka2"]], track=True)
        ev_sin2 = OP("scalar", lambda s: s.activation(out=s_flt2[0:1, :], in_=s_flt[0:1, :],
                                                      func=Act.Sin),
                     waits=[ev_arg2], track=True)
        ev_fr = OP("vector", lambda v: (
            v.tensor_tensor(out=s_fr[0:1, :], in0=s_fr[0:1, :], in1=s_flt2[0:1, :],
                            op=Alu.mult),
            v.tensor_scalar(out=s_fr[0:1, :], in0=s_fr[0:1, :], scalar1=1.0 / 1024.0,
                            scalar2=None, op0=Alu.mult))[-1],
            waits=[ev_sin2], track=True)
        ev_frs = OP("sync", lambda g: g.dma_start(out=AP(d_frb, 0, [[1024, 1], [1, 1024]]),
                                                  in_=s_fr[0:1, :]),
                    waits=[ev_fr], track=True, dma=True, ch="hrch")
        ev_cosl = {}
        ev_cosl[0] = OP("sync", lambda e: e.dma_start(
            out=s_cosb[:, :], in_=d_cos.ap()[0:128, 0:512]), track=True, dma=True,
            ch="cosa")
        ev_frc = OP("sync", lambda e: e.dma_start(out=s_frc[:, :],
                                                  in_=AP(d_frb, 0, [[1, 128], [128, 8]]),
                                                  allow_slow_non_contiguous=True),
                    waits=[ev_frs], track=True, dma=True, ch="frc")
        if dbg:
            OP("sync", lambda e: e.dma_start(out=d_dbg_r.ap()[0:1, 0:16],
                                             in_=s_sc[0:1, 0:16]),
               waits=[ev_arg2], track=True, dma=True, ch="dbg")
            OP("sync", lambda e: e.dma_start(out=d_dbg_r.ap()[1:2, 0:8],
                                             in_=s_cnt[0:1, 0:8]),
               waits=[ev_arg2], track=True, dma=True, ch="dbg")
            ev_dbgfr = OP("sync", lambda e: e.dma_start(out=d_dbg_r.ap()[2:3, :],
                                                        in_=s_fr[0:1, :]),
                          waits=[ev_frs], track=True, dma=True, ch="dbg")
        if stage >= 3:
            ev_red = {}
            _wb = [s_wn, s_sqs, s_outb]
            for i in range(9):
                m_, ii = divmod(i, 3)
                w = 768 if i == 8 else 1024
                buf = _wb[i % 3]
                def _ld(e, m_=m_, ii=ii, w=w, buf=buf):
                    if m_ == 0:
                        return e.dma_start(out=buf[:, 0:w],
                                           in_=d_wgn.ap()[ii * 128:(ii + 1) * 128, :])
                    if m_ == 1:
                        return e.dma_start(out=buf[:, 0:w],
                                           in_=d_wun.ap()[ii * 128:(ii + 1) * 128, :])
                    return e.dma_start(out=buf[:, 0:w],
                                       in_=d_wdn.ap()[:, ii * 1024:ii * 1024 + w])
                evl = OP("gpsimd", _ld, waits=[ev_red.get(i - 3)], track=True,
                         dma=True, ch="gw" + "abc"[i % 3])
                dcol = i if m_ < 2 else 7
                evr1 = OP("vector", lambda v, w=w, buf=buf, dcol=dcol: v.tensor_reduce(
                    s_rmx[:, dcol:dcol + 1], buf[:, 0:w], AX, Alu.max,
                    apply_absolute_value=True), waits=[evl], track=True)
                if m_ < 2:
                    ev_red[i] = evr1
                else:
                    def _rd(v, ii=ii):
                        if ii == 0:
                            return v.tensor_copy(out=s_rmx[:, 6:7], in_=s_rmx[:, 7:8])
                        return v.tensor_tensor(out=s_rmx[:, 6:7], in0=s_rmx[:, 6:7],
                                               in1=s_rmx[:, 7:8], op=Alu.max)
                    ev_red[i] = OP("vector", _rd, waits=[evr1], track=True)
            ev_inv = vchain([
                lambda v: v.reciprocal(out=s_inv[:, 0:7], in_=s_rmx[:, 0:7]),
                lambda v: v.tensor_scalar(out=s_inv[:, 0:7], in0=s_inv[:, 0:7],
                                          scalar1=127.0, scalar2=None, op0=Alu.mult),
                lambda v: v.tensor_scalar(out=s_sc7[:, 0:7], in0=s_rmx[:, 0:7],
                                          scalar1=1.0 / 127.0, scalar2=None, op0=Alu.mult),
            ], waits=[ev_red[i] for i in range(9)])
            ev_scs = OP("gpsimd", lambda g: g.dma_start(out=d_sc_in.ap(), in_=s_sc7[:, :]),
                        waits=[ev_inv], track=True, dma=True, ch="grt")
            ev_sc_ag = OP("gpsimd", lambda g: g.collective_compute(
                "AllGather", Alu.bypass, replica_groups=RG,
                ins=[d_sc_in.ap().opt()], outs=[d_sc_out.ap().opt()]),
                waits=[ev_scs], track=True, ch="cc")
            ev_invs = OP("gpsimd", lambda g: g.dma_start(
                out=AP(d_rowb, 0, [[1, 128], [128, 7]]), in_=s_inv[:, 0:7],
                allow_slow_non_contiguous=True),
                waits=[ev_inv], track=True, dma=True, ch="grt")

        ev_hrmm = {}
        for idx in range(16):
            kc, nk = divmod(idx, 2)
            buf = s_cosb[:, :] if idx % 2 == 0 else s_y[:, 0:512]
            if idx >= 1:
                ev_cosl[idx] = OP("sync" if idx % 2 == 0 else "scalar",
                    lambda e, kc=kc, nk=nk, buf=buf: e.dma_start(
                    out=buf,
                    in_=d_cos.ap()[kc * 128:(kc + 1) * 128, nk * 512:(nk + 1) * 512]),
                    waits=[ev_hrmm.get(idx - 2)] +
                          ([ev_frs, ev_cosl[0], ev_dbgfr] if idx == 1 else []),
                    track=True, dma=True, ch="cosa" if idx % 2 == 0 else "cosb")
            pdst = p_m if nk == 0 else p_m2
            ev_hrmm[idx] = OP("tensor", lambda t, kc=kc, pd=pdst, buf=buf: t.matmul(
                pd[0:1, 0:512], lhsT=s_frc[:, kc:kc + 1], rhs=buf,
                start=(kc == 0), stop=(kc == 7)),
                waits=[ev_cosl[idx], ev_frc], track=True)
        ev_hrev = OP("vector", lambda v: (
            v.tensor_copy(out=s_hr[0:1, 0:512], in_=p_m[0:1, 0:512]),
            v.tensor_copy(out=s_hr[0:1, 512:1024], in_=p_m2[0:1, 0:512]))[-1],
            waits=[ev_hrmm[14], ev_hrmm[15]], track=True)
        ev_hrs = OP("gpsimd", lambda g: [
            g.dma_start(out=AP(d_hrb, 128, [[1024, 1], [1, 1024]]), in_=s_hr[0:1, :]),
            g.dma_start(out=AP(d_hrb, 0, [[128, 1], [1, 128]]), in_=s_hr[0:1, 896:1024]),
            g.dma_start(out=AP(d_hrb, 1152, [[896, 1], [1, 896]]), in_=s_hr[0:1, 0:896])],
            waits=[ev_hrev], track=True, dma=True, ndma=3, ch="hrch")
        ev_hkw = OP("scalar", lambda g: g.dma_start(
            out=s_hkw[:, :].rearrange("p (d c) -> p d c", d=8),
            in_=AP(d_hrb, 1, [[1, 128], [128, 8], [1, 512]])),
            waits=[ev_hrs], track=True, dma=True, ndma=1, ch="hrch")
        if stage >= 3:
            ev_invr = OP("sync", lambda e: e.dma_start(
                out=s_invr[0:1, :], in_=AP(d_rowb, 0, [[896, 1], [1, 896]])),
                waits=[ev_invs], track=True, dma=True, ch="invr")
            ev_bcq = {}
            for ch in range(2):
                w = 512 if ch == 0 else 384
                evb = OP("tensor", lambda t, ch=ch, w=w: t.matmul(
                    p_x1[0][:, ch * 512:ch * 512 + w], lhsT=s_onef[:, :],
                    rhs=s_invr[0:1, ch * 512:ch * 512 + w],
                    start=True, stop=True),
                    waits=[ev_invr, ev_ms], track=True)
                ev_bcq[ch] = OP("vector", lambda v, ch=ch, w=w: v.tensor_copy(
                    out=s_invf[:, ch * 512:ch * 512 + w],
                    in_=p_x1[0][:, ch * 512:ch * 512 + w]),
                    waits=[evb], track=True)
            ev_invf = ev_bcq[1]
            wbufs = [s_wtr, s_sqs[:, 0:GTPR * 128], s_sqs[:, 512:896],
                     s_ysq[:, 0:GTPR * 128]]
            qbufs = [s_hrw[:, 0:GTPR * 128], s_hrw[:, 512:896],
                     s_zqt[:, 0:GTPR * 128], s_zqt[:, 512:896]]
            ev_qw = {}
            ev_push = {}
            gidx = 0
            for m in range(3):
                src = [d_wgt, d_wut, d_wdt][m]
                nht = 8 if m < 2 else NIT
                wd_ = 384 if m < 2 else 128
                for ht in range(nht):
                    gi = gidx; gidx += 1
                    par = gi % 4
                    wbuf = wbufs[par]
                    qbuf = qbufs[par]
                    ib = (gi % 2) * 512
                    ev_wl = OP("sync", lambda e, src=src, ht=ht, w=wd_, wbuf=wbuf:
                               e.dma_start(out=wbuf[:, 0:w],
                                           in_=src.ap()[ht * 128:(ht + 1) * 128, :]),
                               waits=[ev_qw.get(gi - 4),
                                      ev_inv if gi == 1 else None],
                               track=True, dma=True, ch="wtr" + "abcd"[par])
                    ev_qw[gi] = OP("vector", lambda v, m=m, w=wd_, wbuf=wbuf,
                                   qbuf=qbuf, ib=ib: (
                        v.tensor_tensor(out=s_i32[:, ib:ib + w],
                                        in0=wbuf[:, 0:w],
                                        in1=s_invf[:, (0, 384, 768)[m]:(0, 384, 768)[m] + w],
                                        op=Alu.mult),
                        v.tensor_copy(out=qbuf[:, 0:w], in_=s_i32[:, ib:ib + w]))[-1],
                        waits=[ev_wl, ev_invf, ev_push.get(gi - 4)], track=True)
                    def _push(g, m=m, ht=ht, qbuf=qbuf):
                        if m < 2:
                            return g.dma_start(
                                out=AP(d_wq_in, (m * 24 + ht) * 16384,
                                       [[128, 128], [8 * 16384, GTPR], [1, 128]]),
                                in_=qbuf[:, 0:GTPR * 128].rearrange(
                                    "p (a c2) -> p a c2", a=GTPR))
                        return g.dma_start(out=d_wq_in.ap()[48 + ht],
                                           in_=qbuf[:, 0:128])
                    ev_push[gi] = OP("gpsimd", _push, waits=[ev_qw[gi]], track=True,
                                     dma=True, ndma=1, ch="push" + "abcd"[par])
            ev_qw_last = ev_qw[gidx - 1]
            ev_push_all = [ev_push[gidx - 1], ev_push[gidx - 2],
                           ev_push[gidx - 3], ev_push[gidx - 4]]
            ev_q_ag = OP("gpsimd", lambda g: g.collective_compute(
                "AllGather", Alu.bypass, replica_groups=RG,
                ins=[d_wq_in.ap().opt()], outs=[d_wq_out.ap().opt()]),
                waits=ev_push_all, track=True, ch="cc")
        def _w2all(t):
            ins = None
            for Iq in range(8):
                for J in range(8):
                    d = (J - Iq) % 8
                    ins = t.matmul(p_m[:, Iq * 24:(Iq + 1) * 24],
                                   lhsT=s_hkw[:, d * 512:d * 512 + 128],
                                   rhs=s_wel[:, J * 24:(J + 1) * 24],
                                   start=(J == 0), stop=(J == 7))
            return ins
        evm = OP("tensor", _w2all, waits=[ev_hkw, ev_ld["wel"], ev_hrev], track=True)
        ev_w2_all = OP("scalar", lambda sc: sc.activation(
            out=s_wel2[:, :], in_=p_m[:, 0:192], func=Act.Copy),
            waits=[evm], track=True)
        ev_dbg_w2dma = None
        if dbg:
            OP("sync", lambda e: e.dma_start(out=d_dbg_r.ap()[3:4, :],
                                             in_=s_hr[0:1, :]),
               waits=[ev_hrs], track=True, dma=True, ch="dbg")
            evc = OP("vector", lambda v: v.tensor_copy(out=s_hbuf[:, 0:192],
                                                       in_=s_wel2[:, :]),
                     waits=[ev_w2_all], track=True)
            ev_dbg_w2dma = OP("sync", lambda e: e.dma_start(out=d_dbg_w2.ap(),
                                                           in_=s_hbuf[:, 0:192]),
                              waits=[evc], track=True, dma=True, ch="dbg")
            OP("gpsimd", lambda e: e.dma_start(out=d_dbg_hr.ap(),
                                               in_=AP(d_hrb, 0, [[2048, 1], [1, 2048]])),
               waits=[ev_hrs], track=True, dma=True, ch="dbg")





        ev_scl = OP("sync", lambda e: e.dma_start(
            out=s_scall[:, :], in_=AP(d_sc_out, 0, [[8, 128], [1024, 8], [1, 8]]),
            allow_slow_non_contiguous=True),
            waits=[ev_sc_ag], track=True, dma=True, ch="scl")
        ev_sdp = OP("vector", lambda v: [v.tensor_copy(
            out=s_sdp[:, r:r + 1], in_=s_scall[:, r * 8 + 6:r * 8 + 7])
            for r in range(8)][-1], waits=[ev_scl], track=True)
        ev_sds = OP("gpsimd", lambda g: g.dma_start(
            out=AP(d_rowb, 256, [[1, 128], [128, 8]]), in_=s_sdp[:, :],
            allow_slow_non_contiguous=True),
            waits=[ev_sdp], track=True, dma=True, ch="grt")
        ev_sdr = OP("sync", lambda e: e.dma_start(
            out=s_sdr[0:1, :], in_=AP(d_rowb, 256, [[1024, 1], [1, 1024]])),
            waits=[ev_sds], track=True, dma=True, ch="sdr")
        ev_y = {}
        ev_sq = {}
        ev_lnz = {}
        ev_ztr = {}
        ev_zev = {}
        ev_x1d = {}
        ev_dbgz = {}
        ev_corr = {}
        ev_gst = {}
        def emit_chain_step(c, step):
            xb = (c % 2) * 4096
            cb = (c % 2) * 512
            if step == 0:
                emit_chain_s0(c, xb, cb)
            elif step == 1:
                emit_chain_s1(c, cb)
            elif step == 2:
                emit_chain_s2(c)
            else:
                emit_chain_s3(c, cb)
        def emit_chain(c):
            for st_ in range(4):
                emit_chain_step(c, st_)
        def emit_chain_s0(c, xb, cb):
            xbuf, xb = xtf_buf(c)
            def _lat(t, c=c, xb=xb, xbuf=xbuf):
                ins = None
                for Iq in range(8):
                    ins = t.matmul(p_s[:, :], lhsT=s_wel2[:, Iq * 24:(Iq + 1) * 24],
                                   rhs=xbuf[:, xb + Iq * 512:xb + Iq * 512 + 512],
                                   start=(Iq == 0), stop=False)
                return t.matmul(p_s[:, :], lhsT=s_bel[:, :], rhs=s_one5[:, :],
                                start=False, stop=True)
            evlat = OP("tensor", _lat,
                       waits=[ev_xc[c], ev_w2_all, ev_ld["bel"], ev_ms, ev_invf,
                              *ev_push_all,
                              ev_corr.get(c - 1) if c >= 1 else None],
                       track=True)
            ev_gst[(c, "bits")] = OP("vector", lambda v, cb=cb: v.tensor_scalar(
                out=s_bits[:, cb:cb + 512], in0=p_s[:, :], scalar1=0.0, scalar2=None,
                op0=Alu.is_gt), waits=[evlat], track=True)
        def emit_chain_s1(c, cb):
            evsyn = OP("tensor", lambda t, cb=cb: t.matmul(
                p_s[0:12, :], lhsT=s_hmt[:, :], rhs=s_bits[:, cb:cb + 512],
                start=True, stop=True),
                waits=[ev_gst[(c, "bits")], ev_ld["hmt"]], track=True)
            ev_gst[(c, "synb")] = OP("vector", lambda v: v.tensor_scalar(
                out=s_syn[0:12, :], in0=p_s[0:12, :], scalar1=0.5, scalar2=None,
                op0=Alu.is_gt), waits=[evsyn], track=True)
        def emit_chain_s2(c):
            evw = OP("tensor", lambda t: t.matmul(p_s[0:1, :], lhsT=s_onek[:, :],
                                                  rhs=s_syn[0:12, :], start=True, stop=True),
                     waits=[ev_gst[(c, "synb")]], track=True)
            ev_gst[(c, "fix")] = OP("vector", lambda v: (
                v.tensor_scalar(out=s_fixa[0:1, :], in0=p_s[0:1, :], scalar1=3.5,
                                scalar2=None, op0=Alu.is_lt),
                v.scalar_tensor_tensor(out=s_fix[0:1, :], in0=p_s[0:1, :], scalar=0.5,
                                       op0=Alu.is_gt, in1=s_fixa[0:1, :], op1=Alu.mult))[-1],
                waits=[evw], track=True)
        def emit_chain_s3(c, cb):
            evfb = OP("tensor", lambda t: t.matmul(p_s[0:24, :], lhsT=s_mask24[:, :],
                                                   rhs=s_fix[0:1, :], start=True, stop=True),
                      waits=[ev_gst[(c, "fix")], ev_ld["mask24"]], track=True)
            ev_corr[c] = OP("vector", lambda v, cb=cb: (
                v.tensor_tensor(out=s_pp[:, :], in0=s_syn[:, :], in1=p_s[0:24, :],
                                op=Alu.mult),
                v.tensor_scalar(out=s_t12[:, :], in0=s_bits[:, cb:cb + 512], scalar1=-2.0,
                                scalar2=1.0, op0=Alu.mult, op1=Alu.add),
                v.tensor_tensor(out=s_pp[:, :], in0=s_pp[:, :], in1=s_t12[:, :],
                                op=Alu.mult),
                v.tensor_tensor(out=s_corr[:, cb:cb + 512], in0=s_bits[:, cb:cb + 512],
                                in1=s_pp[:, :], op=Alu.add))[-1],
                waits=[evfb, ev_x1d.get((c - 2, 3))], track=True)

        def emit_head(t):
            c, sub = divmod(t, 4)
            xbuf, xb = xtf_buf(c)
            cb = (c % 2) * 512
            par = t % 2
            lb = (t % 2) * 8
            def _x1(te, c=c, sub=sub, par=par, xb=xb, cb=cb, xbuf=xbuf):
                ins = None
                for Jc in range(2):
                    for Iq in range(8):
                        d0 = (4 * Jc - Iq) % 8
                        ins = te.matmul(
                            p_x1[par][:, Jc * 512:(Jc + 1) * 512],
                            lhsT=xbuf[:, xb + Iq * 512 + sub * 128:
                                      xb + Iq * 512 + sub * 128 + 128],
                            rhs=s_hkw[:, d0 * 512:(d0 + 1) * 512],
                            start=(Iq == 0), stop=False)
                    te.matmul(
                        p_x1[par][:, Jc * 512:(Jc + 1) * 512],
                        lhsT=s_corr[:, cb + sub * 128:cb + sub * 128 + 128],
                        rhs=s_wle[:, Jc * 512:(Jc + 1) * 512],
                        start=False, stop=False)
                    ins = te.matmul(
                        p_x1[par][:, Jc * 512:(Jc + 1) * 512],
                        lhsT=s_one5[0:1, 0:128],
                        rhs=s_wleb[0:1, Jc * 512:(Jc + 1) * 512],
                        start=False, stop=True)
                return ins
            ev_x1d[(c, sub)] = OP("tensor", _x1,
                                  waits=[ev_corr[c], ev_hkw, ev_ld["wle"],
                                         ev_ld["wleb"], ev_y.get(t - 2)], track=True)
            yb = s_y if t % 2 == 0 else s_sdx
            ev_y[t] = OP("vector", lambda v, t=t, par=par, lb=lb, yb=yb:
                         v.scalar_tensor_tensor(
                out=yb[:, :], in0=p_x1[par][:, :], scalar=1.0, op0=Alu.mult,
                in1=s_xn[:, (t % 2) * E:(t % 2) * E + E], op1=Alu.add,
                accum_out=s_lnc[:, lb:lb + 1]),
                waits=[ev_x1d[(c, sub)], ev_xn[t], ev_lnz.get(t - 2)], track=True)
            if t + 2 < NT:
                ev_xn[t + 2] = OP("scalar", lambda e, t2=t + 2: e.dma_start(
                    out=s_xn[:, (t2 % 2) * E:(t2 % 2) * E + E],
                    in_=d_xnat.ap()[t2 * 128:(t2 + 1) * 128, :]),
                    waits=[ev_y[t]], track=True, dma=True,
                    ch="xna" if t % 2 == 0 else "xnb")
            ev_sq[t] = OP("scalar", lambda s, lb=lb, yb=yb: s.activation(
                out=s_sqs[:, :], in_=yb[:, :], func=Act.Square,
                accum_out=s_lnc[:, lb + 1:lb + 2]),
                waits=[ev_y[t], ev_qw_last if t == 0 else None], track=True)

        ev_s6 = {}
        ev_za = {}
        def emit_sch_a(t):
            lb = (t % 2) * 8
            ev1 = OP("scalar", lambda s, lb=lb: s.activation(
                out=s_lnc[:, lb + 2:lb + 3], in_=s_lnc[:, lb:lb + 1],
                func=Act.Identity, scale=1.0 / E), waits=[ev_y[t]], track=True)
            ev2 = OP("scalar", lambda s, lb=lb: s.activation(
                out=s_lnc[:, lb + 3:lb + 4], in_=s_lnc[:, lb + 2:lb + 3],
                func=Act.Copy, scale=-1.0), waits=[ev1], track=True)
            ev3 = OP("scalar", lambda s, lb=lb: s.activation(
                out=s_lnc[:, lb + 4:lb + 5], in_=s_lnc[:, lb + 2:lb + 3],
                func=Act.Square), waits=[ev2], track=True)
            ev4 = OP("scalar", lambda s, lb=lb: s.activation(
                out=s_lnc[:, lb + 4:lb + 5], in_=s_lnc[:, lb + 4:lb + 5],
                func=Act.Copy, scale=-1.0), waits=[ev3], track=True)
            ev5 = OP("scalar", lambda s, lb=lb: s.activation(
                out=s_lnc[:, lb + 5:lb + 6], in_=s_lnc[:, lb + 1:lb + 2],
                func=Act.Identity, scale=1.0 / E, bias=s_lnc[:, lb + 4:lb + 5]),
                waits=[ev4, ev_sq[t]], track=True)
            ev_s6[t] = OP("scalar", lambda s, lb=lb: s.activation(
                out=s_lnc[:, lb + 5:lb + 6], in_=s_lnc[:, lb + 5:lb + 6],
                func=Act.Sqrt, bias=s_col[:, 4:5]), waits=[ev5], track=True)
        def emit_vtail(t):
            lb = (t % 2) * 8
            ev_za[t] = OP("vector", lambda v, lb=lb: v.reciprocal(
                out=s_lnc[:, lb + 6:lb + 7], in_=s_lnc[:, lb + 5:lb + 6]),
                waits=[ev_s6[t]], track=True)
        def emit_sch_b(t):
            lb = (t % 2) * 8
            yb = s_y if t % 2 == 0 else s_sdx
            ev7 = OP("scalar", lambda s, lb=lb: s.activation(
                out=s_lnc[:, lb + 7:lb + 8], in_=s_lnc[:, lb + 6:lb + 7],
                func=Act.Identity, scale=s_lnc[:, lb + 3:lb + 4]),
                waits=[ev_za[t]], track=True)
            ev_lnz[t] = OP("scalar", lambda s, t=t, lb=lb, yb=yb: s.activation(
                out=s_z[:, t * E:(t + 1) * E], in_=yb[:, :],
                func=Act.Identity, scale=s_lnc[:, lb + 6:lb + 7],
                bias=s_lnc[:, lb + 7:lb + 8]),
                waits=[ev7, ev_x1d[(2, 3)] if t == 12 else None], track=True)
        def emit_zrow(t):
            evzr = OP("vector", lambda v, t=t: v.tensor_reduce(
                s_col[:, 5 + t % 2:6 + t % 2], s_z[:, t * E:(t + 1) * E], AX,
                Alu.max, apply_absolute_value=True), waits=[ev_lnz[t]], track=True)
            return OP("vector", lambda v, t=t: v.tensor_tensor(
                out=s_m1a[:, :], in0=s_m1a[:, :],
                in1=s_col[:, 5 + t % 2:6 + t % 2],
                op=Alu.max), waits=[evzr], track=True)

        def emit_tail_t(t):
            ev_ztr[t] = OP("tensor", lambda te, t=t: [te.transpose(
                p_t[:, ht * 128:(ht + 1) * 128],
                s_z[:, t * E + ht * 128:t * E + ht * 128 + 128],
                s_id[:, :]) for ht in range(8)][-1],
                waits=[ev_lnz[t], ev_ld["id"], ev_zev.get(t - 1)], track=True)
            ev_zev[t] = OP("vector", lambda v, t=t: v.tensor_copy(
                out=s_zqt[:, :].rearrange("p (h c) -> p h c",
                                          h=8)[:, :, t * 128:(t + 1) * 128],
                in_=p_t[:, :].rearrange("p (h c) -> p h c", h=8)),
                waits=[ev_ztr[t]] + (ev_push_all if t == 0 else []), track=True)
            if dbg:
                evdz = OP("vector", lambda v, t=t: v.tensor_copy(
                    out=s_outb[:, :], in_=s_z[:, t * E:(t + 1) * E]),
                    waits=[ev_lnz[t], ev_dbgz.get(t - 1)], track=True)
                ev_dbgz[t] = OP("sync", lambda e, t=t: e.dma_start(
                    out=d_dbg_z.ap()[t * 128:(t + 1) * 128, :], in_=s_outb[:, :]),
                    waits=[evdz], track=True, dma=True, ch="dbg")

        emit_chain(0)
        ev_m1u = None
        for t in range(NT):
            c, sub = divmod(t, 4)
            emit_head(t)
            if c + 1 < NCH:
                emit_chain_step(c + 1, sub)
            if t >= 1:
                emit_vtail(t - 1)
                emit_sch_b(t - 1)
            emit_sch_a(t)
            if t >= 2:
                ev_m1u = emit_zrow(t - 2)
            if t >= 3:
                emit_tail_t(t - 3)
            if sub == 3 and c == 0:
                ev_xc[3] = OP("gpsimd", _mk_xc(3),
                              waits=[ev_x1d[(0, 3)]], track=True, dma=True,
                              ndma=1, ch="xca")
        emit_vtail(NT - 1)
        emit_sch_b(NT - 1)
        ev_m1u = emit_zrow(NT - 2)
        ev_m1u = emit_zrow(NT - 1)
        emit_tail_t(NT - 3)
        emit_tail_t(NT - 2)
        emit_tail_t(NT - 1)
        for wi in range(110):
            OP("tensor", lambda t: t.matmul(
                p_s[:, :], lhsT=s_id[:, 0:24], rhs=s_hkw[:, 0:512],
                start=True, stop=True),
               waits=[ev_corr[NCH - 1]] if wi == 0 else ())

        if stage < 3:
            ev_m1c = OP("gpsimd", lambda g: g.dma_start(out=AP(d_rowb, 0, [[1, 128]]),
                                                        in_=s_m1a[:, 0:1]),
                        waits=[ev_m1u, ev_ms], track=True, dma=True, ch="m1g")
            ev_m1r = OP("sync", lambda e: e.dma_start(
                out=s_y[0:1, 0:128], in_=AP(d_rowb, 0, [[128, 1], [1, 128]])),
                waits=[ev_m1c], track=True, dma=True, ch="m1r")
            ev_m1p = OP("vector", lambda v: v.tensor_reduce(
                s_m8[0:1, 0:1], s_y[0:1, 0:128], AX, Alu.max),
                waits=[ev_m1r], track=True)
            ev_od = {}
            for t in range(NT):
                evc = OP("vector", lambda v, t=t: v.tensor_copy(
                    out=s_outb[:, :], in_=s_z[:, t * E:(t + 1) * E]),
                    waits=[ev_lnz[t], ev_od.get(t - 1)], track=True)
                ev_od[t] = OP("sync", lambda e, t=t: e.dma_start(
                    out=d_out.ap()[t * 128:(t + 1) * 128, :], in_=s_outb[:, :]),
                    waits=[evc], track=True, dma=True, ch="out")
            if dbg:
                evm8 = OP("gpsimd", lambda g: g.dma_start(out=d_dbg_m.ap()[0:1, 0:8],
                                                          in_=s_m8[:, :]),
                          waits=[ev_m1p], track=True, dma=True, ch="dbg")
                OP("gpsimd", lambda g: g.engine_nop(), waits=[evm8])
            OP("sync", lambda e: e.nop(), waits=[ev_od[NT - 1]])
            sems = {c: stk.enter_context(nc.semaphore("sem_" + c)) for c in sorted(S.channels)}
            blk = stk.enter_context(nc.Block())
            S.emit(blk, sems)
            return nc

        def _wqld_mk(ch, base):
            def _wqld(e, ch=ch, base=base):
                out = []
                for gu in range(2):
                    for li, it in enumerate(ch):
                        r, itl = it // GTPR, it % GTPR
                        slot0 = gu * 24 + itl * 8
                        dst = s_wq[:, base + (gu * 4 + li) * 1024:
                                   base + (gu * 4 + li) * 1024 + 1024]
                        out.append(e.dma_start(
                            out=dst.rearrange("p (s j) -> p s j", s=8),
                            in_=d_wq_out.ap()[r, slot0:slot0 + 8].transpose([1, 0, 2])))
                return out
            return _wqld
        CHW = [[0, 1, 2, 3], [4, 5, 6, 7], [8, 9, 10, 11], [12, 13, 14, 15],
               [16, 17, 18, 19], [20, 21]]
        ev_wqld = {}
        for qi in range(2):
            ev_wqld[qi] = OP("sync", _wqld_mk(CHW[qi], (qi % 2) * 8192),
                             waits=[ev_q_ag], track=True, dma=True,
                             ndma=2 * len(CHW[qi]),
                             ch="wqa" if qi % 2 == 0 else "wqb")
        ev_m1c = OP("gpsimd", lambda g: g.dma_start(out=AP(d_m1_in, 0, [[1, 128]]),
                                                    in_=s_m1a[:, 0:1]),
                    waits=[ev_m1u, ev_ms], track=True, dma=True, ch="m1g")
        ev_m1ar = OP("gpsimd", lambda g: g.collective_compute(
            "AllReduce", Alu.max, replica_groups=RG,
            ins=[d_m1_in.ap().opt()], outs=[d_m1_out.ap().opt()]),
            waits=[ev_m1c], track=True, ch="cc")
        ev_m1l = OP("sync", lambda e: e.dma_start(out=s_y[0:1, 0:128], in_=d_m1_out.ap()),
                    waits=[ev_m1ar], track=True, dma=True, ch="m1r")
        ev_m1p = OP("vector", lambda v: v.tensor_reduce(
            s_mg[0:1, 0:1], s_y[0:1, 0:128], AX, Alu.max),
            waits=[ev_m1l], track=True)
        ev_sig = OP("scalar", lambda s: s.activation(out=s_mg[0:1, 8:10],
                                                     in_=s_clip[0:1, :], func=Act.Sigmoid),
                    waits=[ev_ld["clip"]], track=True)
        ev_s1 = vchain([
            lambda v: v.tensor_tensor(out=s_mg[0:1, 10:11], in0=s_mg[0:1, 8:9],
                                      in1=s_mg[0:1, 0:1], op=Alu.mult),
            lambda v: v.tensor_scalar(out=s_mg[0:1, 10:11], in0=s_mg[0:1, 10:11],
                                      scalar1=1.0 / 127.0, scalar2=None, op0=Alu.mult),
            lambda v: v.reciprocal(out=s_mg[0:1, 11:12], in_=s_mg[0:1, 10:11]),
        ], waits=[ev_m1p, ev_sig])
        ev_bc1 = OP("tensor", lambda t: t.matmul(p_m[:, 0:2], lhsT=s_onef[:, :],
                                                 rhs=s_mg[0:1, 10:12], start=True, stop=True),
                    waits=[ev_s1, ev_bcq[0], ev_bcq[1]], track=True)
        ev_col1 = OP("vector", lambda v: v.tensor_copy(out=s_col[:, 0:2], in_=p_m[:, 0:2]),
                     waits=[ev_bc1], track=True)
        ev_zqa = {}
        ev_zqb = {}
        for k in range(32):
            ib = (k % 2) * 512
            sl = s_zqt[:, k * 512:(k + 1) * 512]
            ev_zqa[k] = OP("scalar", lambda s, sl=sl, ib=ib: s.activation(
                out=s_i32[:, ib:ib + 512], in_=sl, func=Act.Identity,
                scale=s_col[:, 1:2]),
                waits=[ev_col1 if k == 0 else None,
                       ev_zev[NT - 1] if k == 0 else None,
                       ev_zqb.get(k - 2)], track=True)
            ev_zqb[k] = OP("vector", lambda v, sl=sl, ib=ib: v.tensor_scalar(
                out=sl, in0=s_i32[:, ib:ib + 512], scalar1=-128.0,
                scalar2=127.0, op0=Alu.max, op1=Alu.min),
                waits=[ev_zqa[k]], track=True)
        ev_zq = ev_zqb[31]
        def _sgf(v):
            ins = None
            for it in range(NIT):
                r, c0 = it // GTPR, it % GTPR
                v.tensor_scalar(out=s_sgf[:, it:it + 1],
                                in0=s_scall[:, r * 8 + c0:r * 8 + c0 + 1],
                                scalar1=s_col[:, 0:1], scalar2=None, op0=Alu.mult)
                ins = v.tensor_scalar(out=s_suf[:, it:it + 1],
                                      in0=s_scall[:, r * 8 + 3 + c0:r * 8 + 4 + c0],
                                      scalar1=s_col[:, 0:1], scalar2=None, op0=Alu.mult)
            return ins
        ev_sgf = OP("vector", _sgf, waits=[ev_scl, ev_col1], track=True)
        ev_qsfa = OP("vector", lambda v: v.memset(s_qscf[:, 0:128], 1.0),
                     waits=[ev_sgf], track=True)
        def _qsfb(v):
            ins = None
            for it in range(NIT - 1, -1, -1):
                ins = v.tensor_scalar(out=s_qscf[:, it * 128:(it + 1) * 128],
                                      in0=s_qscf[:, 0:128],
                                      scalar1=s_suf[:, it:it + 1], scalar2=None,
                                      op0=Alu.mult)
            return ins
        ev_qsf = OP("vector", _qsfb, waits=[ev_qsfa], track=True)

        ev_sdb = {}
        for half in range(2):
            evb = OP("tensor", lambda t, h=half: t.matmul(
                p_m2[:, 0:512], lhsT=s_onef[:, :], rhs=s_sdr[0:1, h * 512:h * 512 + 512],
                start=True, stop=True),
                waits=[ev_sdr, ev_sdb.get(0), ev_bcq[1], ev_lnz[NT - 1]], track=True)
            ev_sdb[half] = OP("vector", lambda v, h=half: v.tensor_copy(
                out=s_sdx[:, h * 512:h * 512 + 512], in_=p_m2[:, 0:512]),
                waits=[evb], track=True)
        def _mk_qd(lo, hi):
            def _f(e):
                return [e.dma_start(
                    out=s_wq[:, :].rearrange("p (a k j) -> p a k j",
                                             a=NIT, k=8)[:, lo:hi, et, :],
                    in_=d_wq_out.ap()[et, 48 + lo:48 + hi].transpose([1, 0, 2]))
                    for et in range(8)]
            return _f
        ev_h = {}
        ev_hsp = {}
        ev_si = {}
        hidx = 0
        h_of_chunk_end = {}
        for qi, ch in enumerate(CHW):
            base = (qi % 2) * 8192
            if qi >= 2:
                ev_wqld[qi] = OP("sync", _wqld_mk(ch, base),
                                 waits=[ev_q_ag, h_of_chunk_end.get(qi - 2)],
                                 track=True, dma=True, ndma=2 * len(ch),
                                 ch="wqa" if qi % 2 == 0 else "wqb")
            if qi == 1:
                ev_qd1 = OP("scalar", _mk_qd(16, NIT), waits=[ev_q_ag],
                            track=True, dma=True, ndma=8, ch="qd")
            if qi == 5:
                ev_qd2 = OP("sync", _mk_qd(0, 8), waits=[h_of_chunk_end[4]],
                            track=True, dma=True, ndma=8, ch="qd")
            for c in range(NCH):
                for li, it in enumerate(ch):
                    slot = hidx % 3
                    gpd = [p_x1[0][:, 0:512], p_x1[1][:, 0:512],
                           p_m[:, 0:512]][slot]
                    upd = [p_x1[0][:, 512:1024], p_x1[1][:, 512:1024],
                           p_m2[:, 0:512]][slot]
                    def _gu(te, base=base, c=c, li=li, gpd=gpd, upd=upd):
                        ins = None
                        for gu in range(2):
                            pd = gpd if gu == 0 else upd
                            for ht in range(8):
                                ins = te.matmul(
                                    pd, lhsT=s_wq[:, base + (gu * 4 + li) * 1024 + ht * 128:
                                                  base + (gu * 4 + li) * 1024 + ht * 128 + 128],
                                    rhs=s_zqt[:, ht * TPC + c * 512:ht * TPC + c * 512 + 512],
                                    start=(ht == 0), stop=(ht == 7))
                        return ins
                    evgu = OP("tensor", _gu,
                              waits=[ev_wqld[qi], ev_zq, ev_h.get(hidx - 3),
                                     ev_sdb[1] if hidx == 2 else None], track=True)
                    ev_si[hidx] = OP("scalar", lambda sc, it=it, gpd=gpd, hi=hidx:
                                     sc.activation(
                        out=s_hf[:, (hi % 2) * 512:(hi % 2) * 512 + 512],
                        in_=gpd, func=Act.Silu, scale=s_sgf[:, it:it + 1]),
                        waits=[evgu, ev_sgf, ev_h.get(hidx - 2)], track=True)
                    ev_h[hidx] = OP("vector", lambda v, upd=upd, hi=hidx: v.tensor_tensor(
                        out=s_hrw[:, (hi % 2) * 512:(hi % 2) * 512 + 512],
                        in0=s_hf[:, (hi % 2) * 512:(hi % 2) * 512 + 512],
                        in1=upd, op=Alu.mult),
                        waits=[ev_si[hidx], ev_hsp.get(hidx - 2)] +
                              (ev_push_all if hidx < 2 else []), track=True)
                    evhr2 = OP("vector", lambda v, hi=hidx: v.tensor_reduce(
                        s_lnc[:, 6:7], s_hrw[:, (hi % 2) * 512:(hi % 2) * 512 + 512],
                        AX, Alu.max, apply_absolute_value=True),
                        waits=[ev_h[hidx]], track=True)
                    ev_m2u = OP("vector", lambda v, it=it: v.scalar_tensor_tensor(
                        out=s_m2a[:, :], in0=s_lnc[:, 6:7], scalar=s_suf[:, it:it + 1],
                        op0=Alu.mult, in1=s_m2a[:, :], op1=Alu.max),
                        waits=[evhr2], track=True)
                    ev_hsp[hidx] = OP("gpsimd", lambda g, it=it, c=c, hi=hidx: g.dma_start(
                        out=d_spill.ap()[it, :, c * 512:(c + 1) * 512],
                        in_=s_hrw[:, (hi % 2) * 512:(hi % 2) * 512 + 512]),
                        waits=[ev_h[hidx]], track=True, dma=True,
                        ch="spa" if hidx % 2 == 0 else "spb")
                    hidx += 1
            h_of_chunk_end[qi] = ev_h[hidx - 1]
        ev_hsp_all = [ev_hsp[hidx - 1], ev_hsp[hidx - 2]]
        for wi in range(110):
            OP("tensor", lambda t: t.matmul(
                p_s[:, :], lhsT=s_id[:, 0:24], rhs=s_hkw[:, 0:512],
                start=True, stop=True))

        ev_m2c = OP("gpsimd", lambda g: g.dma_start(out=AP(d_m2_in, 0, [[1, 128]]),
                                                    in_=s_m2a[:, 0:1]),
                    waits=[ev_m2u], track=True, dma=True, ch="m2g")
        ev_m2ar = OP("gpsimd", lambda g: g.collective_compute(
            "AllReduce", Alu.max, replica_groups=RG,
            ins=[d_m2_in.ap().opt()], outs=[d_m2_out.ap().opt()]),
            waits=[ev_m2c], track=True, ch="cc")
        ev_m2l = OP("sync", lambda e: e.dma_start(out=s_y[0:1, 0:128], in_=d_m2_out.ap()),
                    waits=[ev_m2ar], track=True, dma=True, ch="m2r")
        ev_m2p = OP("vector", lambda v: v.tensor_reduce(
            s_mg[0:1, 12:13], s_y[0:1, 0:128], AX, Alu.max),
            waits=[ev_m2l], track=True)
        ev_s2 = vchain([
            lambda v: v.tensor_tensor(out=s_mg[0:1, 13:14], in0=s_mg[0:1, 9:10],
                                      in1=s_mg[0:1, 12:13], op=Alu.mult),
            lambda v: v.tensor_scalar(out=s_mg[0:1, 13:14], in0=s_mg[0:1, 13:14],
                                      scalar1=1.0 / 127.0, scalar2=None, op0=Alu.mult),
            lambda v: v.reciprocal(out=s_mg[0:1, 14:15], in_=s_mg[0:1, 13:14]),
        ], waits=[ev_m2p, ev_sig])
        ev_bc2 = OP("tensor", lambda t: t.matmul(p_m[:, 2:4], lhsT=s_onef[:, :],
                                                 rhs=s_mg[0:1, 13:15], start=True, stop=True),
                    waits=[ev_s2, ev_col1], track=True)
        ev_col2 = OP("vector", lambda v: v.tensor_copy(out=s_col[:, 2:4], in_=p_m[:, 2:4]),
                     waits=[ev_bc2], track=True)
        def _qscm(v):
            ins = None
            for seg in range(3):
                w = 1024 if seg < 2 else 768
                ins = v.tensor_scalar(out=s_qscf[:, seg * 1024:seg * 1024 + w],
                                      in0=s_qscf[:, seg * 1024:seg * 1024 + w],
                                      scalar1=s_col[:, 3:4], scalar2=None, op0=Alu.mult)
            return ins
        ev_qsc = OP("vector", _qscm, waits=[ev_col2, ev_qsf], track=True)
        ev_sdx2 = OP("vector", lambda v: v.tensor_scalar(
            out=s_sdx[:, :], in0=s_sdx[:, :], scalar1=s_col[:, 2:3], scalar2=None,
            op0=Alu.mult), waits=[ev_sdb[1], ev_col2], track=True)
        ev_qdld = OP("scalar", _mk_qd(8, 16), waits=[ev_h[hidx - 1]],
                     track=True, dma=True, ndma=8, ch="qd")

        ev_hl = {}
        ev_qh = {}
        ev_dn = {}
        ev_y2 = {}
        ev_sq2 = {}
        ev_ov = {}
        ev_od = {}

        def emit_front(t):
            ev_hl[t] = OP("sync", lambda e, t=t: e.dma_start(
                out=s_zqt[:, (t % 2) * IF:(t % 2) * IF + IF],
                in_=d_spill.ap()[:, :, t * 128:(t + 1) * 128].transpose([1, 0, 2])),
                waits=[ev_hsp[80 + 2 * (t // 4)], ev_hsp[81 + 2 * (t // 4)],
                       ev_qh.get(t - 2), ev_dbg_w2dma], track=True, dma=True,
                ch="hla" if t % 2 == 0 else "hlb")
            def _qh(v, t=t):
                hb = (t % 2) * IF
                qb = (t % 2) * IF
                ins = None
                for seg in range(3):
                    w = 1024 if seg < 2 else 768
                    o = seg * 1024
                    v.tensor_tensor(out=s_i32[:, 0:w], in0=s_zqt[:, hb + o:hb + o + w],
                                    in1=s_qscf[:, o:o + w], op=Alu.mult)
                    ins = v.tensor_scalar(out=s_qht[:, qb + o:qb + o + w], in0=s_i32[:, 0:w],
                                          scalar1=-128.0, scalar2=127.0,
                                          op0=Alu.max, op1=Alu.min)
                return ins
            ev_qh[t] = OP("vector", _qh, waits=[ev_hl[t], ev_qsc, ev_dn.get(t - 2)],
                          track=True)
            slot = t % 3
            pds = [[p_x1[0][:, 0:512], p_x1[0][:, 512:1024]],
                   [p_x1[1][:, 0:512], p_x1[1][:, 512:1024]],
                   [p_m[:, 0:512], p_m2[:, 0:512]]][slot]
            def _dn(te, t=t, pds=pds):
                qb = (t % 2) * IF
                ins = None
                for ec in range(2):
                    for it in range(NIT):
                        ins = te.matmul(pds[ec],
                                        lhsT=s_qht[:, qb + it * 128:qb + it * 128 + 128],
                                        rhs=s_wq[:, it * 1024 + ec * 512:
                                                 it * 1024 + ec * 512 + 512],
                                        start=(it == 0), stop=(it == NIT - 1))
                return ins
            ev_dn[t] = OP("tensor", _dn, waits=[ev_qh[t], ev_qdld, ev_y2.get(t - 3)],
                          track=True)

        def emit_back(t):
            slot = t % 3
            pds = [[p_x1[0][:, 0:512], p_x1[0][:, 512:1024]],
                   [p_x1[1][:, 0:512], p_x1[1][:, 512:1024]],
                   [p_m[:, 0:512], p_m2[:, 0:512]]][slot]
            def _y2(v, t=t, pds=pds):
                ins = None
                for ec in range(2):
                    sl = slice(ec * 512, (ec + 1) * 512)
                    v.tensor_tensor(out=s_ysq[:, sl], in0=pds[ec],
                                    in1=s_sdx[:, sl], op=Alu.mult)
                    ins = v.scalar_tensor_tensor(
                        out=s_y[:, sl], in0=s_ysq[:, sl], scalar=1.0, op0=Alu.mult,
                        in1=s_z[:, t * E + ec * 512:t * E + ec * 512 + 512], op1=Alu.add,
                        accum_out=s_lnc[:, 0:1] if ec == 0 else s_lnc[:, 1:2])
                return ins
            ev_y2[t] = OP("vector", _y2, waits=[ev_dn[t], ev_sdx2], track=True)
            ev_sq2[t] = OP("scalar", lambda s: s.activation(
                out=s_sqs[:, :], in_=s_y[:, :], func=Act.Square, accum_out=s_lnc[:, 2:3]),
                waits=[ev_y2[t]], track=True)
            evv2 = vchain([
                lambda v: v.tensor_tensor(out=s_lnc[:, 0:1], in0=s_lnc[:, 0:1],
                                          in1=s_lnc[:, 1:2], op=Alu.add),
                lambda v: v.tensor_scalar(out=s_lnc[:, 3:4], in0=s_lnc[:, 0:1],
                                          scalar1=1.0 / E, scalar2=None, op0=Alu.mult),
                lambda v: v.tensor_tensor(out=s_lnc[:, 4:5], in0=s_lnc[:, 3:4],
                                          in1=s_lnc[:, 3:4], op=Alu.mult),
                lambda v: v.scalar_tensor_tensor(out=s_lnc[:, 4:5], in0=s_lnc[:, 2:3],
                                                 scalar=1.0 / E, op0=Alu.mult,
                                                 in1=s_lnc[:, 4:5], op1=Alu.subtract),
                lambda v: v.tensor_scalar(out=s_lnc[:, 4:5], in0=s_lnc[:, 4:5],
                                          scalar1=1e-5, scalar2=None, op0=Alu.add),
            ], waits=[ev_sq2[t]])
            evrt2 = OP("scalar", lambda s: s.activation(
                out=s_lnc[:, 5:6], in_=s_lnc[:, 4:5], func=Act.Sqrt),
                waits=[evv2], track=True)
            evoa = OP("vector", lambda v: v.reciprocal(
                out=s_lnc[:, 6:7], in_=s_lnc[:, 5:6]), waits=[evrt2], track=True)
            obuf = s_outb if t % 2 == 0 else s_wn
            ev_ov[t] = OP("vector", lambda v, t=t, obuf=obuf: v.tensor_scalar(
                out=obuf[:, :], in0=s_y[:, :], scalar1=s_lnc[:, 3:4],
                scalar2=s_lnc[:, 6:7], op0=Alu.subtract, op1=Alu.mult),
                waits=[evoa, ev_od.get(t - 2)], track=True)
            ev_od[t] = OP("gpsimd", lambda g, t=t, obuf=obuf: g.dma_start(
                out=d_out.ap()[t * 128:(t + 1) * 128, :],
                in_=obuf[:, :]),
                waits=[ev_ov[t]], track=True, dma=True,
                ch="outa" if t % 2 == 0 else "outb")

        for t in range(NT + 1):
            if t < NT:
                emit_front(t)
            if t >= 1:
                emit_back(t - 1)

        if dbg:
            OP("sync", lambda e: e.dma_start(out=d_dbg_m.ap(), in_=s_mg[0:1, :]),
               waits=[ev_s2], track=True, dma=True, ch="dbg")
        OP("gpsimd", lambda g: g.engine_nop(),
           waits=[ev_od[NT - 1], ev_od[NT - 2]])
        sems = {c: stk.enter_context(nc.semaphore("sem_" + c)) for c in sorted(S.channels)}
        blk = stk.enter_context(nc.Block())
        S.emit(blk, sems)
    return nc


def prep_inputs(inp):
    import ml_dtypes
    x = np.ascontiguousarray(np.asarray(inp["L0_attn_x"], dtype=np.float32).reshape(NC * TPC, E))
    wg = np.asarray(inp["L0_ffn_w_gate"], dtype=np.float32)
    wu = np.asarray(inp["L0_ffn_w_up"], dtype=np.float32)
    wd = np.asarray(inp["L0_ffn_w_down"], dtype=np.float32)
    wel = np.ascontiguousarray(np.asarray(inp["L0_attn_W_el"], dtype=np.float32))
    bel = np.ascontiguousarray(np.asarray(inp["L0_attn_b_el"], dtype=np.float32).reshape(1, 24))
    wle = np.ascontiguousarray(np.asarray(inp["L0_attn_W_le"], dtype=np.float32))
    wleb = np.ascontiguousarray(np.asarray(inp["L0_attn_b_le"], dtype=np.float32).reshape(1, E))
    mask24 = np.zeros((1, 24), dtype=np.float32); mask24[0, :12] = 1.0
    clipv = np.array([[float(np.asarray(inp["L0_ffn_clip_x"]).reshape(-1)[0]),
                       float(np.asarray(inp["L0_ffn_clip_down"]).reshape(-1)[0])]],
                     dtype=np.float32)
    k = np.fft.fftfreq(E).astype(np.float32)
    ka = (np.abs(k) + np.float32(1e-8)).reshape(1, E).astype(np.float32)
    ka2 = (ka * ka).astype(np.float32)
    dd = np.arange(E)
    cosm = np.ascontiguousarray(np.cos(2 * np.pi * np.outer(dd, dd) / E).astype(np.float32))
    lxr = np.zeros((1, 8), dtype=np.float32)
    lxr[0, :5] = LX.astype(np.float32)
    _A = np.array([[1,1,1,0,1,1,1,0,0,0,1,1],[1,1,0,1,1,1,0,0,0,1,1,0],[1,0,1,1,1,0,0,0,1,1,0,1],[0,1,1,1,0,0,0,1,1,0,1,1],[0,0,0,0,1,1,1,1,1,1,1,1],[1,1,1,1,0,0,0,0,0,0,0,0],[0,0,0,0,1,1,1,1,0,0,0,0],[1,1,0,0,1,1,0,0,1,1,0,0],[1,0,1,0,1,0,1,0,1,0,1,0],[0,1,0,1,0,1,0,1,0,1,0,1],[1,0,0,1,1,0,0,1,1,0,0,1],[0,1,1,0,0,1,1,0,0,1,1,0]], dtype=np.float32)
    Hm = np.concatenate([_A.T, np.eye(12, dtype=np.float32)], axis=1)
    hmt = np.ascontiguousarray(Hm.T).astype(ml_dtypes.bfloat16)
    ident = np.eye(128, dtype=ml_dtypes.bfloat16)
    xfirst = np.ascontiguousarray(x[0:1, :])
    wgT = np.ascontiguousarray(wg.T)
    wuT = np.ascontiguousarray(wu.T)
    wdT = np.ascontiguousarray(wd.T)
    in_maps = []
    for c in range(NC):
        sh = np.ascontiguousarray(x[c * TPC:(c + 1) * TPC, :])
        xt = np.ascontiguousarray(sh.T)
        xtf = np.ascontiguousarray(xt.reshape(8, 128, TPC)[:, ::-1, :].reshape(E, TPC))
        git = [min(GTPR * c + i, NIT - 1) for i in range(GTPR)]
        wgn = np.ascontiguousarray(np.concatenate([wg[i * 128:(i + 1) * 128, :] for i in git], axis=0))
        wun = np.ascontiguousarray(np.concatenate([wu[i * 128:(i + 1) * 128, :] for i in git], axis=0))
        wgt = np.ascontiguousarray(np.concatenate([wgT[:, i * 128:(i + 1) * 128] for i in git], axis=1))
        wut = np.ascontiguousarray(np.concatenate([wuT[:, i * 128:(i + 1) * 128] for i in git], axis=1))
        wdn = np.ascontiguousarray(wd[c * 128:(c + 1) * 128, :])
        wdt = np.ascontiguousarray(wdT[:, c * 128:(c + 1) * 128])
        in_maps.append({
            "xnat": sh, "xtf": xtf, "xfirst": xfirst, "ka": ka, "ka2": ka2,
            "lxr": lxr, "clipv": clipv, "cosm": cosm, "wel": wel, "bel": bel,
            "wle": wle, "wleb": wleb, "mask24": mask24, "hmt": hmt, "ident": ident,
            "wgn": wgn, "wun": wun, "wdn": wdn, "wgt": wgt, "wut": wut, "wdt": wdt,
        })
    return in_maps


def kernel(**inputs):
    from concourse.bass_utils import run_bass_kernel_spmd
    in_maps = prep_inputs(inputs)
    nc = build_nc(stage=3, dbg=False)
    res = run_bass_kernel_spmd(nc, in_maps, core_ids=list(range(NC)))
    out = np.concatenate([np.asarray(res.results[c]["out"]) for c in range(NC)], axis=0)
    return np.ascontiguousarray(out.reshape(2, 8192, E)).astype(np.float32, copy=False)


# version 18 (final)
# speedup vs baseline: 1.0513x; 1.0301x over previous
import numpy as np
import concourse.bass as bass
import concourse.bass_isa as bass_isa
import concourse.mybir as mybir
from concourse.ap import AP
from contextlib import ExitStack

F32 = mybir.dt.float32
BF16 = mybir.dt.bfloat16
I32 = mybir.dt.int32
Alu = mybir.AluOpType
Act = mybir.ActivationFunctionType
AX = mybir.AxisListType.X

NC = 8
E = 1024
IF = 2816
TPC = 2048
NT = TPC // 128
NCH = TPC // 512
NIT = IF // 128
SUP = [list(range(0, 11)), list(range(11, 22))]
GTPR = 3
PI = float(np.pi)
LX = np.log([2., 4., 8., 16., 32.])
LXS = float(LX.sum())
DEN = float(5.0 * (LX ** 2).sum() - LXS * LXS)
ENGS = ("sync", "gpsimd", "vector", "scalar", "tensor")


class Sched:
    def __init__(self):
        self.stream = {e: [] for e in ENGS}
        self.count = {}
        self.channels = set()

    def op(self, eng, fn, waits=(), track=False, dma=False, ndma=1, ch=None):
        ch = ch or eng
        self.channels.add(ch)
        amt = (16 * ndma) if dma else 1
        ev = None
        if track:
            self.count[ch] = self.count.get(ch, 0) + amt
            ev = (ch, self.count[ch])
        self.stream[eng].append(
            (tuple(w for w in waits if w is not None), fn, track, dma, ndma, ch))
        return ev

    def emit(self, block, sems):
        for ename in ENGS:
            stream = self.stream[ename]
            def body(engine, stream=stream):
                last = {}
                for waits, fn, track, dma, ndma, ch in stream:
                    for (src, val) in waits:
                        if last.get(src, -1) >= val:
                            continue
                        engine.wait_ge(sems[src], val)
                        last[src] = val
                    ins = fn(engine)
                    if dma:
                        lst = ins if isinstance(ins, (list, tuple)) else [ins]
                        assert len(lst) == ndma, (len(lst), ndma)
                        for i in lst:
                            i.then_inc(sems[ch], 16)
                    elif track:
                        ins.then_inc(sems[ch], 1)
            getattr(block, ename)(body)


def build_nc(stage=3, dbg=False):
    nc = bass.Bass("TRN2", target_bir_lowering=False)
    dt = nc.dram_tensor
    d_xnat = dt("xnat", [TPC, E], F32, kind="ExternalInput")
    d_xtf = dt("xtf", [E, TPC], F32, kind="ExternalInput")
    d_xfirst = dt("xfirst", [1, E], F32, kind="ExternalInput")
    d_ka = dt("ka", [1, E], F32, kind="ExternalInput")
    d_ka2 = dt("ka2", [1, E], F32, kind="ExternalInput")
    d_lx = dt("lxr", [1, 8], F32, kind="ExternalInput")
    d_clip = dt("clipv", [1, 2], F32, kind="ExternalInput")
    d_cos = dt("cosm", [E, E], F32, kind="ExternalInput")
    d_wel = dt("wel", [E, 24], F32, kind="ExternalInput")
    d_bel = dt("bel", [1, 24], F32, kind="ExternalInput")
    d_wle = dt("wle", [24, E], F32, kind="ExternalInput")
    d_wleb = dt("wleb", [1, E], F32, kind="ExternalInput")
    d_mask24 = dt("mask24", [1, 24], F32, kind="ExternalInput")
    d_hmt = dt("hmt", [24, 12], BF16, kind="ExternalInput")
    d_id = dt("ident", [128, 128], BF16, kind="ExternalInput")
    d_wgn = dt("wgn", [GTPR * 128, E], F32, kind="ExternalInput")
    d_wun = dt("wun", [GTPR * 128, E], F32, kind="ExternalInput")
    d_wdn = dt("wdn", [128, IF], F32, kind="ExternalInput")
    d_wgt = dt("wgt", [E, GTPR * 128], F32, kind="ExternalInput")
    d_wut = dt("wut", [E, GTPR * 128], F32, kind="ExternalInput")
    d_wdt = dt("wdt", [IF, E], F32, kind="ExternalInput")
    d_out = dt("out", [TPC, E], F32, kind="ExternalOutput")
    if dbg:
        d_dbg_hr = dt("dbg_hr", [1, 2048], F32, kind="ExternalOutput")
        d_dbg_z = dt("dbg_z", [TPC, E], F32, kind="ExternalOutput")
        d_dbg_m = dt("dbg_m", [1, 16], F32, kind="ExternalOutput")
        d_dbg_qd = dt("dbg_qd", [128, 2048], F32, kind="ExternalOutput")
        d_dbg_w2 = dt("dbg_w2", [128, 192], BF16, kind="ExternalOutput")
        d_dbg_r = dt("dbg_r", [8, 1024], F32, kind="ExternalOutput")

    d_hrb = dt("hrb", [2048], BF16, kind="Internal")
    d_frb = dt("frb", [1024], F32, kind="Internal")
    d_rowb = dt("rowb", [4096], F32, kind="Internal")
    d_spill = dt("spill", [NIT, 128, TPC], BF16, kind="Internal")
    d_sc_in = dt("sc_in", [128, 16], F32, kind="Internal")
    d_sc_out = dt("sc_out", [NC, 128, 16], F32, kind="Internal", addr_space="Shared")
    d_wq_in = dt("wq_in", [48, 128, 128], BF16, kind="Internal")
    d_wq_out = dt("wq_out", [NC, 48, 128, 128], BF16, kind="Internal", addr_space="Shared")
    d_m1_in = dt("m1_in", [1, 128], F32, kind="Internal")
    d_m1_out = dt("m1_out", [1, 128], F32, kind="Internal")
    d_m2_in = dt("m2_in", [1, 128], F32, kind="Internal")
    d_m2_out = dt("m2_out", [1, 128], F32, kind="Internal")
    RG = [list(range(NC))]

    with ExitStack() as stk:
        def sb(name, shape, dtt):
            return stk.enter_context(nc.sbuf_tensor(name, shape, dtt))
        def psum(name, shape, dtt):
            return stk.enter_context(nc.psum_tensor(name, shape, dtt))

        s_z = sb("s_z", [128, NT * E], BF16)
        s_zqt = sb("s_zqt", [128, 8 * TPC], BF16)
        s_wq = sb("s_wq", [128, 22528], BF16)
        s_xn = sb("s_xn", [128, 2 * E], F32)
        s_xtf = sb("s_xtf", [128, 2 * 8 * 512], BF16)
        s_hkw = sb("s_hkw", [128, 8 * 512], BF16)
        s_y = sb("s_y", [128, E], F32)
        s_ysq = sb("s_ysq", [128, E], F32)
        s_sqs = sb("s_sqs", [128, E], F32)
        s_i32 = sb("s_i32", [128, E], I32)
        s_cosb = sb("s_cosb", [128, 512], F32)
        s_hf = sb("s_hf", [128, 2 * 512], F32)
        s_hrw = sb("s_hrw", [128, 2 * 512], BF16)
        s_hbuf = sb("s_hbuf", [128, IF], BF16)
        s_outb = sb("s_outb", [128, E], F32)
        s_wel2 = sb("s_wel2", [128, 8 * 24], BF16)
        s_wel = sb("s_wel", [128, 8 * 24], BF16)
        s_wle = sb("s_wle", [24, E], BF16)
        s_mask24 = sb("s_mask24", [1, 24], F32)
        s_wleb = sb("s_wleb", [1, E], BF16)
        s_bel = sb("s_bel", [1, 24], BF16)
        s_hmt = sb("s_hmt", [24, 12], BF16)
        s_id = sb("s_id", [128, 128], BF16)
        s_one5 = sb("s_one5", [1, 512], BF16)
        s_onef = sb("s_onef", [1, 128], F32)
        s_onek = sb("s_onek", [12, 1], BF16)
        s_corr = sb("s_corr", [24, 2 * 512], BF16)
        s_bits = sb("s_bits", [24, 2 * 512], BF16)
        s_syn = sb("s_syn", [24, 512], BF16)
        s_frc = sb("s_frc", [128, 8], F32)
        s_clip = sb("s_clip", [1, 2], F32)
        s_wn = sb("s_wn", [128, E], F32)
        s_wtr = sb("s_wtr", [128, GTPR * 128], F32)
        s_rmx = sb("s_rmx", [128, 8], F32)
        s_inv = sb("s_inv", [128, 8], F32)
        s_sc7 = sb("s_sc7", [128, 8], F32)
        s_scall = sb("s_scall", [128, NC * 16], F32)
        s_sgf = sb("s_sgf", [128, 24], F32)
        s_suf = sb("s_suf", [128, 24], F32)
        s_qsc = sb("s_qsc", [128, 24], F32)
        s_qscf = sb("s_qscf", [128, IF], F32)
        s_sdp = sb("s_sdp", [128, 16], F32)
        s_sdx = sb("s_sdx", [128, E], F32)
        s_m1a = sb("s_m1a", [128, 1], F32)
        s_m2a = sb("s_m2a", [128, 1], F32)
        s_m8 = sb("s_m8", [1, 8], F32)
        s_mg = sb("s_mg", [1, 16], F32)
        s_col = sb("s_col", [128, 8], F32)
        s_lnc = sb("s_lnc", [128, 16], F32)

        s_xf1 = s_sdx[0:1, :]
        s_ka = s_qscf[0:1, 896:1920]
        s_ka2 = s_sdx[0:1, :]
        s_pp = s_hf[0:24, 0:512]
        s_t12 = s_hf[0:24, 512:1024]
        pass
        s_flt = s_ysq[0:1, :]
        s_fr = s_y[0:1, :]
        s_hr = s_outb[0:1, 0:E]
        s_invf = s_qscf[:, 0:896]
        s_invr = s_qscf[0:1, 0:896]
        s_sdr = s_wn[0:1, 0:1024]
        s_dinvr = s_qscf[0:1, 1024:2048]
        s_dinvf = s_ysq
        s_fix = s_ysq[0:1, 0:512]
        s_fixa = s_ysq[0:1, 512:1024]
        s_scb = s_lnc[0:1, 0:8]
        s_sc = s_qsc[0:1, 0:16]
        s_cnt = s_sgf[0:1, 0:8]
        s_lxr = s_suf[0:1, 0:8]
        s_qht = s_xtf[:, 0:2 * IF]
        s_qst = s_hrw[:, 0:GTPR * 128]

        p_x1 = [psum("p_x1a", [128, E], F32), psum("p_x1b", [128, E], F32)]
        p_s = psum("p_s", [24, 512], F32)
        p_t = psum("p_t", [128, E], BF16)
        p_m = psum("p_m", [128, 512], F32)
        p_m2 = psum("p_m2", [128, 512], F32)
        s_flt2 = p_x1[0][0:1, :]

        blk_holder = []
        S = Sched()
        OP = S.op

        def vchain(steps, waits=(), eng="vector"):
            ev = None
            first = True
            for fn in steps:
                w = list(waits) + ([ev] if ev else []) if first else [ev]
                ev = OP(eng, fn, waits=w, track=True)
                first = False
            return ev

        ev_ld = {}
        def load(name, dst, srcap, eng="sync", waits=()):
            ev_ld[name] = OP(eng, lambda e, d=dst, s=srcap: e.dma_start(out=d, in_=s),
                             waits=waits, track=True, dma=True,
                             ch="sld" if eng == "sync" else "gld")
        load("xf1", s_xf1[:, :], d_xfirst.ap())
        load("ka", s_ka[:, :], d_ka.ap())
        load("lxr", s_lxr[:, :], d_lx.ap())
        load("clip", s_clip[:, :], d_clip.ap())
        load("id", s_id[:, :], d_id.ap())
        load("hmt", s_hmt[:, :], d_hmt.ap())
        load("wel", s_wel[:, :], AP(d_wel, 0, [[24, 128], [3072, 8], [1, 24]]), "gpsimd")
        load("bel", s_bel[:, :], d_bel.ap(), "gpsimd")
        load("wle", s_wle[:, :], d_wle.ap(), "gpsimd")
        load("mask24", s_mask24[:, :], d_mask24.ap())
        load("wleb", s_wleb[:, :], d_wleb.ap(), "gpsimd")
        ev_sld7 = ("sld", S.count["sld"])
        ev_gld4 = ("gld", S.count["gld"])
        for _n in ("xf1", "ka", "lxr", "clip", "id", "hmt", "mask24"):
            ev_ld[_n] = ev_sld7
        for _n in ("wel", "bel", "wle", "wleb"):
            ev_ld[_n] = ev_gld4
        ev_ms = OP("vector", lambda v: (
            v.memset(s_one5[:, :], 1.0), v.memset(s_onef[:, :], 1.0),
            v.memset(s_onek[:, :], 1.0), v.memset(s_syn[:, :], 0.0),
            v.memset(s_m1a[:, :], 0.0), v.memset(s_col[:, 4:5], 1e-5),
            v.memset(s_m2a[:, :], 0.0), v.memset(s_m8[:, :], 0.0))[-1], track=True)
        ev_xn = {}
        ev_xc = {}
        for t in range(2):
            ev_xn[t] = OP("sync", lambda e, t=t: e.dma_start(
                out=s_xn[:, (t % 2) * E:(t % 2) * E + E],
                in_=d_xnat.ap()[t * 128:(t + 1) * 128, :]), track=True, dma=True,
                ch="xna" if t % 2 == 0 else "xnb")
        def xtf_buf(c):
            if c == 2:
                return s_z, 12 * E
            if c == 3:
                return s_xtf, 0
            return s_xtf, (c % 2) * 4096
        def _mk_xc(c):
            buf, xb = xtf_buf(c)
            def _xc(g, buf=buf, xb=xb):
                return g.dma_start(
                    out=buf[:, xb:xb + 4096].rearrange("p (q c2) -> p q c2", q=8),
                    in_=d_xtf.ap()[:, c * 512:(c + 1) * 512].rearrange(
                        "(q p) c2 -> q p c2", q=8).transpose([1, 0, 2]))
            return _xc
        for c in range(2):
            ev_xc[c] = OP("gpsimd", _mk_xc(c), track=True, dma=True, ndma=1,
                          ch=("xca", "xcb")[c])
        for wi in range(64):
            OP("tensor", lambda t: t.matmul(
                p_m[:, 0:512], lhsT=s_id[:, :], rhs=s_xtf[:, 0:512],
                start=True, stop=True),
               waits=[ev_ld["id"], ev_xc[0]] if wi == 0 else ())

        ev_q_ag = ev_sc_ag = ev_invf = None
        ev_qw_last = None
        ev_push_all = []
        ev_dbgfr = None
        ev_cntl = None
        for j, n_ in enumerate([4, 16, 64, 256, 1024]):
            evs = OP("vector", lambda v, n=n_: v.tensor_reduce(
                s_sc[0:1, 0:1], s_xf1[0:1, 0:n], AX, Alu.add),
                waits=[ev_ld["xf1"]], track=True)
            evm = OP("vector", lambda v, n=n_: v.tensor_scalar(
                out=s_sc[0:1, 1:2], in0=s_sc[0:1, 0:1], scalar1=1.0 / n,
                scalar2=None, op0=Alu.mult), waits=[evs], track=True)
            evg = OP("vector", lambda v, n=n_: v.tensor_scalar(
                out=s_flt[0:1, 0:n], in0=s_xf1[0:1, 0:n],
                scalar1=s_sc[0:1, 1:2], scalar2=None, op0=Alu.is_gt),
                waits=[evm], track=True)
            evc = OP("vector", lambda v, n=n_: v.tensor_reduce(
                s_sc[0:1, 2:3], s_flt[0:1, 0:n], AX, Alu.add),
                waits=[evg], track=True)
            ev_cntl = OP("vector", lambda v, jj=j: v.tensor_scalar(
                out=s_cnt[0:1, jj:jj + 1], in0=s_sc[0:1, 2:3],
                scalar1=1.0, scalar2=None, op0=Alu.max), waits=[evc], track=True)
        ev_lncnt = OP("scalar", lambda s: s.activation(out=s_cnt[0:1, 0:5],
                                                       in_=s_cnt[0:1, 0:5], func=Act.Ln),
                      waits=[ev_cntl], track=True)
        load("ka2", s_ka2[:, :], d_ka2.ap(), waits=[ev_cntl])
        ev_dsa = vchain([
            lambda v: v.tensor_reduce(s_sc[0:1, 3:4], s_cnt[0:1, 0:5], AX, Alu.add),
            lambda v: v.tensor_tensor(out=s_scb[0:1, 0:5], in0=s_cnt[0:1, 0:5],
                                      in1=s_lxr[0:1, 0:5], op=Alu.mult),
            lambda v: v.tensor_reduce(s_sc[0:1, 4:5], s_scb[0:1, 0:5], AX, Alu.add),
        ], waits=[ev_lncnt, ev_ld["lxr"]])
        ev_dsc = vchain([
            lambda v: v.tensor_scalar(out=s_sc[0:1, 5:6], in0=s_sc[0:1, 3:4], scalar1=LXS,
                                      scalar2=None, op0=Alu.mult),
            lambda v: v.tensor_scalar(out=s_sc[0:1, 6:7], in0=s_sc[0:1, 4:5], scalar1=5.0,
                                      scalar2=None, op0=Alu.mult),
            lambda v: v.tensor_tensor(out=s_sc[0:1, 7:8], in0=s_sc[0:1, 5:6],
                                      in1=s_sc[0:1, 6:7], op=Alu.subtract),
            lambda v: v.tensor_scalar(out=s_sc[0:1, 8:9], in0=s_sc[0:1, 7:8],
                                      scalar1=1.0 / DEN, scalar2=None, op0=Alu.mult),
            lambda v: v.tensor_scalar(out=s_sc[0:1, 9:10], in0=s_sc[0:1, 8:9], scalar1=1.0,
                                      scalar2=2.0, op0=Alu.max, op1=Alu.min),
            lambda v: v.tensor_scalar(out=s_sc[0:1, 10:11], in0=s_sc[0:1, 9:10],
                                      scalar1=0.2, scalar2=1.0, op0=Alu.mult, op1=Alu.add),
            lambda v: v.tensor_tensor(out=s_sc[0:1, 11:12], in0=s_sc[0:1, 10:11],
                                      in1=s_sc[0:1, 9:10], op=Alu.mult),
            lambda v: v.tensor_scalar(out=s_sc[0:1, 12:13], in0=s_sc[0:1, 9:10],
                                      scalar1=0.5, scalar2=1.0, op0=Alu.mult, op1=Alu.add),
            lambda v: v.tensor_scalar(out=s_sc[0:1, 13:14], in0=s_sc[0:1, 9:10],
                                      scalar1=0.1, scalar2=None, op0=Alu.mult),
        ], waits=[ev_dsa])
        def _arg1(v):
            v.tensor_scalar(out=s_flt[0:1, :], in0=s_ka[0:1, :], scalar1=s_sc[0:1, 12:13],
                            scalar2=s_sc[0:1, 11:12], op0=Alu.mult, op1=Alu.add)
            v.tensor_scalar(out=s_flt2[0:1, :], in0=s_flt[0:1, :], scalar1=PI,
                            scalar2=None, op0=Alu.is_gt)
            return v.scalar_tensor_tensor(out=s_flt[0:1, :], in0=s_flt2[0:1, :],
                                          scalar=-2.0 * PI, in1=s_flt[0:1, :],
                                          op0=Alu.mult, op1=Alu.add)
        ev_arg1 = OP("vector", _arg1, waits=[ev_dsc, ev_ld["ka"]], track=True)
        ev_sin1 = OP("scalar", lambda s: s.activation(out=s_fr[0:1, :], in_=s_flt[0:1, :],
                                                      func=Act.Sin),
                     waits=[ev_arg1], track=True)
        def _arg2(v):
            v.tensor_scalar(out=s_flt[0:1, :], in0=s_ka2[0:1, :], scalar1=s_sc[0:1, 13:14],
                            scalar2=s_sc[0:1, 11:12], op0=Alu.mult, op1=Alu.add)
            v.tensor_tensor(out=s_flt[0:1, :], in0=s_flt[0:1, :], in1=s_ka[0:1, :],
                            op=Alu.subtract)
            v.tensor_scalar(out=s_flt[0:1, :], in0=s_flt[0:1, :], scalar1=PI / 2,
                            scalar2=None, op0=Alu.add)
            v.tensor_scalar(out=s_flt2[0:1, :], in0=s_flt[0:1, :], scalar1=PI,
                            scalar2=None, op0=Alu.is_gt)
            return v.scalar_tensor_tensor(out=s_flt[0:1, :], in0=s_flt2[0:1, :],
                                          scalar=-2.0 * PI, in1=s_flt[0:1, :],
                                          op0=Alu.mult, op1=Alu.add)
        ev_arg2 = OP("vector", _arg2, waits=[ev_sin1, ev_ld["ka2"]], track=True)
        ev_sin2 = OP("scalar", lambda s: s.activation(out=s_flt2[0:1, :], in_=s_flt[0:1, :],
                                                      func=Act.Sin),
                     waits=[ev_arg2], track=True)
        ev_fr = OP("vector", lambda v: (
            v.tensor_tensor(out=s_fr[0:1, :], in0=s_fr[0:1, :], in1=s_flt2[0:1, :],
                            op=Alu.mult),
            v.tensor_scalar(out=s_fr[0:1, :], in0=s_fr[0:1, :], scalar1=1.0 / 1024.0,
                            scalar2=None, op0=Alu.mult))[-1],
            waits=[ev_sin2], track=True)
        ev_frs = OP("sync", lambda g: g.dma_start(out=AP(d_frb, 0, [[1024, 1], [1, 1024]]),
                                                  in_=s_fr[0:1, :]),
                    waits=[ev_fr], track=True, dma=True, ch="hrch")
        ev_cosl = {}
        ev_cosl[0] = OP("sync", lambda e: e.dma_start(
            out=s_cosb[:, :], in_=d_cos.ap()[0:128, 0:512]), track=True, dma=True,
            ch="cosa")
        ev_frc = OP("sync", lambda e: e.dma_start(out=s_frc[:, :],
                                                  in_=AP(d_frb, 0, [[1, 128], [128, 8]]),
                                                  allow_slow_non_contiguous=True),
                    waits=[ev_frs], track=True, dma=True, ch="frc")
        if dbg:
            OP("sync", lambda e: e.dma_start(out=d_dbg_r.ap()[0:1, 0:16],
                                             in_=s_sc[0:1, 0:16]),
               waits=[ev_arg2], track=True, dma=True, ch="dbg")
            OP("sync", lambda e: e.dma_start(out=d_dbg_r.ap()[1:2, 0:8],
                                             in_=s_cnt[0:1, 0:8]),
               waits=[ev_arg2], track=True, dma=True, ch="dbg")
            ev_dbgfr = OP("sync", lambda e: e.dma_start(out=d_dbg_r.ap()[2:3, :],
                                                        in_=s_fr[0:1, :]),
                          waits=[ev_frs], track=True, dma=True, ch="dbg")
        if stage >= 3:
            ev_red = {}
            _wb = [s_wn, s_sqs, s_outb]
            for i in range(9):
                m_, ii = divmod(i, 3)
                w = 768 if i == 8 else 1024
                buf = _wb[i % 3]
                def _ld(e, m_=m_, ii=ii, w=w, buf=buf):
                    if m_ == 0:
                        return e.dma_start(out=buf[:, 0:w],
                                           in_=d_wgn.ap()[ii * 128:(ii + 1) * 128, :])
                    if m_ == 1:
                        return e.dma_start(out=buf[:, 0:w],
                                           in_=d_wun.ap()[ii * 128:(ii + 1) * 128, :])
                    return e.dma_start(out=buf[:, 0:w],
                                       in_=d_wdn.ap()[:, ii * 1024:ii * 1024 + w])
                evl = OP("gpsimd", _ld, waits=[ev_red.get(i - 3)], track=True,
                         dma=True, ch="gw" + "abc"[i % 3])
                dcol = i if m_ < 2 else 7
                evr1 = OP("vector", lambda v, w=w, buf=buf, dcol=dcol: v.tensor_reduce(
                    s_rmx[:, dcol:dcol + 1], buf[:, 0:w], AX, Alu.max,
                    apply_absolute_value=True), waits=[evl], track=True)
                if m_ < 2:
                    ev_red[i] = evr1
                else:
                    def _rd(v, ii=ii):
                        if ii == 0:
                            return v.tensor_copy(out=s_rmx[:, 6:7], in_=s_rmx[:, 7:8])
                        return v.tensor_tensor(out=s_rmx[:, 6:7], in0=s_rmx[:, 6:7],
                                               in1=s_rmx[:, 7:8], op=Alu.max)
                    ev_red[i] = OP("vector", _rd, waits=[evr1], track=True)
            ev_inv = vchain([
                lambda v: v.reciprocal(out=s_inv[:, 0:7], in_=s_rmx[:, 0:7]),
                lambda v: v.tensor_scalar(out=s_inv[:, 0:7], in0=s_inv[:, 0:7],
                                          scalar1=127.0, scalar2=None, op0=Alu.mult),
                lambda v: v.tensor_scalar(out=s_sc7[:, 0:7], in0=s_rmx[:, 0:7],
                                          scalar1=1.0 / 127.0, scalar2=None, op0=Alu.mult),
            ], waits=[ev_red[i] for i in range(9)])
            ev_scs = OP("gpsimd", lambda g: [
                g.dma_start(out=AP(d_sc_in, 0, [[16, 128], [1, 8]]), in_=s_sc7[:, :]),
                g.dma_start(out=AP(d_sc_in, 8, [[16, 128], [1, 8]]), in_=s_inv[:, :])],
                waits=[ev_inv], track=True, dma=True, ndma=2, ch="grt")
            ev_sc_ag = OP("gpsimd", lambda g: g.collective_compute(
                "AllGather", Alu.bypass, replica_groups=RG,
                ins=[d_sc_in.ap().opt()], outs=[d_sc_out.ap().opt()]),
                waits=[ev_scs], track=True, ch="cc")
            ev_invs = OP("gpsimd", lambda g: g.dma_start(
                out=AP(d_rowb, 0, [[1, 128], [128, 7]]), in_=s_inv[:, 0:7],
                allow_slow_non_contiguous=True),
                waits=[ev_inv], track=True, dma=True, ch="grt")

        ev_hrmm = {}
        _cbufs = [s_cosb[:, :], s_y[:, 0:512], s_hf[:, 0:512], s_hf[:, 512:1024]]
        for idx in range(16):
            kc, nk = divmod(idx, 2)
            buf = _cbufs[idx % 4]
            if idx >= 1:
                ev_cosl[idx] = OP("sync" if idx % 2 == 0 else "scalar",
                    lambda e, kc=kc, nk=nk, buf=buf: e.dma_start(
                    out=buf,
                    in_=d_cos.ap()[kc * 128:(kc + 1) * 128, nk * 512:(nk + 1) * 512]),
                    waits=[ev_hrmm.get(idx - 4)] +
                          ([ev_frs, ev_cosl[0], ev_dbgfr] if idx == 1 else []),
                    track=True, dma=True, ch="cos" + "abcd"[idx % 4])
            pdst = p_m if nk == 0 else p_m2
            ev_hrmm[idx] = OP("tensor", lambda t, kc=kc, pd=pdst, buf=buf: t.matmul(
                pd[0:1, 0:512], lhsT=s_frc[:, kc:kc + 1], rhs=buf,
                start=(kc == 0), stop=(kc == 7)),
                waits=[ev_cosl[idx], ev_frc], track=True)
        ev_hrev = OP("vector", lambda v: (
            v.tensor_copy(out=s_hr[0:1, 0:512], in_=p_m[0:1, 0:512]),
            v.tensor_copy(out=s_hr[0:1, 512:1024], in_=p_m2[0:1, 0:512]))[-1],
            waits=[ev_hrmm[14], ev_hrmm[15]], track=True)
        ev_hrs = OP("gpsimd", lambda g: [
            g.dma_start(out=AP(d_hrb, 128, [[1024, 1], [1, 1024]]), in_=s_hr[0:1, :]),
            g.dma_start(out=AP(d_hrb, 0, [[128, 1], [1, 128]]), in_=s_hr[0:1, 896:1024]),
            g.dma_start(out=AP(d_hrb, 1152, [[896, 1], [1, 896]]), in_=s_hr[0:1, 0:896])],
            waits=[ev_hrev], track=True, dma=True, ndma=3, ch="hrch")
        ev_hkw = OP("scalar", lambda g: g.dma_start(
            out=s_hkw[:, :].rearrange("p (d c) -> p d c", d=8),
            in_=AP(d_hrb, 1, [[1, 128], [128, 8], [1, 512]])),
            waits=[ev_hrs], track=True, dma=True, ndma=1, ch="hrch")
        if stage >= 3:
            ev_invr = OP("sync", lambda e: e.dma_start(
                out=s_invr[0:1, :], in_=AP(d_rowb, 0, [[896, 1], [1, 896]])),
                waits=[ev_invs], track=True, dma=True, ch="invr")
            ev_xc[2] = OP("gpsimd", _mk_xc(2), waits=[ev_inv], track=True,
                          dma=True, ndma=1, ch="xcc")
            ev_bcq = {}
            for ch in range(2):
                w = 512 if ch == 0 else 384
                evb = OP("tensor", lambda t, ch=ch, w=w: t.matmul(
                    p_x1[0][:, ch * 512:ch * 512 + w], lhsT=s_onef[:, :],
                    rhs=s_invr[0:1, ch * 512:ch * 512 + w],
                    start=True, stop=True),
                    waits=[ev_invr, ev_ms], track=True)
                ev_bcq[ch] = OP("vector", lambda v, ch=ch, w=w: v.tensor_copy(
                    out=s_invf[:, ch * 512:ch * 512 + w],
                    in_=p_x1[0][:, ch * 512:ch * 512 + w]),
                    waits=[evb], track=True)
            ev_invf = ev_bcq[1]
            wbufs = [s_wtr, s_sqs[:, 0:GTPR * 128], s_sqs[:, 512:896],
                     s_ysq[:, 0:GTPR * 128]]
            qbufs = [s_hrw[:, 0:GTPR * 128], s_hrw[:, 512:896],
                     s_zqt[:, 0:GTPR * 128], s_zqt[:, 512:896]]
            ev_qw = {}
            ev_push = {}
            gidx = 0
            for m in range(2):
                src = [d_wgt, d_wut][m]
                nht = 8
                wd_ = 384
                for ht in range(nht):
                    gi = gidx; gidx += 1
                    par = gi % 4
                    wbuf = wbufs[par]
                    qbuf = qbufs[par]
                    ib = (gi % 2) * 512
                    ev_wl = OP("sync", lambda e, src=src, ht=ht, w=wd_, wbuf=wbuf:
                               e.dma_start(out=wbuf[:, 0:w],
                                           in_=src.ap()[ht * 128:(ht + 1) * 128, :]),
                               waits=[ev_qw.get(gi - 4),
                                      ev_inv if gi == 1 else None],
                               track=True, dma=True, ch="wtr" + "abcd"[par])
                    ev_qw[gi] = OP("vector", lambda v, m=m, w=wd_, wbuf=wbuf,
                                   qbuf=qbuf, ib=ib: (
                        v.tensor_tensor(out=s_i32[:, ib:ib + w],
                                        in0=wbuf[:, 0:w],
                                        in1=s_invf[:, (0, 384, 768)[m]:(0, 384, 768)[m] + w],
                                        op=Alu.mult),
                        v.tensor_copy(out=qbuf[:, 0:w], in_=s_i32[:, ib:ib + w]))[-1],
                        waits=[ev_wl, ev_invf, ev_push.get(gi - 4)], track=True)
                    def _push(g, m=m, ht=ht, qbuf=qbuf):
                        return g.dma_start(
                            out=AP(d_wq_in, (m * 24 + ht) * 16384,
                                   [[128, 128], [8 * 16384, GTPR], [1, 128]]),
                            in_=qbuf[:, 0:GTPR * 128].rearrange(
                                "p (a c2) -> p a c2", a=GTPR))
                    ev_push[gi] = OP("gpsimd", _push, waits=[ev_qw[gi]], track=True,
                                     dma=True, ndma=1, ch="push" + "abcd"[par])
            ev_qw_last = ev_qw[gidx - 1]
            ev_push_all = [ev_push[gidx - 1], ev_push[gidx - 2],
                           ev_push[gidx - 3], ev_push[gidx - 4]]
            ev_q_ag = OP("gpsimd", lambda g: g.collective_compute(
                "AllGather", Alu.bypass, replica_groups=RG,
                ins=[d_wq_in.ap().opt()], outs=[d_wq_out.ap().opt()]),
                waits=ev_push_all, track=True, ch="cc")
        def _w2all(t):
            ins = None
            for Iq in range(8):
                for J in range(8):
                    d = (J - Iq) % 8
                    ins = t.matmul(p_m[:, Iq * 24:(Iq + 1) * 24],
                                   lhsT=s_hkw[:, d * 512:d * 512 + 128],
                                   rhs=s_wel[:, J * 24:(J + 1) * 24],
                                   start=(J == 0), stop=(J == 7))
            return ins
        evm = OP("tensor", _w2all, waits=[ev_hkw, ev_ld["wel"], ev_hrev], track=True)
        ev_w2_all = OP("scalar", lambda sc: sc.activation(
            out=s_wel2[:, :], in_=p_m[:, 0:192], func=Act.Copy),
            waits=[evm], track=True)
        ev_dbg_w2dma = None
        if dbg:
            OP("sync", lambda e: e.dma_start(out=d_dbg_r.ap()[3:4, :],
                                             in_=s_hr[0:1, :]),
               waits=[ev_hrs], track=True, dma=True, ch="dbg")
            evc = OP("vector", lambda v: v.tensor_copy(out=s_hbuf[:, 0:192],
                                                       in_=s_wel2[:, :]),
                     waits=[ev_w2_all], track=True)
            ev_dbg_w2dma = OP("sync", lambda e: e.dma_start(out=d_dbg_w2.ap(),
                                                           in_=s_hbuf[:, 0:192]),
                              waits=[evc], track=True, dma=True, ch="dbg")
            OP("gpsimd", lambda e: e.dma_start(out=d_dbg_hr.ap(),
                                               in_=AP(d_hrb, 0, [[2048, 1], [1, 2048]])),
               waits=[ev_hrs], track=True, dma=True, ch="dbg")





        ev_scl = OP("sync", lambda e: e.dma_start(
            out=s_scall[:, :], in_=AP(d_sc_out, 0, [[16, 128], [2048, 8], [1, 16]]),
            allow_slow_non_contiguous=True),
            waits=[ev_sc_ag], track=True, dma=True, ch="scl")
        ev_sdp = OP("vector", lambda v: [(
            v.tensor_copy(out=s_sdp[:, r:r + 1],
                          in_=s_scall[:, r * 16 + 6:r * 16 + 7]),
            v.tensor_copy(out=s_sdp[:, 8 + r:9 + r],
                          in_=s_scall[:, r * 16 + 14:r * 16 + 15]))[-1]
            for r in range(8)][-1], waits=[ev_scl], track=True)
        ev_sds = OP("gpsimd", lambda g: g.dma_start(
            out=AP(d_rowb, 256, [[1, 128], [128, 16]]), in_=s_sdp[:, :],
            allow_slow_non_contiguous=True),
            waits=[ev_sdp], track=True, dma=True, ch="grt")
        ev_sdr = OP("sync", lambda e: [
            e.dma_start(out=s_sdr[0:1, :],
                        in_=AP(d_rowb, 256, [[1024, 1], [1, 1024]])),
            e.dma_start(out=s_dinvr[0:1, :],
                        in_=AP(d_rowb, 256 + 1024, [[1024, 1], [1, 1024]]))],
            waits=[ev_sds, ev_hrs], track=True, dma=True, ndma=2, ch="sdr")
        ev_y = {}
        ev_sq = {}
        ev_lnz = {}
        ev_ztr = {}
        ev_zev = {}
        ev_x1d = {}
        ev_dbgz = {}
        ev_corr = {}
        ev_gst = {}
        def emit_chain_step(c, step):
            xb = (c % 2) * 4096
            cb = (c % 2) * 512
            if step == 0:
                emit_chain_s0(c, xb, cb)
            elif step == 1:
                emit_chain_s1(c, cb)
            elif step == 2:
                emit_chain_s2(c)
            else:
                emit_chain_s3(c, cb)
        def emit_chain(c):
            for st_ in range(4):
                emit_chain_step(c, st_)
        def emit_chain_s0(c, xb, cb):
            xbuf, xb = xtf_buf(c)
            def _lat(t, c=c, xb=xb, xbuf=xbuf):
                ins = None
                for Iq in range(8):
                    ins = t.matmul(p_s[:, :], lhsT=s_wel2[:, Iq * 24:(Iq + 1) * 24],
                                   rhs=xbuf[:, xb + Iq * 512:xb + Iq * 512 + 512],
                                   start=(Iq == 0), stop=False)
                return t.matmul(p_s[:, :], lhsT=s_bel[:, :], rhs=s_one5[:, :],
                                start=False, stop=True)
            evlat = OP("tensor", _lat,
                       waits=[ev_xc[c], ev_w2_all, ev_ld["bel"], ev_ms, ev_invf,
                              *ev_push_all,
                              ev_corr.get(c - 1) if c >= 1 else None],
                       track=True)
            ev_gst[(c, "bits")] = OP("vector", lambda v, cb=cb: v.tensor_scalar(
                out=s_bits[:, cb:cb + 512], in0=p_s[:, :], scalar1=0.0, scalar2=None,
                op0=Alu.is_gt), waits=[evlat], track=True)
        def emit_chain_s1(c, cb):
            evsyn = OP("tensor", lambda t, cb=cb: t.matmul(
                p_s[0:12, :], lhsT=s_hmt[:, :], rhs=s_bits[:, cb:cb + 512],
                start=True, stop=True),
                waits=[ev_gst[(c, "bits")], ev_ld["hmt"]], track=True)
            ev_gst[(c, "synb")] = OP("vector", lambda v: v.tensor_scalar(
                out=s_syn[0:12, :], in0=p_s[0:12, :], scalar1=0.5, scalar2=None,
                op0=Alu.is_gt), waits=[evsyn], track=True)
        def emit_chain_s2(c):
            evw = OP("tensor", lambda t: t.matmul(p_s[0:1, :], lhsT=s_onek[:, :],
                                                  rhs=s_syn[0:12, :], start=True, stop=True),
                     waits=[ev_gst[(c, "synb")]], track=True)
            ev_gst[(c, "fix")] = OP("vector", lambda v: (
                v.tensor_scalar(out=s_fixa[0:1, :], in0=p_s[0:1, :], scalar1=3.5,
                                scalar2=None, op0=Alu.is_lt),
                v.scalar_tensor_tensor(out=s_fix[0:1, :], in0=p_s[0:1, :], scalar=0.5,
                                       op0=Alu.is_gt, in1=s_fixa[0:1, :], op1=Alu.mult))[-1],
                waits=[evw], track=True)
        def emit_chain_s3(c, cb):
            evfb = OP("tensor", lambda t: t.matmul(p_s[0:24, :], lhsT=s_mask24[:, :],
                                                   rhs=s_fix[0:1, :], start=True, stop=True),
                      waits=[ev_gst[(c, "fix")], ev_ld["mask24"]], track=True)
            ev_corr[c] = OP("vector", lambda v, cb=cb: (
                v.tensor_tensor(out=s_pp[:, :], in0=s_syn[:, :], in1=p_s[0:24, :],
                                op=Alu.mult),
                v.tensor_scalar(out=s_t12[:, :], in0=s_bits[:, cb:cb + 512], scalar1=-2.0,
                                scalar2=1.0, op0=Alu.mult, op1=Alu.add),
                v.tensor_tensor(out=s_pp[:, :], in0=s_pp[:, :], in1=s_t12[:, :],
                                op=Alu.mult),
                v.tensor_tensor(out=s_corr[:, cb:cb + 512], in0=s_bits[:, cb:cb + 512],
                                in1=s_pp[:, :], op=Alu.add))[-1],
                waits=[evfb, ev_x1d.get((c - 2, 3))], track=True)

        def emit_head(t):
            c, sub = divmod(t, 4)
            xbuf, xb = xtf_buf(c)
            cb = (c % 2) * 512
            par = t % 2
            lb = (t % 2) * 8
            def _x1(te, c=c, sub=sub, par=par, xb=xb, cb=cb, xbuf=xbuf):
                ins = None
                for Jc in range(2):
                    for Iq in range(8):
                        d0 = (4 * Jc - Iq) % 8
                        ins = te.matmul(
                            p_x1[par][:, Jc * 512:(Jc + 1) * 512],
                            lhsT=xbuf[:, xb + Iq * 512 + sub * 128:
                                      xb + Iq * 512 + sub * 128 + 128],
                            rhs=s_hkw[:, d0 * 512:(d0 + 1) * 512],
                            start=(Iq == 0), stop=False)
                    te.matmul(
                        p_x1[par][:, Jc * 512:(Jc + 1) * 512],
                        lhsT=s_corr[:, cb + sub * 128:cb + sub * 128 + 128],
                        rhs=s_wle[:, Jc * 512:(Jc + 1) * 512],
                        start=False, stop=False)
                    ins = te.matmul(
                        p_x1[par][:, Jc * 512:(Jc + 1) * 512],
                        lhsT=s_one5[0:1, 0:128],
                        rhs=s_wleb[0:1, Jc * 512:(Jc + 1) * 512],
                        start=False, stop=True)
                return ins
            ev_x1d[(c, sub)] = OP("tensor", _x1,
                                  waits=[ev_corr[c], ev_hkw, ev_ld["wle"],
                                         ev_ld["wleb"], ev_y.get(t - 2)], track=True)
            yb = s_y if t % 2 == 0 else s_sdx
            ev_y[t] = OP("vector", lambda v, t=t, par=par, lb=lb, yb=yb:
                         v.scalar_tensor_tensor(
                out=yb[:, :], in0=p_x1[par][:, :], scalar=1.0, op0=Alu.mult,
                in1=s_xn[:, (t % 2) * E:(t % 2) * E + E], op1=Alu.add,
                accum_out=s_lnc[:, lb:lb + 1]),
                waits=[ev_x1d[(c, sub)], ev_xn[t], ev_lnz.get(t - 2)], track=True)
            if t + 2 < NT:
                ev_xn[t + 2] = OP("scalar", lambda e, t2=t + 2: e.dma_start(
                    out=s_xn[:, (t2 % 2) * E:(t2 % 2) * E + E],
                    in_=d_xnat.ap()[t2 * 128:(t2 + 1) * 128, :]),
                    waits=[ev_y[t]], track=True, dma=True,
                    ch="xna" if t % 2 == 0 else "xnb")
            ev_sq[t] = OP("scalar", lambda s, lb=lb, yb=yb: s.activation(
                out=s_sqs[:, :], in_=yb[:, :], func=Act.Square,
                accum_out=s_lnc[:, lb + 1:lb + 2]),
                waits=[ev_y[t], ev_qw_last if t == 0 else None], track=True)

        ev_s6 = {}
        ev_za = {}
        def emit_sch_a(t):
            lb = (t % 2) * 8
            ev1 = OP("scalar", lambda s, lb=lb: s.activation(
                out=s_lnc[:, lb + 2:lb + 3], in_=s_lnc[:, lb:lb + 1],
                func=Act.Identity, scale=1.0 / E), waits=[ev_y[t]], track=True)
            ev2 = OP("scalar", lambda s, lb=lb: s.activation(
                out=s_lnc[:, lb + 3:lb + 4], in_=s_lnc[:, lb + 2:lb + 3],
                func=Act.Copy, scale=-1.0), waits=[ev1], track=True)
            ev3 = OP("scalar", lambda s, lb=lb: s.activation(
                out=s_lnc[:, lb + 4:lb + 5], in_=s_lnc[:, lb + 2:lb + 3],
                func=Act.Square), waits=[ev2], track=True)
            ev4 = OP("scalar", lambda s, lb=lb: s.activation(
                out=s_lnc[:, lb + 4:lb + 5], in_=s_lnc[:, lb + 4:lb + 5],
                func=Act.Copy, scale=-1.0), waits=[ev3], track=True)
            ev5 = OP("scalar", lambda s, lb=lb: s.activation(
                out=s_lnc[:, lb + 5:lb + 6], in_=s_lnc[:, lb + 1:lb + 2],
                func=Act.Identity, scale=1.0 / E, bias=s_lnc[:, lb + 4:lb + 5]),
                waits=[ev4, ev_sq[t]], track=True)
            ev_s6[t] = OP("scalar", lambda s, lb=lb: s.activation(
                out=s_lnc[:, lb + 5:lb + 6], in_=s_lnc[:, lb + 5:lb + 6],
                func=Act.Sqrt, bias=s_col[:, 4:5]), waits=[ev5], track=True)
        def emit_vtail(t):
            lb = (t % 2) * 8
            ev_za[t] = OP("vector", lambda v, lb=lb: v.reciprocal(
                out=s_lnc[:, lb + 6:lb + 7], in_=s_lnc[:, lb + 5:lb + 6]),
                waits=[ev_s6[t]], track=True)
        def emit_sch_b(t):
            lb = (t % 2) * 8
            yb = s_y if t % 2 == 0 else s_sdx
            ev7 = OP("scalar", lambda s, lb=lb: s.activation(
                out=s_lnc[:, lb + 7:lb + 8], in_=s_lnc[:, lb + 6:lb + 7],
                func=Act.Identity, scale=s_lnc[:, lb + 3:lb + 4]),
                waits=[ev_za[t]], track=True)
            ev_lnz[t] = OP("scalar", lambda s, t=t, lb=lb, yb=yb: s.activation(
                out=s_z[:, t * E:(t + 1) * E], in_=yb[:, :],
                func=Act.Identity, scale=s_lnc[:, lb + 6:lb + 7],
                bias=s_lnc[:, lb + 7:lb + 8]),
                waits=[ev7, ev_x1d[(2, 3)] if t == 12 else None], track=True)
        def emit_zrow(t):
            evzr = OP("vector", lambda v, t=t: v.tensor_reduce(
                s_col[:, 5 + t % 2:6 + t % 2], s_z[:, t * E:(t + 1) * E], AX,
                Alu.max, apply_absolute_value=True), waits=[ev_lnz[t]], track=True)
            return OP("vector", lambda v, t=t: v.tensor_tensor(
                out=s_m1a[:, :], in0=s_m1a[:, :],
                in1=s_col[:, 5 + t % 2:6 + t % 2],
                op=Alu.max), waits=[evzr], track=True)

        def emit_tail_t(t):
            ev_ztr[t] = OP("tensor", lambda te, t=t: [te.transpose(
                p_t[:, ht * 128:(ht + 1) * 128],
                s_z[:, t * E + ht * 128:t * E + ht * 128 + 128],
                s_id[:, :]) for ht in range(8)][-1],
                waits=[ev_lnz[t], ev_ld["id"], ev_zev.get(t - 1)], track=True)
            ev_zev[t] = OP("vector", lambda v, t=t: v.tensor_copy(
                out=s_zqt[:, :].rearrange("p (h c) -> p h c",
                                          h=8)[:, :, t * 128:(t + 1) * 128],
                in_=p_t[:, :].rearrange("p (h c) -> p h c", h=8)),
                waits=[ev_ztr[t]] + (ev_push_all if t == 0 else []), track=True)
            if dbg:
                evdz = OP("vector", lambda v, t=t: v.tensor_copy(
                    out=s_outb[:, :], in_=s_z[:, t * E:(t + 1) * E]),
                    waits=[ev_lnz[t], ev_dbgz.get(t - 1)], track=True)
                ev_dbgz[t] = OP("sync", lambda e, t=t: e.dma_start(
                    out=d_dbg_z.ap()[t * 128:(t + 1) * 128, :], in_=s_outb[:, :]),
                    waits=[evdz], track=True, dma=True, ch="dbg")

        emit_chain(0)
        ev_m1u = None
        for t in range(NT):
            c, sub = divmod(t, 4)
            emit_head(t)
            if c + 1 < NCH:
                emit_chain_step(c + 1, sub)
            if t >= 1:
                emit_vtail(t - 1)
                emit_sch_b(t - 1)
            emit_sch_a(t)
            if t >= 2:
                ev_m1u = emit_zrow(t - 2)
            if t >= 3:
                emit_tail_t(t - 3)
            if sub == 3 and c == 0:
                ev_xc[3] = OP("gpsimd", _mk_xc(3),
                              waits=[ev_x1d[(0, 3)]], track=True, dma=True,
                              ndma=1, ch="xca")
        emit_vtail(NT - 1)
        emit_sch_b(NT - 1)
        ev_m1u = emit_zrow(NT - 2)
        ev_m1u = emit_zrow(NT - 1)
        emit_tail_t(NT - 3)
        emit_tail_t(NT - 2)
        emit_tail_t(NT - 1)
        for wi in range(110):
            OP("tensor", lambda t: t.matmul(
                p_s[:, :], lhsT=s_id[:, 0:24], rhs=s_hkw[:, 0:512],
                start=True, stop=True),
               waits=[ev_corr[NCH - 1]] if wi == 0 else ())

        if stage < 3:
            ev_m1c = OP("gpsimd", lambda g: g.dma_start(out=AP(d_rowb, 0, [[1, 128]]),
                                                        in_=s_m1a[:, 0:1]),
                        waits=[ev_m1u, ev_ms], track=True, dma=True, ch="m1g")
            ev_m1r = OP("sync", lambda e: e.dma_start(
                out=s_y[0:1, 0:128], in_=AP(d_rowb, 0, [[128, 1], [1, 128]])),
                waits=[ev_m1c], track=True, dma=True, ch="m1r")
            ev_m1p = OP("vector", lambda v: v.tensor_reduce(
                s_m8[0:1, 0:1], s_y[0:1, 0:128], AX, Alu.max),
                waits=[ev_m1r], track=True)
            ev_od = {}
            for t in range(NT):
                evc = OP("vector", lambda v, t=t: v.tensor_copy(
                    out=s_outb[:, :], in_=s_z[:, t * E:(t + 1) * E]),
                    waits=[ev_lnz[t], ev_od.get(t - 1)], track=True)
                ev_od[t] = OP("sync", lambda e, t=t: e.dma_start(
                    out=d_out.ap()[t * 128:(t + 1) * 128, :], in_=s_outb[:, :]),
                    waits=[evc], track=True, dma=True, ch="out")
            if dbg:
                evm8 = OP("gpsimd", lambda g: g.dma_start(out=d_dbg_m.ap()[0:1, 0:8],
                                                          in_=s_m8[:, :]),
                          waits=[ev_m1p], track=True, dma=True, ch="dbg")
                OP("gpsimd", lambda g: g.engine_nop(), waits=[evm8])
            OP("sync", lambda e: e.nop(), waits=[ev_od[NT - 1]])
            sems = {c: stk.enter_context(nc.semaphore("sem_" + c)) for c in sorted(S.channels)}
            blk = stk.enter_context(nc.Block())
            S.emit(blk, sems)
            return nc

        def _wqld_mk(ch, base):
            def _wqld(e, ch=ch, base=base):
                out = []
                for gu in range(2):
                    for li, it in enumerate(ch):
                        r, itl = it // GTPR, it % GTPR
                        slot0 = gu * 24 + itl * 8
                        dst = s_wq[:, base + (gu * 4 + li) * 1024:
                                   base + (gu * 4 + li) * 1024 + 1024]
                        out.append(e.dma_start(
                            out=dst.rearrange("p (s j) -> p s j", s=8),
                            in_=d_wq_out.ap()[r, slot0:slot0 + 8].transpose([1, 0, 2])))
                return out
            return _wqld
        CHW = [[0, 1, 2, 3], [4, 5, 6, 7], [8, 9, 10, 11], [12, 13, 14, 15],
               [16, 17, 18, 19], [20, 21]]
        ev_wqld = {}
        for qi in range(2):
            ev_wqld[qi] = OP("sync", _wqld_mk(CHW[qi], (qi % 2) * 8192),
                             waits=[ev_q_ag], track=True, dma=True,
                             ndma=2 * len(CHW[qi]),
                             ch="wqa" if qi % 2 == 0 else "wqb")
        ev_m1c = OP("gpsimd", lambda g: g.dma_start(out=AP(d_m1_in, 0, [[1, 128]]),
                                                    in_=s_m1a[:, 0:1]),
                    waits=[ev_m1u, ev_ms], track=True, dma=True, ch="m1g")
        ev_m1ar = OP("gpsimd", lambda g: g.collective_compute(
            "AllReduce", Alu.max, replica_groups=RG,
            ins=[d_m1_in.ap().opt()], outs=[d_m1_out.ap().opt()]),
            waits=[ev_m1c], track=True, ch="cc")
        ev_m1l = OP("sync", lambda e: e.dma_start(out=s_y[0:1, 0:128], in_=d_m1_out.ap()),
                    waits=[ev_m1ar], track=True, dma=True, ch="m1r")
        ev_m1p = OP("vector", lambda v: v.tensor_reduce(
            s_mg[0:1, 0:1], s_y[0:1, 0:128], AX, Alu.max),
            waits=[ev_m1l], track=True)
        ev_sig = OP("scalar", lambda s: s.activation(out=s_mg[0:1, 8:10],
                                                     in_=s_clip[0:1, :], func=Act.Sigmoid),
                    waits=[ev_ld["clip"]], track=True)
        ev_s1 = vchain([
            lambda v: v.tensor_tensor(out=s_mg[0:1, 10:11], in0=s_mg[0:1, 8:9],
                                      in1=s_mg[0:1, 0:1], op=Alu.mult),
            lambda v: v.tensor_scalar(out=s_mg[0:1, 10:11], in0=s_mg[0:1, 10:11],
                                      scalar1=1.0 / 127.0, scalar2=None, op0=Alu.mult),
            lambda v: v.reciprocal(out=s_mg[0:1, 11:12], in_=s_mg[0:1, 10:11]),
        ], waits=[ev_m1p, ev_sig])
        ev_bc1 = OP("tensor", lambda t: t.matmul(p_m[:, 0:2], lhsT=s_onef[:, :],
                                                 rhs=s_mg[0:1, 10:12], start=True, stop=True),
                    waits=[ev_s1, ev_bcq[0], ev_bcq[1]], track=True)
        ev_col1 = OP("vector", lambda v: v.tensor_copy(out=s_col[:, 0:2], in_=p_m[:, 0:2]),
                     waits=[ev_bc1], track=True)
        ev_zqa = {}
        ev_zqb = {}
        for k in range(32):
            ib = (k % 2) * 512
            sl = s_zqt[:, k * 512:(k + 1) * 512]
            ev_zqa[k] = OP("scalar", lambda s, sl=sl, ib=ib: s.activation(
                out=s_i32[:, ib:ib + 512], in_=sl, func=Act.Identity,
                scale=s_col[:, 1:2]),
                waits=[ev_col1 if k == 0 else None,
                       ev_zev[NT - 1] if k == 0 else None,
                       ev_zqb.get(k - 2)], track=True)
            ev_zqb[k] = OP("vector", lambda v, sl=sl, ib=ib: v.tensor_scalar(
                out=sl, in0=s_i32[:, ib:ib + 512], scalar1=-128.0,
                scalar2=127.0, op0=Alu.max, op1=Alu.min),
                waits=[ev_zqa[k]], track=True)
        ev_zq = ev_zqb[31]
        def _sgf(v):
            ins = None
            for it in range(NIT):
                r, c0 = it // GTPR, it % GTPR
                v.tensor_scalar(out=s_sgf[:, it:it + 1],
                                in0=s_scall[:, r * 16 + c0:r * 16 + c0 + 1],
                                scalar1=s_col[:, 0:1], scalar2=None, op0=Alu.mult)
                ins = v.tensor_scalar(out=s_suf[:, it:it + 1],
                                      in0=s_scall[:, r * 16 + 3 + c0:r * 16 + 4 + c0],
                                      scalar1=s_col[:, 0:1], scalar2=None, op0=Alu.mult)
            return ins
        ev_sgf = OP("vector", _sgf, waits=[ev_scl, ev_col1], track=True)

        ev_sdb = {}
        for half in range(2):
            evb = OP("tensor", lambda t, h=half: t.matmul(
                p_m2[:, 0:512], lhsT=s_onef[:, :], rhs=s_sdr[0:1, h * 512:h * 512 + 512],
                start=True, stop=True),
                waits=[ev_sdr, ev_sdb.get(0), ev_bcq[1], ev_lnz[NT - 1]], track=True)
            ev_sdb[half] = OP("vector", lambda v, h=half: v.tensor_copy(
                out=s_sdx[:, h * 512:h * 512 + 512], in_=p_m2[:, 0:512]),
                waits=[evb], track=True)
        ev_dinvb = None
        for half in range(2):
            evb = OP("tensor", lambda t, h=half: t.matmul(
                p_m2[:, 0:512], lhsT=s_onef[:, :],
                rhs=s_dinvr[0:1, h * 512:h * 512 + 512],
                start=True, stop=True),
                waits=[ev_sdr, ev_sdb[1], ev_dinvb], track=True)
            ev_dinvb = OP("vector", lambda v, h=half: v.tensor_copy(
                out=s_dinvf[:, h * 512:h * 512 + 512], in_=p_m2[:, 0:512]),
                waits=[evb], track=True)
        ev_qsfa = OP("vector", lambda v: v.memset(s_qscf[:, 0:128], 1.0),
                     waits=[ev_sgf], track=True)
        def _qsfb(v):
            ins = None
            for it in range(NIT - 1, -1, -1):
                ins = v.tensor_scalar(out=s_qscf[:, it * 128:(it + 1) * 128],
                                      in0=s_qscf[:, 0:128],
                                      scalar1=s_suf[:, it:it + 1], scalar2=None,
                                      op0=Alu.mult)
            return ins
        ev_qsf = OP("vector", _qsfb, waits=[ev_qsfa, ev_dinvb], track=True)
        ev_dwl = {}
        ev_qdq = {}
        def emit_dunit(k, a, extra_waits=()):
            buf = s_sqs if k % 2 == 0 else s_outb
            ev_dwl[k] = OP("sync", lambda e, a=a, buf=buf: e.dma_start(
                out=buf[:, :], in_=d_wdt.ap()[a * 128:(a + 1) * 128, :]),
                waits=[ev_qdq.get(k - 2),
                       ev_sq[NT - 1] if k == 0 else None,
                       ev_dinvb if k == 1 else None,
                       (ev_dbgz.get(NT - 1) if dbg else None) if k == 1 else None],
                track=True, dma=True, ch="dwa" if k % 2 == 0 else "dwb")
            ev_qdq[k] = OP("vector", lambda v, a=a, buf=buf: (
                v.tensor_tensor(out=s_i32[:, :], in0=buf[:, :],
                                in1=s_dinvf[:, :], op=Alu.mult),
                v.tensor_copy(out=s_wq[:, a * 1024:(a + 1) * 1024],
                              in_=s_i32[:, :]))[-1],
                waits=[ev_dwl[k], ev_dinvb,
                       ev_zq if k == 0 else None, *extra_waits], track=True)
        ev_h = {}
        ev_hsp = {}
        ev_si = {}
        hidx = 0
        h_of_chunk_end = {}
        for qi, ch in enumerate(CHW):
            base = (qi % 2) * 8192
            if qi >= 2:
                ev_wqld[qi] = OP("sync", _wqld_mk(ch, base),
                                 waits=[ev_q_ag, h_of_chunk_end.get(qi - 2)],
                                 track=True, dma=True, ndma=2 * len(ch),
                                 ch="wqa" if qi % 2 == 0 else "wqb")
            if qi < 6:
                emit_dunit(qi, 16 + qi)
            for c in range(NCH):
                for li, it in enumerate(ch):
                    slot = hidx % 3
                    gpd = [p_x1[0][:, 0:512], p_x1[1][:, 0:512],
                           p_m[:, 0:512]][slot]
                    upd = [p_x1[0][:, 512:1024], p_x1[1][:, 512:1024],
                           p_m2[:, 0:512]][slot]
                    def _gu(te, base=base, c=c, li=li, gpd=gpd, upd=upd):
                        ins = None
                        for gu in range(2):
                            pd = gpd if gu == 0 else upd
                            for ht in range(8):
                                ins = te.matmul(
                                    pd, lhsT=s_wq[:, base + (gu * 4 + li) * 1024 + ht * 128:
                                                  base + (gu * 4 + li) * 1024 + ht * 128 + 128],
                                    rhs=s_zqt[:, ht * TPC + c * 512:ht * TPC + c * 512 + 512],
                                    start=(ht == 0), stop=(ht == 7))
                        return ins
                    evgu = OP("tensor", _gu,
                              waits=[ev_wqld[qi], ev_zq, ev_h.get(hidx - 3),
                                     ev_sdb[1] if hidx == 2 else None], track=True)
                    ev_si[hidx] = OP("scalar", lambda sc, it=it, gpd=gpd, hi=hidx:
                                     sc.activation(
                        out=s_hf[:, (hi % 2) * 512:(hi % 2) * 512 + 512],
                        in_=gpd, func=Act.Silu, scale=s_sgf[:, it:it + 1]),
                        waits=[evgu, ev_sgf, ev_h.get(hidx - 2)], track=True)
                    ev_h[hidx] = OP("vector", lambda v, upd=upd, hi=hidx: v.tensor_tensor(
                        out=s_hrw[:, (hi % 2) * 512:(hi % 2) * 512 + 512],
                        in0=s_hf[:, (hi % 2) * 512:(hi % 2) * 512 + 512],
                        in1=upd, op=Alu.mult),
                        waits=[ev_si[hidx], ev_hsp.get(hidx - 2)] +
                              (ev_push_all if hidx < 2 else []), track=True)
                    evhr2 = OP("vector", lambda v, hi=hidx: v.tensor_reduce(
                        s_lnc[:, 6:7], s_hrw[:, (hi % 2) * 512:(hi % 2) * 512 + 512],
                        AX, Alu.max, apply_absolute_value=True),
                        waits=[ev_h[hidx]], track=True)
                    ev_m2u = OP("vector", lambda v, it=it: v.scalar_tensor_tensor(
                        out=s_m2a[:, :], in0=s_lnc[:, 6:7], scalar=s_suf[:, it:it + 1],
                        op0=Alu.mult, in1=s_m2a[:, :], op1=Alu.max),
                        waits=[evhr2], track=True)
                    ev_hsp[hidx] = OP("gpsimd", lambda g, it=it, c=c, hi=hidx: g.dma_start(
                        out=d_spill.ap()[it, :, c * 512:(c + 1) * 512],
                        in_=s_hrw[:, (hi % 2) * 512:(hi % 2) * 512 + 512]),
                        waits=[ev_h[hidx]], track=True, dma=True,
                        ch="spa" if hidx % 2 == 0 else "spb")
                    hidx += 1
                    if qi == 5:
                        _k = 6 + (hidx - 81)
                        if 0 <= _k - 6 < 8:
                            emit_dunit(_k, _k - 6,
                                       extra_waits=(h_of_chunk_end[4],))
            h_of_chunk_end[qi] = ev_h[hidx - 1]
        ev_hsp_all = [ev_hsp[hidx - 1], ev_hsp[hidx - 2]]
        for wi in range(110):
            OP("tensor", lambda t: t.matmul(
                p_s[:, :], lhsT=s_id[:, 0:24], rhs=s_hkw[:, 0:512],
                start=True, stop=True))

        ev_m2c = OP("gpsimd", lambda g: g.dma_start(out=AP(d_m2_in, 0, [[1, 128]]),
                                                    in_=s_m2a[:, 0:1]),
                    waits=[ev_m2u], track=True, dma=True, ch="m2g")
        ev_m2ar = OP("gpsimd", lambda g: g.collective_compute(
            "AllReduce", Alu.max, replica_groups=RG,
            ins=[d_m2_in.ap().opt()], outs=[d_m2_out.ap().opt()]),
            waits=[ev_m2c], track=True, ch="cc")
        ev_m2l = OP("sync", lambda e: e.dma_start(out=s_y[0:1, 0:128], in_=d_m2_out.ap()),
                    waits=[ev_m2ar], track=True, dma=True, ch="m2r")
        ev_m2p = OP("vector", lambda v: v.tensor_reduce(
            s_mg[0:1, 12:13], s_y[0:1, 0:128], AX, Alu.max),
            waits=[ev_m2l], track=True)
        ev_s2 = vchain([
            lambda v: v.tensor_tensor(out=s_mg[0:1, 13:14], in0=s_mg[0:1, 9:10],
                                      in1=s_mg[0:1, 12:13], op=Alu.mult),
            lambda v: v.tensor_scalar(out=s_mg[0:1, 13:14], in0=s_mg[0:1, 13:14],
                                      scalar1=1.0 / 127.0, scalar2=None, op0=Alu.mult),
            lambda v: v.reciprocal(out=s_mg[0:1, 14:15], in_=s_mg[0:1, 13:14]),
        ], waits=[ev_m2p, ev_sig])
        ev_bc2 = OP("tensor", lambda t: t.matmul(p_m[:, 2:4], lhsT=s_onef[:, :],
                                                 rhs=s_mg[0:1, 13:15], start=True, stop=True),
                    waits=[ev_s2, ev_col1], track=True)
        ev_col2 = OP("vector", lambda v: v.tensor_copy(out=s_col[:, 2:4], in_=p_m[:, 2:4]),
                     waits=[ev_bc2], track=True)
        def _qscm(v):
            ins = None
            for seg in range(3):
                w = 1024 if seg < 2 else 768
                ins = v.tensor_scalar(out=s_qscf[:, seg * 1024:seg * 1024 + w],
                                      in0=s_qscf[:, seg * 1024:seg * 1024 + w],
                                      scalar1=s_col[:, 3:4], scalar2=None, op0=Alu.mult)
            return ins
        ev_qsc = OP("vector", _qscm, waits=[ev_col2, ev_qsf], track=True)
        ev_sdx2 = OP("vector", lambda v: v.tensor_scalar(
            out=s_sdx[:, :], in0=s_sdx[:, :], scalar1=s_col[:, 2:3], scalar2=None,
            op0=Alu.mult), waits=[ev_sdb[1], ev_col2], track=True)
        for _i in range(8):
            emit_dunit(14 + _i, 8 + _i, extra_waits=(ev_h[hidx - 1],))
        ev_qdld = ev_qdq[21]
        if dbg:
            OP("gpsimd", lambda e: e.dma_start(
                out=d_dbg_qd.ap()[:, 0:1024], in_=s_wq[:, 0:1024]),
                waits=[ev_qdq[6]], track=True, dma=True, ch="dbg")
            OP("vector", lambda v: v.tensor_copy(out=s_sqs[:, :],
                                                 in_=s_dinvf[:, :]),
               waits=[ev_qdq[21]], track=True)
            OP("sync", lambda e: e.dma_start(
                out=d_dbg_qd.ap()[:, 1024:2048], in_=s_sqs[:, :]),
                waits=[("vector", S.count["vector"])], track=True, dma=True,
                ch="dbg")
            OP("sync", lambda e: e.dma_start(
                out=d_dbg_r.ap()[4:5, 0:1024],
                in_=s_dinvr[0:1, :]), waits=[ev_qdq[21]], track=True, dma=True,
                ch="dbg")
            OP("sync", lambda e: e.dma_start(
                out=d_dbg_r.ap()[5:6, 0:1024],
                in_=AP(d_rowb, 1280, [[1024, 1], [1, 1024]])),
                waits=[ev_qdq[21]], track=True, dma=True, ch="dbg")
            OP("sync", lambda e: e.dma_start(
                out=d_dbg_r.ap()[6:7, 0:128],
                in_=s_scall[0:1, :]), waits=[ev_qdq[21]], track=True, dma=True,
                ch="dbg")

        ev_hl = {}
        ev_qh = {}
        ev_dn = {}
        ev_y2 = {}
        ev_sq2 = {}
        ev_ov = {}
        ev_od = {}

        def emit_front(t):
            ev_hl[t] = OP("sync", lambda e, t=t: e.dma_start(
                out=s_zqt[:, (t % 2) * IF:(t % 2) * IF + IF],
                in_=d_spill.ap()[:, :, t * 128:(t + 1) * 128].transpose([1, 0, 2])),
                waits=[ev_hsp[80 + 2 * (t // 4)], ev_hsp[81 + 2 * (t // 4)],
                       ev_qh.get(t - 2), ev_dbg_w2dma], track=True, dma=True,
                ch="hla" if t % 2 == 0 else "hlb")
            def _qh(v, t=t):
                hb = (t % 2) * IF
                qb = (t % 2) * IF
                ins = None
                for seg in range(3):
                    w = 1024 if seg < 2 else 768
                    o = seg * 1024
                    v.tensor_tensor(out=s_i32[:, 0:w], in0=s_zqt[:, hb + o:hb + o + w],
                                    in1=s_qscf[:, o:o + w], op=Alu.mult)
                    ins = v.tensor_scalar(out=s_qht[:, qb + o:qb + o + w], in0=s_i32[:, 0:w],
                                          scalar1=-128.0, scalar2=127.0,
                                          op0=Alu.max, op1=Alu.min)
                return ins
            ev_qh[t] = OP("vector", _qh, waits=[ev_hl[t], ev_qsc, ev_dn.get(t - 2)],
                          track=True)
            slot = t % 3
            pds = [[p_x1[0][:, 0:512], p_x1[0][:, 512:1024]],
                   [p_x1[1][:, 0:512], p_x1[1][:, 512:1024]],
                   [p_m[:, 0:512], p_m2[:, 0:512]]][slot]
            def _dn(te, t=t, pds=pds):
                qb = (t % 2) * IF
                ins = None
                for ec in range(2):
                    for it in range(NIT):
                        ins = te.matmul(pds[ec],
                                        lhsT=s_qht[:, qb + it * 128:qb + it * 128 + 128],
                                        rhs=s_wq[:, it * 1024 + ec * 512:
                                                 it * 1024 + ec * 512 + 512],
                                        start=(it == 0), stop=(it == NIT - 1))
                return ins
            ev_dn[t] = OP("tensor", _dn, waits=[ev_qh[t], ev_qdld, ev_y2.get(t - 3)],
                          track=True)

        def emit_back(t):
            slot = t % 3
            pds = [[p_x1[0][:, 0:512], p_x1[0][:, 512:1024]],
                   [p_x1[1][:, 0:512], p_x1[1][:, 512:1024]],
                   [p_m[:, 0:512], p_m2[:, 0:512]]][slot]
            def _y2(v, t=t, pds=pds):
                ins = None
                for ec in range(2):
                    sl = slice(ec * 512, (ec + 1) * 512)
                    v.tensor_tensor(out=s_ysq[:, sl], in0=pds[ec],
                                    in1=s_sdx[:, sl], op=Alu.mult)
                    ins = v.scalar_tensor_tensor(
                        out=s_y[:, sl], in0=s_ysq[:, sl], scalar=1.0, op0=Alu.mult,
                        in1=s_z[:, t * E + ec * 512:t * E + ec * 512 + 512], op1=Alu.add,
                        accum_out=s_lnc[:, 0:1] if ec == 0 else s_lnc[:, 1:2])
                return ins
            ev_y2[t] = OP("vector", _y2, waits=[ev_dn[t], ev_sdx2], track=True)
            ev_sq2[t] = OP("scalar", lambda s: s.activation(
                out=s_sqs[:, :], in_=s_y[:, :], func=Act.Square, accum_out=s_lnc[:, 2:3]),
                waits=[ev_y2[t]], track=True)
            evv2 = vchain([
                lambda v: v.tensor_tensor(out=s_lnc[:, 0:1], in0=s_lnc[:, 0:1],
                                          in1=s_lnc[:, 1:2], op=Alu.add),
                lambda v: v.tensor_scalar(out=s_lnc[:, 3:4], in0=s_lnc[:, 0:1],
                                          scalar1=1.0 / E, scalar2=None, op0=Alu.mult),
                lambda v: v.tensor_tensor(out=s_lnc[:, 4:5], in0=s_lnc[:, 3:4],
                                          in1=s_lnc[:, 3:4], op=Alu.mult),
                lambda v: v.scalar_tensor_tensor(out=s_lnc[:, 4:5], in0=s_lnc[:, 2:3],
                                                 scalar=1.0 / E, op0=Alu.mult,
                                                 in1=s_lnc[:, 4:5], op1=Alu.subtract),
                lambda v: v.tensor_scalar(out=s_lnc[:, 4:5], in0=s_lnc[:, 4:5],
                                          scalar1=1e-5, scalar2=None, op0=Alu.add),
            ], waits=[ev_sq2[t]])
            evrt2 = OP("scalar", lambda s: s.activation(
                out=s_lnc[:, 5:6], in_=s_lnc[:, 4:5], func=Act.Sqrt),
                waits=[evv2], track=True)
            evoa = OP("vector", lambda v: v.reciprocal(
                out=s_lnc[:, 6:7], in_=s_lnc[:, 5:6]), waits=[evrt2], track=True)
            obuf = s_outb if t % 2 == 0 else s_wn
            ev_ov[t] = OP("vector", lambda v, t=t, obuf=obuf: v.tensor_scalar(
                out=obuf[:, :], in0=s_y[:, :], scalar1=s_lnc[:, 3:4],
                scalar2=s_lnc[:, 6:7], op0=Alu.subtract, op1=Alu.mult),
                waits=[evoa, ev_od.get(t - 2)], track=True)
            ev_od[t] = OP("gpsimd", lambda g, t=t, obuf=obuf: g.dma_start(
                out=d_out.ap()[t * 128:(t + 1) * 128, :],
                in_=obuf[:, :]),
                waits=[ev_ov[t]], track=True, dma=True,
                ch="outa" if t % 2 == 0 else "outb")

        for t in range(NT + 1):
            if t < NT:
                emit_front(t)
            if t >= 1:
                emit_back(t - 1)

        if dbg:
            OP("sync", lambda e: e.dma_start(out=d_dbg_m.ap(), in_=s_mg[0:1, :]),
               waits=[ev_s2], track=True, dma=True, ch="dbg")
        OP("gpsimd", lambda g: g.engine_nop(),
           waits=[ev_od[NT - 1], ev_od[NT - 2]])
        sems = {c: stk.enter_context(nc.semaphore("sem_" + c)) for c in sorted(S.channels)}
        blk = stk.enter_context(nc.Block())
        S.emit(blk, sems)
    return nc


def prep_inputs(inp):
    import ml_dtypes
    x = np.ascontiguousarray(np.asarray(inp["L0_attn_x"], dtype=np.float32).reshape(NC * TPC, E))
    wg = np.asarray(inp["L0_ffn_w_gate"], dtype=np.float32)
    wu = np.asarray(inp["L0_ffn_w_up"], dtype=np.float32)
    wd = np.asarray(inp["L0_ffn_w_down"], dtype=np.float32)
    wel = np.ascontiguousarray(np.asarray(inp["L0_attn_W_el"], dtype=np.float32))
    bel = np.ascontiguousarray(np.asarray(inp["L0_attn_b_el"], dtype=np.float32).reshape(1, 24))
    wle = np.ascontiguousarray(np.asarray(inp["L0_attn_W_le"], dtype=np.float32))
    wleb = np.ascontiguousarray(np.asarray(inp["L0_attn_b_le"], dtype=np.float32).reshape(1, E))
    mask24 = np.zeros((1, 24), dtype=np.float32); mask24[0, :12] = 1.0
    clipv = np.array([[float(np.asarray(inp["L0_ffn_clip_x"]).reshape(-1)[0]),
                       float(np.asarray(inp["L0_ffn_clip_down"]).reshape(-1)[0])]],
                     dtype=np.float32)
    k = np.fft.fftfreq(E).astype(np.float32)
    ka = (np.abs(k) + np.float32(1e-8)).reshape(1, E).astype(np.float32)
    ka2 = (ka * ka).astype(np.float32)
    dd = np.arange(E)
    cosm = np.ascontiguousarray(np.cos(2 * np.pi * np.outer(dd, dd) / E).astype(np.float32))
    lxr = np.zeros((1, 8), dtype=np.float32)
    lxr[0, :5] = LX.astype(np.float32)
    _A = np.array([[1,1,1,0,1,1,1,0,0,0,1,1],[1,1,0,1,1,1,0,0,0,1,1,0],[1,0,1,1,1,0,0,0,1,1,0,1],[0,1,1,1,0,0,0,1,1,0,1,1],[0,0,0,0,1,1,1,1,1,1,1,1],[1,1,1,1,0,0,0,0,0,0,0,0],[0,0,0,0,1,1,1,1,0,0,0,0],[1,1,0,0,1,1,0,0,1,1,0,0],[1,0,1,0,1,0,1,0,1,0,1,0],[0,1,0,1,0,1,0,1,0,1,0,1],[1,0,0,1,1,0,0,1,1,0,0,1],[0,1,1,0,0,1,1,0,0,1,1,0]], dtype=np.float32)
    Hm = np.concatenate([_A.T, np.eye(12, dtype=np.float32)], axis=1)
    hmt = np.ascontiguousarray(Hm.T).astype(ml_dtypes.bfloat16)
    ident = np.eye(128, dtype=ml_dtypes.bfloat16)
    xfirst = np.ascontiguousarray(x[0:1, :])
    wgT = np.ascontiguousarray(wg.T)
    wuT = np.ascontiguousarray(wu.T)
    wdT = np.ascontiguousarray(wd.T)
    in_maps = []
    for c in range(NC):
        sh = np.ascontiguousarray(x[c * TPC:(c + 1) * TPC, :])
        xt = np.ascontiguousarray(sh.T)
        xtf = np.ascontiguousarray(xt.reshape(8, 128, TPC)[:, ::-1, :].reshape(E, TPC))
        git = [min(GTPR * c + i, NIT - 1) for i in range(GTPR)]
        wgn = np.ascontiguousarray(np.concatenate([wg[i * 128:(i + 1) * 128, :] for i in git], axis=0))
        wun = np.ascontiguousarray(np.concatenate([wu[i * 128:(i + 1) * 128, :] for i in git], axis=0))
        wgt = np.ascontiguousarray(np.concatenate([wgT[:, i * 128:(i + 1) * 128] for i in git], axis=1))
        wut = np.ascontiguousarray(np.concatenate([wuT[:, i * 128:(i + 1) * 128] for i in git], axis=1))
        wdn = np.ascontiguousarray(wd[c * 128:(c + 1) * 128, :])
        wdt = wdT
        in_maps.append({
            "xnat": sh, "xtf": xtf, "xfirst": xfirst, "ka": ka, "ka2": ka2,
            "lxr": lxr, "clipv": clipv, "cosm": cosm, "wel": wel, "bel": bel,
            "wle": wle, "wleb": wleb, "mask24": mask24, "hmt": hmt, "ident": ident,
            "wgn": wgn, "wun": wun, "wdn": wdn, "wgt": wgt, "wut": wut, "wdt": wdt,
        })
    return in_maps


def kernel(**inputs):
    from concourse.bass_utils import run_bass_kernel_spmd
    in_maps = prep_inputs(inputs)
    nc = build_nc(stage=3, dbg=False)
    res = run_bass_kernel_spmd(nc, in_maps, core_ids=list(range(NC)))
    out = np.concatenate([np.asarray(res.results[c]["out"]) for c in range(NC)], axis=0)
    return np.ascontiguousarray(out.reshape(2, 8192, E)).astype(np.float32, copy=False)
